# version 17
# baseline (speedup 1.0000x reference)
import numpy as np
import ml_dtypes
from contextlib import ExitStack

import concourse.bass as bass
import concourse.mybir as mybir
from concourse.bass_utils import run_bass_kernel_spmd

F32 = mybir.dt.float32
BF16 = mybir.dt.bfloat16
FP8 = mybir.dt.float8e4
ALU = mybir.AluOpType
AF = mybir.ActivationFunctionType
AX = mybir.AxisListType.X

MAGIC = float(1.5 * 2 ** 23)
NCORE = 8
TC = 2048
CH = 512
NCH = TC // CH
NTT = CH // 128
D = 1024
DT = D // 128
F3 = 3072
INNER = 4096
NIC = INNER // 512
LN_EPS = 1e-5
DEN_EPS = 1e-6
WQN = INNER * D


class Prog:
    ENGINES = ("t", "s", "v", "g", "y")

    def __init__(self):
        self.ops = {e: [] for e in self.ENGINES}
        self.cnt = {}
        self.waited = {e: {} for e in self.ENGINES}

    def tok(self, sem):
        return (sem, self.cnt.get(sem, 0))

    AUTO = {"s": "SE", "v": "VE", "g": "GE"}
    AUTO_CHAIN = True

    def emit(self, eng, fn, deps=(), inc=None, inc_n=1):
        deps = list(deps)
        auto = self.AUTO.get(eng) if self.AUTO_CHAIN else None
        if auto is not None:
            deps.append((auto, self.cnt.get(auto, 0)))
            if inc is None:
                inc, inc_n = auto, 1
        w = self.waited[eng]
        flat = []
        for d in deps:
            if d is None:
                continue
            if isinstance(d, list):
                flat.extend(x for x in d if x is not None)
            else:
                flat.append(d)
        best = {}
        for sem, val in flat:
            if val > best.get(sem, 0):
                best[sem] = val
        dd = []
        for sem, val in best.items():
            if val > 0 and w.get(sem, 0) < val:
                w[sem] = val
                dd.append((sem, val))
        if inc is not None:
            self.cnt[inc] = self.cnt.get(inc, 0) + inc_n
        self.ops[eng].append((dd, fn, (inc, inc_n) if inc is not None else None))
        return self.tok(inc) if inc is not None else None


PHASE_FULL = 99


def build_nc(phase=PHASE_FULL):
    nc = bass.Bass("TRN2", target_bir_lowering=False)

    x_h = nc.dram_tensor("x", [TC, D], F32, kind="ExternalInput").ap()
    wqkv_h = nc.dram_tensor("wqkv", [D, F3], F32, kind="ExternalInput").ap()
    wgate_h = nc.dram_tensor("wgate", [D, D], F32, kind="ExternalInput").ap()
    wproj_h = nc.dram_tensor("wproj", [D, D], F32, kind="ExternalInput").ap()
    biasf_h = nc.dram_tensor("biasf", [128, 40], F32, kind="ExternalInput").ap()
    bvbc_h = nc.dram_tensor("bvbc", [128, D], F32, kind="ExternalInput").ap()
    bpbc_h = nc.dram_tensor("bpbc", [128, D], F32, kind="ExternalInput").ap()
    identf_h = nc.dram_tensor("identf", [128, 128], F32, kind="ExternalInput").ap()
    identb_h = nc.dram_tensor("identb", [128, 128], BF16, kind="ExternalInput").ap()
    trilb_h = nc.dram_tensor("trilb", [128, 128], BF16, kind="ExternalInput").ap()
    onesb_h = nc.dram_tensor("onesb", [128, 128], BF16, kind="ExternalInput").ap()
    onesf_h = nc.dram_tensor("onesf", [128, 128], F32, kind="ExternalInput").ap()
    maskI_h = nc.dram_tensor("maskI", [128, NCORE * 128], BF16, kind="ExternalInput").ap()
    wsh_h = nc.dram_tensor("wsh", [128, 12288], F32, kind="ExternalInput").ap()
    out_h = nc.dram_tensor("out", [TC, D], F32, kind="ExternalOutput").ap()

    sc_qT = nc.dram_tensor("sc_qT", [NCH * 128, DT * CH], BF16).ap()
    sc_num = nc.dram_tensor("sc_num", [NCH * 128, DT * CH], BF16).ap()
    cc1_in = nc.dram_tensor("cc1_in", [1, 4], F32).ap()
    cc1_out = nc.dram_tensor("cc1_out", [NCORE, 4], F32).ap()
    cc2_in = nc.dram_tensor("cc2_in", [128, 12288], FP8).ap()
    cc2_out = nc.dram_tensor("cc2_out", [NCORE * 128, 12288], FP8).ap()
    cc3_in = nc.dram_tensor("cc3_in", [D, 1032], BF16).ap()
    cc3a_out = nc.dram_tensor("cc3a_out", [NCORE * 512, 1032], BF16).ap()
    cc3b_out = nc.dram_tensor("cc3b_out", [NCORE * 512, 1032], BF16).ap()

    p = Prog()
    es = ExitStack()
    sems = {}
    for sname in ("LDC", "LDG", "LDW0", "LDW1", "LDX0", "LDX1", "LDB0", "LDB1",
                  "LDBW", "LDS", "LDS2", "SPQ", "SPN", "SPW0", "SPW1", "SPC", "SPO",
                  "SPS", "SPSA", "TE", "SE", "VE", "GE", "CC"):
        sems[sname] = es.enter_context(nc.semaphore(sname))

    B = {}

    def sb(stack, name, shape, dt):
        B[name] = stack.enter_context(nc.sbuf_tensor("sb_" + name, shape, dt))
        return B[name]

    for nm, sh, dt in [
        ("identb", [128, 128], BF16), ("BPBC", [128, D], F32),
        ("den_all", [128, NCH * NTT], F32), ("SC", [128, 96], F32),
        ("MGV", [128, 4], F32), ("SWV", [128, 4], F32),
        ("DTOT", [128, NTT], F32), ("RDEN", [128, NTT], F32), ("PM8", [128, 8], F32),
    ]:
        sb(es, nm, sh, dt)

    psf = es.enter_context(nc.psum_tensor("psf", [128, 3584], F32))
    psb = es.enter_context(nc.psum_tensor("psb", [128, 1024], BF16))

    def bankf(i, w=512):
        return psf[:, 512 * i: 512 * i + w]

    T, S, V, G, Y = "t", "s", "v", "g", "y"

    def E(eng, fn, deps=(), inc=None, inc_n=1):
        return p.emit(eng, fn, deps, inc, inc_n)

    bank = {f"f{i}": None for i in range(7)}
    bank["b7"] = None

    def finish_probe(deps):
        tpr = E(G, lambda e: e.dma_start(out=out_h[0:128, 0:96], in_=B["SC"][:]),
                deps=deps, inc="SPO", inc_n=16)
        E(G, lambda e: e.nop(), deps=[p.tok("SPO")])
        replay_all()
        return True

    replay_done = []

    def replay_all():
        with nc.Block() as block:
            def mk_replay(eng_key):
                def _(eng):
                    for deps, fn, inc in p.ops[eng_key]:
                        for sem, val in deps:
                            eng.wait_ge(sems[sem], val)
                        instr = fn(eng)
                        if inc is not None:
                            instr.then_inc(sems[inc[0]], inc[1])
                return _
            block.tensor(mk_replay("t"))
            block.scalar(mk_replay("s"))
            block.vector(mk_replay("v"))
            block.gpsimd(mk_replay("g"))
            block.sync(mk_replay("y"))
        replay_done.append(True)

    E(Y, lambda e: e.dma_start(out=B["identb"][:], in_=identb_h), inc="LDC", inc_n=16)

    E(V, lambda e: e.memset(B["SC"][:], 0.0))
    E(Y, lambda e: e.dma_start(out=B["BPBC"][:], in_=bpbc_h), inc="LDC", inc_n=16)
    t_rowcast = E(S, lambda e: e.nop(), inc="SE")

    esA = ExitStack()
    for nm, sh, dt in [
        ("trilb", [128, 128], BF16), ("onesb", [128, 128], BF16),
        ("onesf", [128, 128], F32), ("BVBC", [128, D], F32),
        ("identf", [128, 128], F32), ("biasf", [128, 40], F32),
        ("CC1S", [128, 4], F32), ("CC1G", [128, 4], F32),
        ("ABSA", [128, 12], F32), ("AB3", [128, 4], F32),
        ("wqkv_bf", [128, DT * F3], BF16), ("wgate_bf", [128, DT * D], BF16),
        ("Sb", [128, DT * 1032], BF16),
        ("WST0", [128, 1024], F32), ("WST1", [128, 1024], F32),
        ("XT0", [128, D], F32), ("XT1", [128, D], F32),
        ("XH0", [128, D], F32), ("XH1", [128, D], F32),
        ("xnT0", [128, DT * CH], BF16), ("xnT1", [128, DT * CH], BF16),
        ("qT", [128, DT * CH], BF16),
        ("kT", [128, DT * CH], BF16), ("kTok", [128, NTT * D], BF16),
        ("vtok", [128, NTT * 1032], BF16), ("ATm", [128, NTT * CH], BF16),
        ("numTb", [128, DT * CH], BF16),
        ("GB0", [128, 512], F32), ("GB1", [128, 512], F32),
        ("K1B0", [128, 512], F32), ("K1B1", [128, 512], F32),
        ("RB0", [128, 512], F32), ("RB1", [128, 512], F32),
        ("MB0", [128, 512], F32), ("MB1", [128, 512], F32),
        ("JUNK", [128, 1024], F32), ("JK2", [128, 1024], F32),
        ("WQO0", [128, 1024], FP8), ("WQO1", [128, 1024], FP8),
    ]:
        sb(esA, nm, sh, dt)

    for nm, srch in (("trilb", trilb_h), ("onesb", onesb_h), ("onesf", onesf_h),
                     ("BVBC", bvbc_h), ("identf", identf_h), ("biasf", biasf_h)):
        E(Y, (lambda nm=nm, srch=srch: lambda e: e.dma_start(out=B[nm][:], in_=srch))(),
          deps=[t_rowcast], inc="LDC", inc_n=16)
    t_consts = p.tok("LDC")
    for tt in range(NTT):
        E(V, (lambda tt=tt: lambda e: e.memset(
            B["vtok"][:, tt * 1032 + 1024: tt * 1032 + 1025], 1.0))())
        E(V, (lambda tt=tt: lambda e: e.memset(
            B["vtok"][:, tt * 1032 + 1025: tt * 1032 + 1032], 0.0))())
    for i in range(DT):
        E(V, (lambda i=i: lambda e: e.memset(
            B["Sb"][:, i * 1032 + 1025: i * 1032 + 1032], 0.0))())
    E(V, lambda e: e.memset(B["CC1G"][:], 0.0))

    cast_toks = []
    pieces = []
    for k in range(DT):
        for c0 in (0, 1024, 2048):
            pieces.append((wqkv_h, "wqkv_bf", F3, k, c0, 1024))
    for k in range(DT):
        pieces.append((wgate_h, "wgate_bf", D, k, 0, 1024))
    for idx, (srch, dstbuf, wdt, k, c0, wlen) in enumerate(pieces):
        wst = B[f"WST{idx % 2}"]
        deps = [cast_toks[idx - 2]] if idx >= 2 else [t_rowcast]
        tl = E(Y, (lambda wst=wst, srch=srch, k=k, c0=c0, wlen=wlen: lambda e: e.dma_start(
            out=wst[:, 0:wlen], in_=srch[k * 128:(k + 1) * 128, c0:c0 + wlen]))(),
            deps=deps, inc=f"LDW{idx % 2}", inc_n=16)
        tc = E(S, (lambda wst=wst, k=k, dstbuf=dstbuf, wdt=wdt, c0=c0, wlen=wlen:
                   lambda e: e.copy(
            out=B[dstbuf][:, k * wdt + c0: k * wdt + c0 + wlen], in_=wst[:, 0:wlen]))(),
            deps=[tl, t_rowcast], inc="SE")
        cast_toks.append(tc)
    wcast_done = cast_toks[-1]
    if phase == 1:
        finish_probe([wcast_done])
        esA.close()
        esB = None
        es.close()
        return nc

    def emit_quant():
        abs_toks = []
        for i in range(12):
            wst = B[f"WST{i % 2}"]
            deps = [wcast_done] if i < 2 else [abs_toks[i - 2]]
            tl = E(Y, (lambda i=i, wst=wst: lambda e: e.dma_start(
                out=wst[:, 0:1024], in_=wsh_h[:, i * 1024:(i + 1) * 1024]))(),
                deps=deps, inc=f"LDW{i % 2}", inc_n=16)
            ta = E(S, (lambda i=i, wst=wst: lambda e: e.activation(
                out=B["JUNK"][:], in_=wst[:, 0:1024], func=AF.Abs,
                accum_out=B["ABSA"][:, i:i + 1]))(),
                deps=[tl] + ([abs_toks[i - 1]] if i >= 1 else []), inc="SE")
            abs_toks.append(ta)
        E(V, lambda e: e.memset(B["AB3"][:, 3:4], 0.0), deps=[t_rowcast])
        tr = None
        for m in range(3):
            tr = E(V, (lambda m=m: lambda e: e.reduce_sum(
                out=B["AB3"][:, m:m + 1], in_=B["ABSA"][:, 4 * m:4 * m + 4], axis=AX))(),
                deps=[abs_toks[4 * m + 3]], inc="VE")
        tmm = E(T, lambda e: e.matmul(psf[:, 3072:3076], B["onesf"][:],
                                      B["AB3"][:, 0:4], start=True, stop=True),
                deps=[tr, t_consts, bank["f6"]], inc="TE")
        tcp = E(V, lambda e: e.tensor_copy(out=B["CC1S"][:], in_=psf[:, 3072:3076]),
                deps=[tmm], inc="VE")
        tspw = E(G, lambda e: e.dma_start(out=cc1_in, in_=B["CC1S"][0:1, 0:4]),
                 deps=[tcp], inc="SPC", inc_n=16)
        tcc1 = E(G, lambda e: e.collective_compute(
            "AllGather", ALU.bypass, replica_groups=[list(range(NCORE))],
            ins=[cc1_in], outs=[cc1_out]), deps=[tspw], inc="CC")
        A["quant_cc1"] = tcc1
        A["quant_tcp"] = tcp
        A["quant_abs"] = abs_toks
        return "SPLIT"

    def quant_fin():
        tcc1 = A["quant_cc1"]
        tcp = A["quant_tcp"]
        abs_toks = A["quant_abs"]
        tg_ = E(Y, lambda e: e.dma_start(out=B["CC1G"][0:NCORE, 0:4], in_=cc1_out),
                deps=[tcc1], inc="LDG", inc_n=16)
        tmm2 = E(T, lambda e: e.matmul(psf[:, 3072:3076], B["onesf"][:],
                                       B["CC1G"][:], start=True, stop=True),
                 deps=[tg_, tcp], inc="TE")
        tmg = E(V, lambda e: e.tensor_scalar(
            out=B["MGV"][:], in0=psf[:, 3072:3076], scalar1=1.0 / WQN,
            scalar2=1e-5, op0=ALU.mult, op1=ALU.max), deps=[tmm2], inc="VE")
        sw_ready = E(V, lambda e: e.reciprocal(out=B["SWV"][:], in_=B["MGV"][:]),
                     deps=[tmg], inc="VE")
        bank["f6"] = sw_ready

        spw_toks = []
        t1_toks = []
        q_prev = (None, None)
        for i in range(12):
            m = i // 4
            wst = B[f"WST{i % 2}"]
            wqo = B[f"WQO{i % 2}"]
            deps = [abs_toks[-1], sw_ready] + ([t1_toks[i - 2]] if i >= 2 else [])
            tl = E(Y, (lambda i=i, wst=wst: lambda e: e.dma_start(
                out=wst[:, 0:1024], in_=wsh_h[:, i * 1024:(i + 1) * 1024]))(),
                deps=deps, inc=f"LDW{i % 2}", inc_n=16)
            t1 = E(V, (lambda m=m, wst=wst: lambda e: e.tensor_scalar(
                out=B["JUNK"][:], in0=wst[:, 0:1024], scalar1=B["SWV"][:, m:m + 1],
                scalar2=MAGIC, op0=ALU.mult, op1=ALU.add))(),
                deps=[tl] + ([q_prev[0]] if i >= 1 else []), inc="VE")
            t1_toks.append(t1)
            t2 = E(V, (lambda: lambda e: e.tensor_scalar(
                out=B["JK2"][:], in0=B["JUNK"][:], scalar1=MAGIC + 1.0,
                scalar2=MAGIC - 1.0, op0=ALU.min, op1=ALU.max))(),
                deps=[t1] + ([q_prev[1]] if i >= 1 else []), inc="VE")
            t3 = E(V, (lambda wqo=wqo: lambda e: e.tensor_scalar(
                out=wqo[:], in0=B["JK2"][:], scalar1=MAGIC, scalar2=None,
                op0=ALU.subtract))(),
                deps=[t2] + ([spw_toks[i - 2]] if i >= 2 else []), inc="VE")
            q_prev = (t1, t2)
            ts_ = E(G, (lambda i=i, wqo=wqo: lambda e: e.dma_start(
                out=cc2_in[:, i * 1024:(i + 1) * 1024], in_=wqo[:]))(),
                deps=[t3], inc=f"SPW{i % 2}", inc_n=16)
            spw_toks.append(ts_)
        cc2_done = E(G, lambda e: e.collective_compute(
            "AllGather", ALU.bypass, replica_groups=[list(range(NCORE))],
            ins=[cc2_in], outs=[cc2_out]), deps=spw_toks, inc="CC")
        if phase == 2:
            finish_probe([cc2_done, sw_ready])
            return None
        return sw_ready, cc2_done


    free = {"qT": None, "numTb": None, "kT": None, "kTok": None, "vtok": None,
            "ATm": None, "xnT": None}
    xt_free = [None, None]
    xh_free = [None, None]
    den_prev = None
    A = {"vts": None}

    def emit_LN(c):
        xn_toks = []
        for tt in range(NTT):
            g = c * NTT + tt
            par = g % 2
            w = 48 * par
            XT = B[f"XT{par}"]
            XH = B[f"XH{par}"]
            tl = E(Y, (lambda c=c, tt=tt, XT=XT: lambda e: e.dma_start(
                out=XT[:], in_=x_h[c * CH + tt * 128: c * CH + (tt + 1) * 128, :]))(),
                deps=[xt_free[par]], inc=f"LDX{par}", inc_n=16)
            E(V, (lambda XT=XT, w=w: lambda e: e.bn_stats(
                B["SC"][:, w:w + 6], XT[:, 0:512]))(), deps=[tl])
            E(V, (lambda XT=XT, w=w: lambda e: e.bn_stats(
                B["SC"][:, w + 6:w + 12], XT[:, 512:1024]))())
            E(V, (lambda w=w: lambda e: e.bn_aggr(
                B["SC"][:, w + 12:w + 14], B["SC"][:, w:w + 12]))())
            E(V, (lambda w=w: lambda e: e.tensor_scalar(
                out=B["SC"][:, w + 14:w + 15], in0=B["SC"][:, w + 13:w + 14],
                scalar1=LN_EPS, scalar2=None, op0=ALU.add))())
            tv = E(V, (lambda w=w: lambda e: e.reciprocal(
                out=B["SC"][:, w + 15:w + 16], in_=B["SC"][:, w + 14:w + 15]))(), inc="VE")
            tsq = E(S, (lambda w=w: lambda e: e.activation(
                out=B["SC"][:, w + 16:w + 17], in_=B["SC"][:, w + 15:w + 16],
                func=AF.Sqrt))(), deps=[tv], inc="SE")
            tneg = E(V, (lambda w=w: lambda e: e.tensor_scalar(
                out=B["SC"][:, w + 17:w + 18], in0=B["SC"][:, w + 12:w + 13],
                scalar1=B["SC"][:, w + 16:w + 17], scalar2=-1.0,
                op0=ALU.mult, op1=ALU.mult))(), deps=[tsq], inc="VE")
            txh = E(S, (lambda XT=XT, XH=XH, w=w: lambda e: e.activation(
                out=XH[:], in_=XT[:], func=AF.Identity,
                scale=B["SC"][:, w + 16:w + 17], bias=B["SC"][:, w + 17:w + 18]))(),
                deps=[tneg, xh_free[par]], inc="SE")
            xt_free[par] = txh
            bname = "f4" if par == 0 else "f2"
            pcol = 2048 if par == 0 else 1024
            ttr = None
            for k in range(DT):
                ttr = E(T, (lambda XH=XH, k=k, pcol=pcol: lambda e: e.transpose(
                    psf[:, pcol + k * 128: pcol + (k + 1) * 128],
                    XH[:, k * 128:(k + 1) * 128], B["identf"][:]))(),
                    deps=([txh, t_consts, bank[bname],
                           bank["f5" if par == 0 else "f3"]] if k == 0 else []),
                    inc="TE" if k == DT - 1 else None)
            tev = None
            XN = B[f"xnT{c % 2}"]
            for k in range(DT):
                tev = E(V, (lambda k=k, tt=tt, pcol=pcol, XN=XN: lambda e: e.tensor_scalar(
                    out=XN[:, k * CH + tt * 128: k * CH + (tt + 1) * 128],
                    in0=psf[:, pcol + k * 128: pcol + (k + 1) * 128],
                    scalar1=B["biasf"][:, 24 + k:25 + k],
                    scalar2=B["biasf"][:, 32 + k:33 + k],
                    op0=ALU.mult, op1=ALU.add))(),
                    deps=([ttr, free["xnT"]] if k == 0 else []),
                    inc="VE" if k == DT - 1 else None)
            xh_free[par] = tev
            bank[bname] = tev
            bank["f5" if par == 0 else "f3"] = tev
            xn_toks.append(tev)
        return xn_toks[-1]

    def emit_attn(c, xn_done):
        XN = B[f"xnT{c % 2}"]
        if phase == 40 and c == 0:
            finish_probe([xn_done])
            return None

        def feat_mm(bk_id, wbuf, wbase, fcol, deps0):
            tk = None
            for k in range(DT):
                tk = E(T, (lambda k=k, bk_id=bk_id, wbuf=wbuf, wbase=wbase, fcol=fcol:
                           lambda e: e.matmul(
                               bankf(bk_id),
                               B[wbuf][:, k * wbase + fcol * 128: k * wbase + fcol * 128 + 128],
                               XN[:, k * CH:(k + 1) * CH],
                               start=(k == 0), stop=(k == DT - 1)))(),
                    deps=(deps0 if k == 0 else []), inc="TE" if k == DT - 1 else None)
            return tk

        kT_toks = []
        tG_toks = []
        tRk_toks = []
        for f in range(DT):
            bg_ = (2 * f) % 4
            bk_ = (2 * f + 1) % 4
            par = f % 2
            tgm = feat_mm(bg_, "wgate_bf", D, f,
                          [xn_done, wcast_done, bank[f"f{bg_}"]])
            tkm = feat_mm(bk_, "wqkv_bf", F3, DT + f, [bank[f"f{bk_}"]])
            tGc = E(V, (lambda bg_=bg_, par=par, f=f: lambda e: e.tensor_scalar(
                out=B[f"MB{par}"][:], in0=bankf(bg_),
                scalar1=B["biasf"][:, 16 + f:17 + f], scalar2=None, op0=ALU.add))(),
                deps=[tgm, tG_toks[f - 2] if f >= 2 else None,
                      kT_toks[f - 2] if f >= 2 else None], inc="VE")
            tG0 = E(S, (lambda par=par: lambda e: e.activation(
                out=B[f"GB{par}"][:], in_=B[f"MB{par}"][:], func=AF.Tanh,
                scale=0.5))(), deps=[tGc], inc="SE")
            tGs = E(S, (lambda par=par: lambda e: e.activation(
                out=B[f"MB{par}"][:], in_=B[f"GB{par}"][:], func=AF.Copy,
                scale=0.5, bias=0.5))(), inc="SE")
            tG_toks.append(tGs)
            bank[f"f{bg_}"] = tGc
            tK1 = E(V, (lambda bk_=bk_, par=par, f=f: lambda e: e.scalar_tensor_tensor(
                out=B[f"K1B{par}"][:], in0=bankf(bk_), scalar=B["biasf"][:, 8 + f:9 + f],
                in1=B[f"MB{par}"][:], op0=ALU.add, op1=ALU.mult))(),
                deps=[tkm, tGs, tRk_toks[f - 2] if f >= 2 else None,
                      kT_toks[f - 2] if f >= 2 else None], inc="VE")
            bank[f"f{bk_}"] = tK1
            tR = E(S, (lambda par=par: lambda e: e.activation(
                out=B[f"RB{par}"][:], in_=B[f"K1B{par}"][:], func=AF.Relu))(),
                deps=[tK1, kT_toks[f - 2] if f >= 2 else None], inc="SE")
            E(S, (lambda par=par: lambda e: e.activation(
                out=B[f"GB{par}"][:], in_=B[f"K1B{par}"][:], func=AF.Relu,
                scale=-1.0))(), inc="SE")
            tE_ = E(S, (lambda par=par: lambda e: e.activation(
                out=B[f"K1B{par}"][:], in_=B[f"GB{par}"][:], func=AF.Exp,
                scale=-1.0))(), inc="SE")
            tRk_toks.append(tE_)
            tKT = E(V, (lambda par=par, f=f: lambda e: e.tensor_tensor(
                out=B["kT"][:, f * CH:(f + 1) * CH], in0=B[f"RB{par}"][:],
                in1=B[f"K1B{par}"][:], op=ALU.add))(),
                deps=[tE_, free["kT"] if f == 0 else None], inc="VE")
            kT_toks.append(tKT)
        kT_done = kT_toks[-1]
        if c + 1 < NCH:
            A["xn_next"] = emit_LN(c + 1)
        if phase == 41 and c == 0:
            finish_probe([kT_done])
            return None

        qT_toks = []
        tRq_toks = []
        for f in range(DT):
            bq_ = (2 * f) % 4
            par = f % 2
            tqm = feat_mm(bq_, "wqkv_bf", F3, f, [bank[f"f{bq_}"]])
            tQc = E(V, (lambda bq_=bq_, par=par, f=f: lambda e: e.tensor_scalar(
                out=B[f"K1B{par}"][:], in0=bankf(bq_),
                scalar1=B["biasf"][:, f:f + 1], scalar2=None, op0=ALU.add))(),
                deps=[tqm, tRq_toks[f - 2] if f >= 2 else tRk_toks[DT - 2 + f],
                      qT_toks[f - 2] if f >= 2 else kT_toks[DT - 2 + f]],
                inc="VE")
            bank[f"f{bq_}"] = tQc
            tR = E(S, (lambda par=par: lambda e: e.activation(
                out=B[f"RB{par}"][:], in_=B[f"K1B{par}"][:], func=AF.Relu))(),
                deps=[tQc, qT_toks[f - 2] if f >= 2 else kT_toks[DT - 2 + f]],
                inc="SE")
            E(S, (lambda par=par: lambda e: e.activation(
                out=B[f"GB{par}"][:], in_=B[f"K1B{par}"][:], func=AF.Relu,
                scale=-1.0))(), inc="SE")
            tE_ = E(S, (lambda par=par: lambda e: e.activation(
                out=B[f"K1B{par}"][:], in_=B[f"GB{par}"][:], func=AF.Exp,
                scale=-1.0))(), inc="SE")
            tRq_toks.append(tE_)
            tQT = E(V, (lambda par=par, f=f: lambda e: e.tensor_tensor(
                out=B["qT"][:, f * CH:(f + 1) * CH], in0=B[f"RB{par}"][:],
                in1=B[f"K1B{par}"][:], op=ALU.add))(),
                deps=[tE_, free["qT"] if f == 0 else None], inc="VE")
            qT_toks.append(tQT)
        qT_done = qT_toks[-1]
        if phase == 42 and c == 0:
            finish_probe([qT_done])
            return None

        v_toks = []
        for tt in range(NTT):
            b0 = "f2" if tt % 2 == 0 else "f4"
            b1 = "f3" if tt % 2 == 0 else "f5"
            vb = 1024 if tt % 2 == 0 else 2048
            tvm = None
            for h in range(2):
                for k in range(DT):
                    tvm = E(T, (lambda tt=tt, h=h, k=k, vb=vb, XN=XN: lambda e: e.matmul(
                        psf[:, vb + h * 512: vb + (h + 1) * 512],
                        XN[:, k * CH + tt * 128: k * CH + (tt + 1) * 128],
                        B["wqkv_bf"][:, k * F3 + 2048 + h * 512: k * F3 + 2048 + (h + 1) * 512],
                        start=(k == 0), stop=(k == DT - 1)))(),
                        deps=([xn_done, bank[b0], bank[b1],
                               free["vtok"] if tt == 0 else None]
                              if (h == 0 and k == 0) else []),
                        inc="TE" if (h == 1 and k == DT - 1) else None)
            tve = E(V, (lambda tt=tt, vb=vb: lambda e: e.tensor_tensor(
                out=B["vtok"][:, tt * 1032: tt * 1032 + 1024], in0=psf[:, vb:vb + 1024],
                in1=B["BVBC"][:], op=ALU.add))(),
                deps=[tvm, t_consts], inc="VE")
            bank[b0] = tve
            bank[b1] = tve
            v_toks.append(tve)
        free["xnT"] = v_toks[-1]
        A["ln_ready"] = True

        if phase == 43 and c == 0:
            finish_probe([v_toks[-1]])
            return None

        at_toks = []
        for si in range(NTT):
            bk_id = si % 2
            n_si = CH - si * 128
            tat = None
            for k in range(DT):
                tat = E(T, (lambda si=si, k=k, bk_id=bk_id, n_si=n_si: lambda e: e.matmul(
                    bankf(bk_id, n_si),
                    B["kT"][:, k * CH + si * 128: k * CH + si * 128 + 128],
                    B["qT"][:, k * CH + si * 128:(k + 1) * CH],
                    start=(k == 0), stop=(k == DT - 1)))(),
                    deps=([kT_done, qT_done, bank[f"f{bk_id}"],
                           free["ATm"] if si == 0 else None] if k == 0 else []),
                    inc="TE" if k == DT - 1 else None)
            tdg = E(V, (lambda si=si, bk_id=bk_id: lambda e: e.tensor_tensor(
                out=B["ATm"][:, si * CH + si * 128: si * CH + si * 128 + 128],
                in0=bankf(bk_id, 128), in1=B["trilb"][:], op=ALU.mult))(),
                deps=[tat], inc="VE")
            tcp2 = tdg
            if si < NTT - 1:
                tcp2 = E(V, (lambda si=si, bk_id=bk_id, n_si=n_si: lambda e: e.tensor_copy(
                    out=B["ATm"][:, si * CH + si * 128 + 128: si * CH + CH],
                    in_=bankf(bk_id)[:, 128:n_si]))(), deps=[tat], inc="VE")
            bank[f"f{bk_id}"] = [tdg, tcp2]
            at_toks.append((tdg, tcp2))
        at_done_v = at_toks[-1][0]
        at_done_s = at_toks[-2][1]

        if phase == 44 and c == 0:
            finish_probe([at_toks[-1][0], at_toks[-2][1]])
            return None

        dgr = []
        for tt in range(NTT):
            for si in range(tt + 1):
                dgr.append((lambda si=si, tt=tt: (
                    lambda st, sp: lambda e: e.matmul(
                        psf[:, 3072 + tt: 3072 + tt + 1],
                        B["ATm"][:, si * CH + tt * 128: si * CH + (tt + 1) * 128],
                        B["onesb"][:, 0:1], start=st, stop=sp)))())
        if c > 0:
            for tt in range(NTT):
                for k in range(DT):
                    dgr.append((lambda tt=tt, k=k: (
                        lambda st, sp: lambda e: e.matmul(
                            psf[:, 3072 + tt: 3072 + tt + 1],
                            B["qT"][:, k * CH + tt * 128: k * CH + (tt + 1) * 128],
                            B["Sb"][:, k * 1032 + 1024: k * 1032 + 1025],
                            start=st, stop=sp)))())
        tden = None
        for i, mk in enumerate(dgr):
            tden = E(T, mk(i == 0, i == len(dgr) - 1),
                     deps=([at_done_v, at_done_s, qT_done, bank["f6"]] if i == 0 else []),
                     inc="TE" if i == len(dgr) - 1 else None)
        tdev = E(V, (lambda c=c: lambda e: e.tensor_copy(
            out=B["den_all"][:, c * NTT:(c + 1) * NTT], in_=psf[:, 3072:3072 + NTT]))(),
            deps=[tden], inc="VE")
        bank["f6"] = tdev

        if phase == 45 and c == 0:
            finish_probe([tdev])
            return None

        num_toks = []
        for j in range(DT):
            bk_id = j % 2
            grp = []
            if c > 0:
                for k in range(DT):
                    grp.append((lambda j=j, k=k, bk_id=bk_id: lambda e: e.matmul(
                        bankf(bk_id),
                        B["Sb"][:, k * 1032 + j * 128: k * 1032 + j * 128 + 128],
                        B["qT"][:, k * CH:(k + 1) * CH],
                        start=(k == 0), stop=False))())
            for si in range(NTT):
                st_ = (c == 0 and si == 0)
                grp.append((lambda j=j, si=si, st_=st_, bk_id=bk_id: lambda e: e.matmul(
                    psf[:, 512 * bk_id + si * 128: 512 * bk_id + 512],
                    B["vtok"][:, si * 1032 + j * 128: si * 1032 + j * 128 + 128],
                    B["ATm"][:, si * CH + si * 128: si * CH + CH],
                    start=st_, stop=(si == NTT - 1)))())
            tn = None
            for i, fn in enumerate(grp):
                tn = E(T, fn,
                       deps=([bank[f"f{bk_id}"], at_done_v, at_done_s,
                              v_toks[-1], qT_done, A["vts"] if j == 0 else None]
                             if i == 0 else []),
                       inc="TE" if i == len(grp) - 1 else None)
            tne = E(V, (lambda j=j, bk_id=bk_id: lambda e: e.tensor_copy(
                out=B["numTb"][:, j * CH:(j + 1) * CH], in_=bankf(bk_id)))(),
                deps=[tn, free["numTb"] if j == 0 else None], inc="VE")
            bank[f"f{bk_id}"] = tne
            num_toks.append(tne)

        if phase == 46 and c == 0:
            finish_probe([num_toks[-1]])
            return None

        kt_toks = []
        for tt in range(NTT):
            ttr = None
            for k in range(DT):
                ttr = E(T, (lambda tt=tt, k=k: lambda e: e.transpose(
                    psb[:, k * 128:(k + 1) * 128],
                    B["kT"][:, k * CH + tt * 128: k * CH + (tt + 1) * 128],
                    B["identb"][:]))(),
                    deps=([kT_done, bank["b7"], free["kTok"] if tt == 0 else None]
                          if k == 0 else []),
                    inc="TE" if k == DT - 1 else None)
            tev = E(V, (lambda tt=tt: lambda e: e.tensor_copy(
                out=B["kTok"][:, tt * D:(tt + 1) * D], in_=psb[:]))(),
                deps=[ttr], inc="VE")
            bank["b7"] = tev
            kt_toks.append(tev)

        if phase == 47 and c == 0:
            finish_probe([kt_toks[-1]])
            return None

        ds_toks = []
        for i in range(DT):
            b0 = "f2" if i % 2 == 0 else "f4"
            b1 = "f3" if i % 2 == 0 else "f5"
            vb = 1024 if i % 2 == 0 else 2048
            tds = None
            for h in range(2):
                for tt in range(NTT):
                    tds = E(T, (lambda i=i, h=h, tt=tt, vb=vb: lambda e: e.matmul(
                        psf[:, vb + h * 512: vb + (h + 1) * 512],
                        B["kTok"][:, tt * D + i * 128: tt * D + i * 128 + 128],
                        B["vtok"][:, tt * 1032 + h * 512: tt * 1032 + (h + 1) * 512],
                        start=(tt == 0), stop=(tt == NTT - 1)))(),
                        deps=([kt_toks[-1], v_toks[-1], bank[b0], bank[b1],
                               num_toks[-1] if i == 0 else None]
                              if (h == 0 and tt == 0) else []),
                        inc="TE" if (h == 1 and tt == NTT - 1) else None)
            tdk = None
            for tt in range(NTT):
                tdk = E(T, (lambda i=i, tt=tt: lambda e: e.matmul(
                    psf[:, 3072 + 8 + i: 3072 + 8 + i + 1],
                    B["kTok"][:, tt * D + i * 128: tt * D + i * 128 + 128],
                    B["vtok"][:, tt * 1032 + 1024: tt * 1032 + 1025],
                    start=(tt == 0), stop=(tt == NTT - 1)))(),
                    deps=([bank["f6"]] if (i == 0 and tt == 0) else []),
                    inc="TE" if tt == NTT - 1 else None)
            if c == 0:
                tsa = E(V, (lambda i=i, vb=vb: lambda e: e.tensor_copy(
                    out=B["Sb"][:, i * 1032: i * 1032 + 1024], in_=psf[:, vb:vb + 1024]))(),
                    deps=[tds], inc="VE")
                tsk = E(V, (lambda i=i: lambda e: e.tensor_copy(
                    out=B["Sb"][:, i * 1032 + 1024: i * 1032 + 1025],
                    in_=psf[:, 3072 + 8 + i: 3072 + 8 + i + 1]))(),
                    deps=[tdk], inc="VE")
            else:
                tsa = E(V, (lambda i=i, vb=vb: lambda e: e.tensor_tensor(
                    out=B["Sb"][:, i * 1032: i * 1032 + 1024], in0=psf[:, vb:vb + 1024],
                    in1=B["Sb"][:, i * 1032: i * 1032 + 1024], op=ALU.add))(),
                    deps=[tds], inc="VE")
                tsk = E(V, (lambda i=i: lambda e: e.tensor_tensor(
                    out=B["Sb"][:, i * 1032 + 1024: i * 1032 + 1025],
                    in0=psf[:, 3072 + 8 + i: 3072 + 8 + i + 1],
                    in1=B["Sb"][:, i * 1032 + 1024: i * 1032 + 1025], op=ALU.add))(),
                    deps=[tdk], inc="VE")
            bank[b0] = tsa
            bank[b1] = tsa
            ds_toks.append(tsk)
        A["vts"] = ds_toks[-1]
        A["ds_toks"] = ds_toks
        bank["f6"] = ds_toks[-1]
        tkb = ds_toks[-1]

        tq_sp = E(G, (lambda c=c: lambda e: e.dma_start(
            out=sc_qT[c * 128:(c + 1) * 128, :], in_=B["qT"][:]))(),
            deps=[qT_done, tden, num_toks[-1]], inc="SPQ", inc_n=16)
        tn_sp = E(G, (lambda c=c: lambda e: e.dma_start(
            out=sc_num[c * 128:(c + 1) * 128, :], in_=B["numTb"][:]))(),
            deps=[num_toks[-1]], inc="SPN", inc_n=16)

        free["qT"] = tq_sp
        free["numTb"] = tn_sp
        free["kT"] = kt_toks[-1]
        free["kTok"] = ds_toks[-1]
        free["vtok"] = ds_toks[-1]
        free["ATm"] = num_toks[-1]
        if phase == 30 + c:
            finish_probe([ds_toks[-1], tkb, tdev, free["qT"], free["numTb"]])
            return None
        return (ds_toks[-1], tkb, tdev)

    xn_cur = emit_LN(0)
    ret = None
    sw_ready = cc2_done = None
    for c in range(NCH):
        ret = emit_attn(c, xn_cur)
        if ret is None:
            esA.close()
            es.close()
            return nc
        if c == 0:
            q = emit_quant()
            if q is None:
                esA.close()
                es.close()
                return nc
        if c == 2:
            q = quant_fin()
            if q is None:
                esA.close()
                es.close()
                return nc
            sw_ready, cc2_done = q
        xn_cur = A.get("xn_next")
    S_done, ks_done, den_prev = ret

    if phase == 3:
        finish_probe([S_done, ks_done, den_prev, free["qT"], free["numTb"]])
        esA.close()
        es.close()
        return nc

    ds8 = A["ds_toks"]
    sps = []
    for i in range(DT):
        sps.append(E(G, (lambda i=i: lambda e: e.dma_start(
            out=cc3_in[i * 128:(i + 1) * 128, :],
            in_=B["Sb"][:, i * 1032:(i + 1) * 1032]))(),
            deps=[ds8[i]], inc="SPSA" if i < 4 else "SPS", inc_n=16))
    cc3a_done = E(G, lambda e: e.collective_compute(
        "AllGather", ALU.bypass, replica_groups=[list(range(NCORE))],
        ins=[cc3_in[0:512, :]], outs=[cc3a_out]), deps=[sps[3]], inc="CC")
    cc3b_done = E(G, lambda e: e.collective_compute(
        "AllGather", ALU.bypass, replica_groups=[list(range(NCORE))],
        ins=[cc3_in[512:1024, :]], outs=[cc3b_out]), deps=[sps[7]], inc="CC")
    cc3_done = cc3b_done

    a_end = [p.tok("TE"), p.tok("SE"), p.tok("VE"),
             p.tok("SPQ"), p.tok("SPN"), p.tok("SPS"), p.tok("SPSA")]
    if phase == 4:
        finish_probe([cc3_done] + a_end)
        esA.close()
        es.close()
        return nc
    esA.close()

    esB = ExitStack()
    for nm, sh, dt in [
        ("wqg", [128, DT * INNER], FP8), ("wqu", [128, DT * INNER], FP8),
        ("wqd", [128, 32 * D], FP8),
        ("Sgp", [128, DT * D], BF16), ("ksgp", [128, DT], BF16),
        ("wproj_bf", [128, DT * D], BF16),
    ]:
        sb(esB, nm, sh, dt)

    wq_toks = []
    for cbl in range(NCORE):
        dep = ([cc2_done] + a_end) if cbl == 0 else []
        wq_toks.append(E(S, (lambda cbl=cbl: lambda e: e.dma_start(
            out=B["wqg"][:, cbl * INNER:(cbl + 1) * INNER],
            in_=cc2_out[cbl * 128:(cbl + 1) * 128, 0:INNER]))(), deps=dep,
            inc="LDBW", inc_n=16))
        wq_toks.append(E(S, (lambda cbl=cbl: lambda e: e.dma_start(
            out=B["wqu"][:, cbl * INNER:(cbl + 1) * INNER],
            in_=cc2_out[cbl * 128:(cbl + 1) * 128, INNER:2 * INNER]))(),
            inc="LDBW", inc_n=16))
        wq_toks.append(E(S, (lambda cbl=cbl: lambda e: e.dma_start(
            out=B["wqd"][:, cbl * 4 * D:(cbl + 1) * 4 * D],
            in_=cc2_out[cbl * 128:(cbl + 1) * 128, 2 * INNER:2 * INNER + 4 * D]))(),
            inc="LDBW", inc_n=16))
    wq_done = wq_toks[-1]

    esX = ExitStack()
    sb(esX, "SG", [128, 2 * NCORE * 1032], BF16)
    sb(esX, "maskI", [128, NCORE * 128], BF16)
    t_mask = E(Y, lambda e: e.dma_start(out=B["maskI"][:], in_=maskI_h),
               deps=a_end + [sps[7]], inc="LDS", inc_n=16)
    red_last = None
    red_evs = []
    for i in range(DT):
        half = cc3a_out if i < 4 else cc3b_out
        hbase = (i % 4) * 128
        sg0 = (i % 2) * NCORE * 1032
        dep = ([cc3a_done if i < 4 else cc3b_done]
               + (a_end if i == 0 else [])
               + ([red_evs[i - 2]] if i >= 2 else []))
        tl = None
        for cc in range(NCORE):
            tl = E(Y, (lambda cc=cc, half=half, hbase=hbase, sg0=sg0:
                       lambda e: e.dma_start(
                out=B["SG"][:, sg0 + cc * 1032: sg0 + (cc + 1) * 1032],
                in_=half[cc * 512 + hbase: cc * 512 + hbase + 128, :]))(),
                deps=(dep if cc == 0 else []),
                inc="LDS" if i % 2 == 0 else "LDS2", inc_n=16)
        evs = []
        for h in range(2):
            tm = None
            for cbl in range(NCORE):
                tm = E(T, (lambda cbl=cbl, h=h, sg0=sg0: lambda e: e.matmul(
                    bankf(h), B["maskI"][:, cbl * 128:(cbl + 1) * 128],
                    B["SG"][:, sg0 + cbl * 1032 + h * 512:
                            sg0 + cbl * 1032 + (h + 1) * 512],
                    start=(cbl == 0), stop=(cbl == NCORE - 1)))(),
                    deps=([tl, bank[f"f{h}"], t_mask] if cbl == 0 else []),
                    inc="TE" if cbl == NCORE - 1 else None)
            tev = E(V, (lambda i=i, h=h: lambda e: e.tensor_copy(
                out=B["Sgp"][:, i * D + h * 512: i * D + (h + 1) * 512],
                in_=bankf(h)))(), deps=[tm] + (a_end if i == 0 else []), inc="VE")
            bank[f"f{h}"] = tev
            evs.append(tev)
        tm = None
        for cbl in range(NCORE):
            tm = E(T, (lambda cbl=cbl, i=i, sg0=sg0: lambda e: e.matmul(
                psf[:, 1024 + i: 1024 + i + 1],
                B["maskI"][:, cbl * 128:(cbl + 1) * 128],
                B["SG"][:, sg0 + cbl * 1032 + 1024: sg0 + cbl * 1032 + 1025],
                start=(cbl == 0), stop=(cbl == NCORE - 1)))(),
                deps=([bank["f2"]] if (i == 0 and cbl == 0) else []),
                inc="TE" if cbl == NCORE - 1 else None)
        tev = E(V, (lambda i=i: lambda e: e.tensor_copy(
            out=B["ksgp"][:, i:i + 1], in_=psf[:, 1024 + i:1024 + i + 1]))(),
            deps=[tm], inc="VE")
        if i == DT - 1:
            bank["f2"] = tev
        red_last = tev
        red_evs.append(tev)
    sgp_done = red_last
    esX.close()

    for nm, sh, dt in [
        ("QTL", [128, DT * CH], BF16), ("NTL", [128, DT * CH], BF16),
        ("XT2a", [128, D], F32), ("XT2b", [128, D], F32),
        ("X2a", [128, D], F32), ("X2b", [128, D], F32),
        ("XQ", [128, D], BF16), ("XQT0", [128, D], BF16), ("XQT1", [128, D], BF16),
        ("SILU0", [128, 512], BF16), ("SILU1", [128, 512], BF16),
        ("PBUF", [128, INNER], F32),
        ("HQ", [128, INNER], BF16), ("HQT", [128, INNER], BF16),
    ]:
        sb(esB, nm, sh, dt)

    wp_toks = []
    for k in range(DT):
        stg2 = B["XT2a"] if k % 2 == 0 else B["XT2b"]
        dep = (a_end + [cc3_done, sgp_done]) if k < 2 else [wp_toks[k - 2]]
        tl = E(Y, (lambda k=k, stg2=stg2: lambda e: e.dma_start(
            out=stg2[:], in_=wproj_h[k * 128:(k + 1) * 128, :]))(),
            deps=dep, inc=f"LDW{k % 2}", inc_n=16)
        tc = E(S, (lambda k=k, stg2=stg2: lambda e: e.copy(
            out=B["wproj_bf"][:, k * D:(k + 1) * D], in_=stg2[:]))(),
            deps=[tl] + (a_end if k == 0 else []), inc="SE")
        wp_toks.append(tc)

    if phase == 5:
        finish_probe([sgp_done, wq_done, wp_toks[-1]])
        esB.close()
        es.close()
        return nc

    st = {
        "qT_load": None, "nT_load": None, "csum_free": None, "denB": None,
        "denB2": None, "denB3": None, "drow_free": None,
        "xt2_free": [wp_toks[-2], wp_toks[-1]], "hpre_free": [None, None],
        "x2_free": [None, None],
        "swin_free": [None, None],
        "xq_free": None, "xqt_free": [None, None], "hq_free": None,
        "qa_sfree": None,
        "slot_tok": [[None, None], [None, None]],
        "hqt_free": None,
        "outst_free": None, "psb_free": bank["b7"],
        "silu_free": [None, None],
    }
    st["qT_load"] = E(Y, lambda e: e.dma_start(out=B["QTL"][:], in_=sc_qT[0:128, :]),
                      deps=[a_end[3], cc3_done], inc="LDB0", inc_n=16)
    st["nT_load"] = E(Y, lambda e: e.dma_start(out=B["NTL"][:], in_=sc_num[0:128, :]),
                      deps=[a_end[4]], inc="LDB1", inc_n=16)

    for c in range(NCH):
        comb_toks = []
        inter_last = None
        for j in range(DT):
            bk_id = j % 2
            tm = None
            for k in range(DT):
                tm = E(T, (lambda j=j, k=k, bk_id=bk_id: lambda e: e.matmul(
                    bankf(bk_id), B["Sgp"][:, k * D + j * 128: k * D + j * 128 + 128],
                    B["QTL"][:, k * CH:(k + 1) * CH],
                    start=(k == 0), stop=(k == DT - 1)))(),
                    deps=([sgp_done, st["qT_load"], bank[f"f{bk_id}"]] if k == 0 else []),
                    inc="TE" if k == DT - 1 else None)
            inter_last = tm
            tcb = E(V, (lambda j=j, bk_id=bk_id: lambda e: e.tensor_tensor(
                out=B["NTL"][:, j * CH:(j + 1) * CH], in0=bankf(bk_id),
                in1=B["NTL"][:, j * CH:(j + 1) * CH], op=ALU.add))(),
                deps=[tm, st["nT_load"], st["csum_free"] if j == 0 else None],
                inc="VE")
            bank[f"f{bk_id}"] = tcb
            comb_toks.append(tcb)
        csum_done = comb_toks[-1]

        tdn = None
        pairs = [(tt, k) for tt in range(NTT) for k in range(DT)]
        for idx, (tt, k) in enumerate(pairs):
            tdn = E(T, (lambda tt=tt, k=k, idx=idx: lambda e: e.matmul(
                psf[:, 3072 + tt:3072 + tt + 1],
                B["QTL"][:, k * CH + tt * 128: k * CH + (tt + 1) * 128],
                B["ksgp"][:, k:k + 1], start=(idx == 0), stop=(idx == len(pairs) - 1)))(),
                deps=([sgp_done, bank["f6"], st["qT_load"]] if idx == 0 else []),
                inc="TE" if idx == len(pairs) - 1 else None)
        tdt = E(V, (lambda c=c: lambda e: e.scalar_tensor_tensor(
            out=B["DTOT"][:], in0=psf[:, 3072:3072 + NTT], scalar=DEN_EPS,
            in1=B["den_all"][:, c * NTT:(c + 1) * NTT], op0=ALU.add, op1=ALU.add))(),
            deps=[tdn, st["denB3"]], inc="VE")
        trd = E(V, lambda e: e.reciprocal(out=B["RDEN"][:], in_=B["DTOT"][:]),
                deps=[tdt], inc="VE")
        st["denB3"] = trd
        drow_done = trd
        bank["f6"] = tdt

        xls = {}
        for tt in range(2):
            par = tt % 2
            XT2 = B["XT2a"] if par == 0 else B["XT2b"]
            xls[tt] = E(Y, (lambda c=c, tt=tt, XT2=XT2: lambda e: e.dma_start(
                out=XT2[:], in_=x_h[c * CH + tt * 128: c * CH + (tt + 1) * 128, :]))(),
                deps=[st["xt2_free"][par]], inc=f"LDX{par}", inc_n=16)
        if c + 1 < NCH:
            st["qT_load"] = E(Y, (lambda c=c: lambda e: e.dma_start(
                out=B["QTL"][:], in_=sc_qT[(c + 1) * 128:(c + 2) * 128, :]))(),
                deps=[tdn, csum_done], inc="LDB0", inc_n=16)
            st["nT_load"] = None

        proj_last = {}
        tile = {}

        def projxq(tt):
            par = tt % 2
            w = 48 * par
            XT2 = B["XT2a"] if par == 0 else B["XT2b"]
            X2p = B["X2a"] if par == 0 else B["X2b"]
            XQT = B["XQT0"] if par == 0 else B["XQT1"]
            pb = ("f2", "f3") if tt % 2 == 0 else ("f4", "f5")
            vb = 1024 if tt % 2 == 0 else 2048
            tpm = None
            for h in range(2):
                for k in range(DT):
                    tpm = E(T, (lambda tt=tt, h=h, k=k, vb=vb: lambda e: e.matmul(
                        psf[:, vb + h * 512: vb + (h + 1) * 512],
                        B["NTL"][:, k * CH + tt * 128: k * CH + (tt + 1) * 128],
                        B["wproj_bf"][:, k * D + h * 512: k * D + (h + 1) * 512],
                        start=(k == 0), stop=(k == DT - 1)))(),
                        deps=([csum_done, drow_done, wp_toks[-1],
                               bank[pb[0]], bank[pb[1]]] if (h == 0 and k == 0) else []),
                        inc="TE" if (h == 1 and k == DT - 1) else None)
            proj_last[tt] = tpm
            thp = E(V, (lambda tt=tt, vb=vb, XT2=XT2: lambda e:
                        e.scalar_tensor_tensor(
                            out=XT2[:], in0=psf[:, vb:vb + 1024],
                            scalar=B["RDEN"][:, tt:tt + 1], in1=XT2[:],
                            op0=ALU.mult, op1=ALU.add))(),
                deps=[tpm, xls[tt], trd], inc="VE")
            thp = E(V, (lambda XT2=XT2: lambda e: e.tensor_tensor(
                out=XT2[:], in0=XT2[:], in1=B["BPBC"][:], op=ALU.add))(), inc="VE")
            tss = E(S, (lambda XT2=XT2, X2p=X2p, w=w: lambda e: e.activation(
                out=X2p[:], in_=XT2[:], func=AF.Identity,
                accum_out=B["SC"][:, w:w + 1]))(),
                deps=[thp, st["swin_free"][par], st["x2_free"][par]], inc="SE")
            tsq2 = E(S, (lambda XT2=XT2, X2p=X2p, w=w: lambda e: e.activation(
                out=X2p[:], in_=XT2[:], func=AF.Square,
                accum_out=B["SC"][:, w + 1:w + 2]))(), inc="SE")
            E(V, (lambda w=w: lambda e: e.tensor_scalar(
                out=B["SC"][:, w + 12:w + 13], in0=B["SC"][:, w:w + 1],
                scalar1=1.0 / 1024.0, scalar2=None, op0=ALU.mult))(),
              deps=[tss, tsq2])
            E(V, (lambda w=w: lambda e: e.tensor_scalar(
                out=B["SC"][:, w + 13:w + 14], in0=B["SC"][:, w + 1:w + 2],
                scalar1=1.0 / 1024.0, scalar2=LN_EPS, op0=ALU.mult, op1=ALU.add))())
            E(V, (lambda w=w: lambda e: e.tensor_scalar(
                out=B["SC"][:, w + 14:w + 15], in0=B["SC"][:, w + 12:w + 13],
                scalar1=B["SC"][:, w + 12:w + 13], scalar2=-1.0,
                op0=ALU.mult, op1=ALU.mult))())
            E(V, (lambda w=w: lambda e: e.tensor_tensor(
                out=B["SC"][:, w + 14:w + 15], in0=B["SC"][:, w + 13:w + 14],
                in1=B["SC"][:, w + 14:w + 15], op=ALU.add))())
            tv = E(V, (lambda w=w: lambda e: e.reciprocal(
                out=B["SC"][:, w + 15:w + 16], in_=B["SC"][:, w + 14:w + 15]))(), inc="VE")
            tsq = E(S, (lambda w=w: lambda e: e.activation(
                out=B["SC"][:, w + 16:w + 17], in_=B["SC"][:, w + 15:w + 16],
                func=AF.Sqrt))(), deps=[tv], inc="SE")
            tneg = E(V, (lambda w=w: lambda e: e.tensor_scalar(
                out=B["SC"][:, w + 17:w + 18], in0=B["SC"][:, w + 12:w + 13],
                scalar1=B["SC"][:, w + 16:w + 17], scalar2=-1.0,
                op0=ALU.mult, op1=ALU.mult))(), deps=[tsq], inc="VE")
            tx2 = E(S, (lambda XT2=XT2, X2p=X2p, w=w: lambda e: e.activation(
                out=X2p[:], in_=XT2[:], func=AF.Identity,
                scale=B["SC"][:, w + 16:w + 17], bias=B["SC"][:, w + 17:w + 18]))(),
                deps=[tneg, st["x2_free"][par]], inc="SE")
            E(V, (lambda X2p=X2p, w=w: lambda e: e.tensor_reduce(
                out=B["SC"][:, w + 18:w + 19], in_=X2p[:],
                axis=AX, op=ALU.max, apply_absolute_value=True))(), deps=[tx2])
            E(V, (lambda w=w: lambda e: e.tensor_scalar(
                out=B["SC"][:, w + 19:w + 20], in0=B["SC"][:, w + 18:w + 19],
                scalar1=1e-5, scalar2=None, op0=ALU.max))())
            E(V, (lambda w=w: lambda e: e.reciprocal(
                out=B["SC"][:, w + 20:w + 21], in_=B["SC"][:, w + 19:w + 20]))())
            ts1 = E(V, (lambda w=w: lambda e: e.tensor_scalar(
                out=B["SC"][:, w + 21:w + 22], in0=B["SC"][:, w + 20:w + 21],
                scalar1=127.0, scalar2=None, op0=ALU.mult))(), inc="VE")
            tc1g = E(V, (lambda w=w: lambda e: e.tensor_scalar(
                out=B["SC"][:, w + 22:w + 23], in0=B["SC"][:, w + 19:w + 20],
                scalar1=B["MGV"][:, 0:1], scalar2=1.0 / 127.0,
                op0=ALU.mult, op1=ALU.mult))(), deps=[sw_ready], inc="VE")
            tc1u = E(V, (lambda w=w: lambda e: e.tensor_scalar(
                out=B["SC"][:, w + 23:w + 24], in0=B["SC"][:, w + 19:w + 20],
                scalar1=B["MGV"][:, 1:2], scalar2=1.0 / 127.0,
                op0=ALU.mult, op1=ALU.mult))(), inc="VE")
            txq = None
            for hh in range(2):
                E(V, (lambda X2p=X2p, XT2=XT2, hh=hh, w=w: lambda e: e.tensor_scalar(
                    out=XT2[:, 0:512], in0=X2p[:, hh * 512:(hh + 1) * 512],
                    scalar1=B["SC"][:, w + 21:w + 22], scalar2=MAGIC,
                    op0=ALU.mult, op1=ALU.add))(),
                  deps=([ts1, st["qa_sfree"]] if hh == 0 else []))
                txq = E(V, (lambda XT2=XT2, hh=hh: lambda e: e.tensor_scalar(
                    out=B["XQ"][:, hh * 512:(hh + 1) * 512], in0=XT2[:, 0:512],
                    scalar1=MAGIC, scalar2=None, op0=ALU.subtract))(),
                    deps=[st["xq_free"] if hh == 0 else None], inc="VE")
            ttr = None
            for k in range(DT):
                ttr = E(T, (lambda k=k: lambda e: e.transpose(
                    psb[:, k * 128:(k + 1) * 128], B["XQ"][:, k * 128:(k + 1) * 128],
                    B["identb"][:]))(),
                    deps=([txq, st["psb_free"]] if k == 0 else []),
                    inc="TE" if k == DT - 1 else None)
            st["xq_free"] = ttr
            txqt = E(V, (lambda XQT=XQT: lambda e: e.tensor_copy(
                out=XQT[:], in_=psb[:]))(),
                deps=[ttr, st["xqt_free"][par]], inc="VE")
            st["psb_free"] = txqt
            tile[tt] = {"tx2": tx2, "txqt": txqt, "tc1g": tc1g, "tc1u": tc1u,
                        "txq": txq}

        def gup(tt):
            par = tt % 2
            w = 48 * par
            XQT = B["XQT0"] if par == 0 else B["XQT1"]
            XT2 = B["XT2a"] if par == 0 else B["XT2b"]
            txqt = tile[tt]["txqt"]
            tc1g = tile[tt]["tc1g"]
            tc1u = tile[tt]["tc1u"]
            gu_last = None
            pb_toks = []
            for ic in range(NIC):
                gb = f"f{ic % 2}"
                ub = f"f{2 + ic % 2}"
                tgm = None
                for k in range(DT):
                    tgm = E(T, (lambda ic=ic, k=k, gb=gb, XQT=XQT: lambda e: e.matmul(
                        bankf(int(gb[1])), XQT[:, k * 128:(k + 1) * 128],
                        B["wqg"][:, k * INNER + ic * 512: k * INNER + (ic + 1) * 512],
                        start=(k == 0), stop=(k == DT - 1)))(),
                        deps=([txqt, wq_done, bank[gb]] if k == 0 else []),
                        inc="TE" if k == DT - 1 else None)
                tum = None
                for k in range(DT):
                    tum = E(T, (lambda ic=ic, k=k, ub=ub, XQT=XQT: lambda e: e.matmul(
                        bankf(int(ub[1])), XQT[:, k * 128:(k + 1) * 128],
                        B["wqu"][:, k * INNER + ic * 512: k * INNER + (ic + 1) * 512],
                        start=(k == 0), stop=(k == DT - 1)))(),
                        deps=([bank[ub]] if k == 0 else []),
                        inc="TE" if k == DT - 1 else None)
                tslc = E(V, (lambda ic=ic, gb=gb, w=w, XT2=XT2: lambda e: e.tensor_scalar(
                    out=XT2[:, (ic % 2) * 512:(ic % 2) * 512 + 512],
                    in0=bankf(int(gb[1])),
                    scalar1=B["SC"][:, w + 22:w + 23], scalar2=None, op0=ALU.mult))(),
                    deps=[tgm, tc1g, st["slot_tok"][par][ic % 2]], inc="VE")
                tsl = E(S, (lambda ic=ic, XT2=XT2: lambda e: e.activation(
                    out=B[f"SILU{ic % 2}"][:],
                    in_=XT2[:, (ic % 2) * 512:(ic % 2) * 512 + 512],
                    func=AF.Silu))(),
                    deps=[tslc, st["silu_free"][ic % 2]], inc="SE")
                st["slot_tok"][par][ic % 2] = tsl
                bank[gb] = tslc
                tup = E(V, (lambda ic=ic, ub=ub, w=w: lambda e: e.tensor_scalar(
                    out=B["PBUF"][:, ic * 512:(ic + 1) * 512], in0=bankf(int(ub[1])),
                    scalar1=B["SC"][:, w + 23:w + 24], scalar2=None, op0=ALU.mult))(),
                    deps=[tum, tc1u], inc="VE")
                bank[ub] = tup
                tpp = E(V, (lambda ic=ic: lambda e: e.tensor_tensor(
                    out=B["PBUF"][:, ic * 512:(ic + 1) * 512],
                    in0=B["PBUF"][:, ic * 512:(ic + 1) * 512],
                    in1=B[f"SILU{ic % 2}"][:], op=ALU.mult))(),
                    deps=[tsl], inc="VE")
                st["silu_free"][ic % 2] = tpp
                tpm8 = E(V, (lambda ic=ic: lambda e: e.tensor_reduce(
                    out=B["PM8"][:, ic:ic + 1], in_=B["PBUF"][:, ic * 512:(ic + 1) * 512],
                    axis=AX, op=ALU.max, apply_absolute_value=True))(), inc="VE")
                pb_toks.append(tpm8)
                gu_last = tum
            tile[tt]["gu_last"] = gu_last
            tile[tt]["pb_done"] = pb_toks[-1]

        def tail_rounds(tt):
            par = tt % 2
            w = 48 * par
            E(V, (lambda w=w: lambda e: e.tensor_reduce(
                out=B["SC"][:, w + 24:w + 25], in_=B["PM8"][:], axis=AX, op=ALU.max))(),
              deps=[tile[tt]["pb_done"]])
            E(V, (lambda w=w: lambda e: e.tensor_scalar(
                out=B["SC"][:, w + 25:w + 26], in0=B["SC"][:, w + 24:w + 25],
                scalar1=1e-5, scalar2=None, op0=ALU.max))())
            E(V, (lambda w=w: lambda e: e.reciprocal(
                out=B["SC"][:, w + 26:w + 27], in_=B["SC"][:, w + 25:w + 26]))())
            ts2 = E(V, (lambda w=w: lambda e: e.tensor_scalar(
                out=B["SC"][:, w + 27:w + 28], in0=B["SC"][:, w + 26:w + 27],
                scalar1=127.0, scalar2=None, op0=ALU.mult))(), inc="VE")
            tc2 = E(V, (lambda w=w: lambda e: e.tensor_scalar(
                out=B["SC"][:, w + 28:w + 29], in0=B["SC"][:, w + 25:w + 26],
                scalar1=B["MGV"][:, 2:3], scalar2=1.0 / 127.0,
                op0=ALU.mult, op1=ALU.mult))(), inc="VE")
            XT2 = B["XT2a"] if par == 0 else B["XT2b"]
            rnd_last = None
            for ic in range(NIC):
                E(V, (lambda ic=ic, w=w, XT2=XT2: lambda e: e.tensor_scalar(
                    out=XT2[:, (ic % 2) * 512:(ic % 2) * 512 + 512],
                    in0=B["PBUF"][:, ic * 512:(ic + 1) * 512],
                    scalar1=B["SC"][:, w + 27:w + 28], scalar2=MAGIC,
                    op0=ALU.mult, op1=ALU.add))(),
                  deps=([ts2] if ic == 0 else []))
                rnd_last = E(V, (lambda ic=ic, XT2=XT2: lambda e: e.tensor_scalar(
                    out=B["HQ"][:, ic * 512:(ic + 1) * 512],
                    in0=XT2[:, (ic % 2) * 512:(ic % 2) * 512 + 512],
                    scalar1=MAGIC, scalar2=None, op0=ALU.subtract))(),
                    deps=[st["hq_free"] if ic == 0 else None], inc="VE")
            tile[tt]["rnd_last"] = rnd_last
            tile[tt]["tc2"] = tc2

        def tail_down(tt):
            par = tt % 2
            w = 48 * par
            XT2 = B["XT2a"] if par == 0 else B["XT2b"]
            X2p = B["X2a"] if par == 0 else B["X2b"]
            rnd_last = tile[tt]["rnd_last"]
            tc2 = tile[tt]["tc2"]
            tx2 = tile[tt]["tx2"]
            hqt_evs = []
            ttr2l = None
            for r in range(4):
                ttr2 = None
                for jj in range(8):
                    jblk = r * 8 + jj
                    ttr2 = E(T, (lambda jblk=jblk, jj=jj: lambda e: e.transpose(
                        psb[:, jj * 128:(jj + 1) * 128],
                        B["HQ"][:, jblk * 128:(jblk + 1) * 128],
                        B["identb"][:]))(),
                        deps=([rnd_last, st["psb_free"],
                               st["hqt_free"] if r == 0 else None] if jj == 0 else []),
                        inc="TE" if jj == 7 else None)
                ttr2l = ttr2
                tev = E(V, (lambda r=r: lambda e: e.tensor_copy(
                    out=B["HQT"][:, r * 1024:(r + 1) * 1024], in_=psb[:]))(),
                    deps=[ttr2], inc="VE")
                st["psb_free"] = tev
                hqt_evs.append(tev)
            st["hq_free"] = ttr2l
            db = ("f4", "f5") if tt % 2 == 0 else ("f2", "f3")
            dbc = 2048 if tt % 2 == 0 else 1024
            tdm = None
            for h in range(2):
                for j in range(32):
                    tdm = E(T, (lambda h=h, j=j, dbc=dbc: lambda e: e.matmul(
                        psf[:, dbc + h * 512: dbc + (h + 1) * 512],
                        B["HQT"][:, j * 128:(j + 1) * 128],
                        B["wqd"][:, j * D + h * 512: j * D + (h + 1) * 512],
                        start=(j == 0), stop=(j == 31)))(),
                        deps=([hqt_evs[-1], bank[db[0]], bank[db[1]]]
                              if (h == 0 and j == 0) else []),
                        inc="TE" if (h == 1 and j == 31) else None)
            st["hqt_free"] = tdm
            st["xqt_free"][par] = tile[tt]["gu_last"]
            th2 = E(V, (lambda X2p=X2p, XT2=XT2, dbc=dbc, w=w: lambda e:
                        e.scalar_tensor_tensor(
                out=XT2[:], in0=psf[:, dbc:dbc + 1024], scalar=B["SC"][:, w + 28:w + 29],
                in1=X2p[:], op0=ALU.mult, op1=ALU.add))(),
                deps=[tdm, tc2, tx2], inc="VE")
            bank[db[0]] = th2
            bank[db[1]] = th2
            tss3 = E(S, (lambda XT2=XT2, X2p=X2p, w=w: lambda e: e.activation(
                out=X2p[:], in_=XT2[:], func=AF.Identity,
                accum_out=B["SC"][:, w + 29:w + 30]))(), deps=[th2], inc="SE")
            tsq3p = E(S, (lambda XT2=XT2, X2p=X2p, w=w: lambda e: e.activation(
                out=X2p[:], in_=XT2[:], func=AF.Square,
                accum_out=B["SC"][:, w + 30:w + 31]))(), inc="SE")
            E(V, (lambda w=w: lambda e: e.tensor_scalar(
                out=B["SC"][:, w + 41:w + 42], in0=B["SC"][:, w + 29:w + 30],
                scalar1=1.0 / 1024.0, scalar2=None, op0=ALU.mult))(),
              deps=[tss3, tsq3p])
            E(V, (lambda w=w: lambda e: e.tensor_scalar(
                out=B["SC"][:, w + 42:w + 43], in0=B["SC"][:, w + 30:w + 31],
                scalar1=1.0 / 1024.0, scalar2=LN_EPS, op0=ALU.mult, op1=ALU.add))())
            E(V, (lambda w=w: lambda e: e.tensor_scalar(
                out=B["SC"][:, w + 43:w + 44], in0=B["SC"][:, w + 41:w + 42],
                scalar1=B["SC"][:, w + 41:w + 42], scalar2=-1.0,
                op0=ALU.mult, op1=ALU.mult))())
            E(V, (lambda w=w: lambda e: e.tensor_tensor(
                out=B["SC"][:, w + 43:w + 44], in0=B["SC"][:, w + 42:w + 43],
                in1=B["SC"][:, w + 43:w + 44], op=ALU.add))())
            tv3 = E(V, (lambda w=w: lambda e: e.reciprocal(
                out=B["SC"][:, w + 44:w + 45], in_=B["SC"][:, w + 43:w + 44]))(), inc="VE")
            tsq3 = E(S, (lambda w=w: lambda e: e.activation(
                out=B["SC"][:, w + 45:w + 46], in_=B["SC"][:, w + 44:w + 45],
                func=AF.Sqrt))(), deps=[tv3], inc="SE")
            tneg3 = E(V, (lambda w=w: lambda e: e.tensor_scalar(
                out=B["SC"][:, w + 46:w + 47], in0=B["SC"][:, w + 41:w + 42],
                scalar1=B["SC"][:, w + 45:w + 46], scalar2=-1.0,
                op0=ALU.mult, op1=ALU.mult))(), deps=[tsq3], inc="VE")
            tout = E(S, (lambda XT2=XT2, X2p=X2p, w=w: lambda e: e.activation(
                out=X2p[:], in_=XT2[:], func=AF.Identity,
                scale=B["SC"][:, w + 45:w + 46], bias=B["SC"][:, w + 46:w + 47]))(),
                deps=[tneg3, th2, st["outst_free"]], inc="SE")
            st["swin_free"][par] = tout
            st["xt2_free"][par] = tout
            tst = E(G, (lambda c=c, tt=tt, X2p=X2p: lambda e: e.dma_start(
                out=out_h[c * CH + tt * 128: c * CH + (tt + 1) * 128, :],
                in_=X2p[:]))(), deps=[tout], inc="SPO", inc_n=16)
            st["outst_free"] = tst
            st["x2_free"][par] = tst
            if tt + 2 < NTT:
                XT2n = B["XT2a"] if par == 0 else B["XT2b"]
                xls[tt + 2] = E(Y, (lambda c=c, tt=tt, XT2n=XT2n: lambda e: e.dma_start(
                    out=XT2n[:],
                    in_=x_h[c * CH + (tt + 2) * 128: c * CH + (tt + 3) * 128, :]))(),
                    deps=[tout], inc=f"LDX{par}", inc_n=16)

        projxq(0)
        projxq(1)
        gup(0)
        tail_rounds(0)
        tail_down(0)
        gup(1)
        tail_rounds(1)
        projxq(2)
        tail_down(1)
        gup(2)
        tail_rounds(2)
        projxq(3)
        tail_down(2)
        gup(3)
        tail_rounds(3)
        tail_down(3)
        st["csum_free"] = proj_last[NTT - 1]
        st["drow_free"] = proj_last[NTT - 1]
        if phase == 6 and c == 0:
            break
        if c + 1 < NCH:
            st["nT_load"] = E(Y, (lambda c=c: lambda e: e.dma_start(
                out=B["NTL"][:], in_=sc_num[(c + 1) * 128:(c + 2) * 128, :]))(),
                deps=[proj_last[NTT - 1]], inc="LDB1", inc_n=16)

    E(G, lambda e: e.nop(), deps=[p.tok("SPO")])
    replay_all()
    esB.close()
    es.close()
    return nc


_NC_CACHE = None


def kernel(L0_attn_x, L0_attn_w_qkv, L0_attn_b_qkv, L0_attn_w_gate, L0_attn_b_gate,
           L0_attn_w_proj, L0_attn_b_proj, L0_attn_ln_g, L0_attn_ln_b,
           L0_ffn_w_gate, L0_ffn_w_up, L0_ffn_w_down):
    global _NC_CACHE
    x = np.ascontiguousarray(np.asarray(L0_attn_x, dtype=np.float32).reshape(16384, 1024))
    wqkv = np.ascontiguousarray(np.asarray(L0_attn_w_qkv, dtype=np.float32))
    bqkv = np.asarray(L0_attn_b_qkv, dtype=np.float32)
    wgate = np.ascontiguousarray(np.asarray(L0_attn_w_gate, dtype=np.float32))
    bgate = np.asarray(L0_attn_b_gate, dtype=np.float32)
    wproj = np.ascontiguousarray(np.asarray(L0_attn_w_proj, dtype=np.float32))
    bproj = np.asarray(L0_attn_b_proj, dtype=np.float32)
    lng = np.asarray(L0_attn_ln_g, dtype=np.float32)
    lnb = np.asarray(L0_attn_ln_b, dtype=np.float32)
    wg = np.asarray(L0_ffn_w_gate, dtype=np.float32)
    wu = np.asarray(L0_ffn_w_up, dtype=np.float32)
    wd = np.asarray(L0_ffn_w_down, dtype=np.float32)

    biasf = np.zeros((128, 40), np.float32)
    biasf[:, 0:8] = bqkv[0:1024].reshape(8, 128).T
    biasf[:, 8:16] = bqkv[1024:2048].reshape(8, 128).T
    biasf[:, 16:24] = bgate.reshape(8, 128).T
    biasf[:, 24:32] = lng.reshape(8, 128).T
    biasf[:, 32:40] = lnb.reshape(8, 128).T
    bvbc = np.ascontiguousarray(np.tile(bqkv[2048:3072].reshape(1, 1024), (128, 1)))
    bpbc = np.ascontiguousarray(np.tile(bproj.reshape(1, 1024), (128, 1)))

    identf = np.eye(128, dtype=np.float32)
    identb = np.eye(128, dtype=ml_dtypes.bfloat16)
    sidx = np.arange(128)
    trilb = (sidx[:, None] <= sidx[None, :]).astype(ml_dtypes.bfloat16)
    onesb = np.ones((128, 128), dtype=ml_dtypes.bfloat16)
    onesf = np.ones((128, 128), dtype=np.float32)

    wgT = np.ascontiguousarray(wg.T)
    wuT = np.ascontiguousarray(wu.T)
    wdT = np.ascontiguousarray(wd.T)

    if _NC_CACHE is None:
        _NC_CACHE = build_nc()
    nc = _NC_CACHE
    in_maps = []
    for c in range(NCORE):
        maskI = np.concatenate(
            [(identb if cc < c else np.zeros_like(identb)) for cc in range(NCORE)],
            axis=1)
        wsh = np.concatenate([
            wgT[c * 128:(c + 1) * 128, :],
            wuT[c * 128:(c + 1) * 128, :],
            wdT[c * 512:(c + 1) * 512, :].reshape(4, 128, 1024)
               .transpose(1, 0, 2).reshape(128, 4096),
        ], axis=1)
        in_maps.append({
            "x": np.ascontiguousarray(x[c * TC:(c + 1) * TC]),
            "wqkv": wqkv, "wgate": wgate, "wproj": wproj,
            "biasf": biasf, "bvbc": bvbc, "bpbc": bpbc,
            "identf": identf, "identb": identb, "trilb": trilb,
            "onesb": onesb, "onesf": onesf,
            "maskI": np.ascontiguousarray(maskI),
            "wsh": np.ascontiguousarray(wsh),
        })
    res = run_bass_kernel_spmd(nc, in_maps, core_ids=list(range(NCORE)))
    out = np.concatenate([res.results[c]["out"] for c in range(NCORE)], axis=0)
    return out.reshape(1, 16384, 1024)


# version 19
# speedup vs baseline: 1.0164x; 1.0164x over previous
import numpy as np
import ml_dtypes
from contextlib import ExitStack

import concourse.bass as bass
import concourse.mybir as mybir
from concourse.bass_utils import run_bass_kernel_spmd

F32 = mybir.dt.float32
BF16 = mybir.dt.bfloat16
FP8 = mybir.dt.float8e4
ALU = mybir.AluOpType
AF = mybir.ActivationFunctionType
AX = mybir.AxisListType.X

MAGIC = float(1.5 * 2 ** 23)
NCORE = 8
TC = 2048
CH = 512
NCH = TC // CH
NTT = CH // 128
D = 1024
DT = D // 128
F3 = 3072
INNER = 4096
NIC = INNER // 512
LN_EPS = 1e-5
DEN_EPS = 1e-6
WQN = INNER * D


class Prog:
    ENGINES = ("t", "s", "v", "g", "y")

    def __init__(self):
        self.ops = {e: [] for e in self.ENGINES}
        self.cnt = {}
        self.waited = {e: {} for e in self.ENGINES}

    def tok(self, sem):
        return (sem, self.cnt.get(sem, 0))

    AUTO = {"s": "SE", "v": "VE", "g": "GE"}
    AUTO_CHAIN = True

    def emit(self, eng, fn, deps=(), inc=None, inc_n=1):
        deps = list(deps)
        auto = self.AUTO.get(eng) if self.AUTO_CHAIN else None
        if auto is not None:
            deps.append((auto, self.cnt.get(auto, 0)))
            if inc is None:
                inc, inc_n = auto, 1
        w = self.waited[eng]
        flat = []
        for d in deps:
            if d is None:
                continue
            if isinstance(d, list):
                flat.extend(x for x in d if x is not None)
            else:
                flat.append(d)
        best = {}
        for sem, val in flat:
            if val > best.get(sem, 0):
                best[sem] = val
        dd = []
        for sem, val in best.items():
            if val > 0 and w.get(sem, 0) < val:
                w[sem] = val
                dd.append((sem, val))
        if inc is not None:
            self.cnt[inc] = self.cnt.get(inc, 0) + inc_n
        self.ops[eng].append((dd, fn, (inc, inc_n) if inc is not None else None))
        return self.tok(inc) if inc is not None else None


PHASE_FULL = 99


def build_nc(phase=PHASE_FULL):
    nc = bass.Bass("TRN2", target_bir_lowering=False)

    x_h = nc.dram_tensor("x", [TC, D], F32, kind="ExternalInput").ap()
    wqkv_h = nc.dram_tensor("wqkv", [D, F3], F32, kind="ExternalInput").ap()
    wgate_h = nc.dram_tensor("wgate", [D, D], F32, kind="ExternalInput").ap()
    wproj_h = nc.dram_tensor("wproj", [D, D], F32, kind="ExternalInput").ap()
    biasf_h = nc.dram_tensor("biasf", [128, 40], F32, kind="ExternalInput").ap()
    bvbc_h = nc.dram_tensor("bvbc", [128, D], F32, kind="ExternalInput").ap()
    bpbc_h = nc.dram_tensor("bpbc", [128, D], F32, kind="ExternalInput").ap()
    identf_h = nc.dram_tensor("identf", [128, 128], F32, kind="ExternalInput").ap()
    identb_h = nc.dram_tensor("identb", [128, 128], BF16, kind="ExternalInput").ap()
    trilb_h = nc.dram_tensor("trilb", [128, 128], BF16, kind="ExternalInput").ap()
    onesb_h = nc.dram_tensor("onesb", [128, 128], BF16, kind="ExternalInput").ap()
    onesf_h = nc.dram_tensor("onesf", [128, 128], F32, kind="ExternalInput").ap()
    maskI_h = nc.dram_tensor("maskI", [128, NCORE * 128], BF16, kind="ExternalInput").ap()
    wsh_h = nc.dram_tensor("wsh", [128, 12288], F32, kind="ExternalInput").ap()
    out_h = nc.dram_tensor("out", [TC, D], F32, kind="ExternalOutput").ap()

    sc_qT = nc.dram_tensor("sc_qT", [NCH * 128, DT * CH], BF16).ap()
    sc_num = nc.dram_tensor("sc_num", [NCH * 128, DT * CH], BF16).ap()
    cc1_in = nc.dram_tensor("cc1_in", [1, 4], F32).ap()
    cc1_out = nc.dram_tensor("cc1_out", [NCORE, 4], F32).ap()
    cc2_in = nc.dram_tensor("cc2_in", [128, 12288], FP8).ap()
    cc2_out = nc.dram_tensor("cc2_out", [NCORE * 128, 12288], FP8).ap()
    cc3_in = nc.dram_tensor("cc3_in", [D, 1032], BF16).ap()
    cc3a_out = nc.dram_tensor("cc3a_out", [NCORE * 512, 1032], BF16).ap()
    cc3b_out = nc.dram_tensor("cc3b_out", [NCORE * 512, 1032], BF16).ap()

    p = Prog()
    es = ExitStack()
    sems = {}
    for sname in ("LDC", "LDG", "LDW0", "LDW1", "LDX0", "LDX1", "LDB0", "LDB1",
                  "LDBW", "LDS", "LDS2", "SPQ", "SPN", "SPW0", "SPW1", "SPC", "SPO",
                  "SPS", "SPSA", "TE", "SE", "VE", "GE", "CC"):
        sems[sname] = es.enter_context(nc.semaphore(sname))

    B = {}

    def sb(stack, name, shape, dt):
        B[name] = stack.enter_context(nc.sbuf_tensor("sb_" + name, shape, dt))
        return B[name]

    for nm, sh, dt in [
        ("identb", [128, 128], BF16), ("BPBC", [128, D], F32),
        ("den_all", [128, NCH * NTT], F32), ("SC", [128, 96], F32),
        ("MGV", [128, 4], F32), ("SWV", [128, 4], F32),
        ("DTOT", [128, NTT], F32), ("RDEN", [128, NTT], F32), ("PM8", [128, 8], F32),
    ]:
        sb(es, nm, sh, dt)

    psf = es.enter_context(nc.psum_tensor("psf", [128, 3584], F32))
    psb = es.enter_context(nc.psum_tensor("psb", [128, 1024], BF16))

    def bankf(i, w=512):
        return psf[:, 512 * i: 512 * i + w]

    T, S, V, G, Y = "t", "s", "v", "g", "y"

    def E(eng, fn, deps=(), inc=None, inc_n=1):
        return p.emit(eng, fn, deps, inc, inc_n)

    bank = {f"f{i}": None for i in range(7)}
    bank["b7"] = None

    def finish_probe(deps):
        tpr = E(G, lambda e: e.dma_start(out=out_h[0:128, 0:96], in_=B["SC"][:]),
                deps=deps, inc="SPO", inc_n=16)
        E(G, lambda e: e.nop(), deps=[p.tok("SPO")])
        replay_all()
        return True

    replay_done = []

    def replay_all():
        with nc.Block() as block:
            def mk_replay(eng_key):
                def _(eng):
                    for deps, fn, inc in p.ops[eng_key]:
                        for sem, val in deps:
                            eng.wait_ge(sems[sem], val)
                        instr = fn(eng)
                        if inc is not None:
                            instr.then_inc(sems[inc[0]], inc[1])
                return _
            block.tensor(mk_replay("t"))
            block.scalar(mk_replay("s"))
            block.vector(mk_replay("v"))
            block.gpsimd(mk_replay("g"))
            block.sync(mk_replay("y"))
        replay_done.append(True)

    E(Y, lambda e: e.dma_start(out=B["identb"][:], in_=identb_h), inc="LDC", inc_n=16)

    E(V, lambda e: e.memset(B["SC"][:], 0.0))
    E(Y, lambda e: e.dma_start(out=B["BPBC"][:], in_=bpbc_h), inc="LDC", inc_n=16)
    t_rowcast = E(S, lambda e: e.nop(), inc="SE")

    esA = ExitStack()
    for nm, sh, dt in [
        ("trilb", [128, 128], BF16), ("onesb", [128, 128], BF16),
        ("onesf", [128, 128], F32), ("BVBC", [128, D], F32),
        ("identf", [128, 128], F32), ("biasf", [128, 40], F32),
        ("CC1S", [128, 4], F32), ("CC1G", [128, 4], F32),
        ("ABSA", [128, 12], F32), ("AB3", [128, 4], F32),
        ("wqkv_bf", [128, DT * F3], BF16), ("wgate_bf", [128, DT * D], BF16),
        ("Sb", [128, DT * 1032], BF16),
        ("WST0", [128, 1024], F32), ("WST1", [128, 1024], F32),
        ("XT0", [128, D], F32), ("XT1", [128, D], F32),
        ("XH0", [128, D], F32), ("XH1", [128, D], F32),
        ("xnT0", [128, DT * CH], BF16), ("xnT1", [128, DT * CH], BF16),
        ("qT", [128, DT * CH], BF16),
        ("kT", [128, DT * CH], BF16), ("kTok", [128, NTT * D], BF16),
        ("vtok", [128, NTT * 1032], BF16), ("ATm", [128, NTT * CH], BF16),
        ("numTb", [128, DT * CH], BF16),
        ("GB0", [128, 512], F32), ("GB1", [128, 512], F32),
        ("K1B0", [128, 512], F32), ("K1B1", [128, 512], F32),
        ("RB0", [128, 512], F32), ("RB1", [128, 512], F32),
        ("MB0", [128, 512], F32), ("MB1", [128, 512], F32),
        ("JUNK", [128, 1024], F32), ("JK2", [128, 1024], F32),
        ("WQO0", [128, 1024], FP8), ("WQO1", [128, 1024], FP8),
    ]:
        sb(esA, nm, sh, dt)

    for nm, srch in (("trilb", trilb_h), ("onesb", onesb_h), ("onesf", onesf_h),
                     ("BVBC", bvbc_h), ("identf", identf_h), ("biasf", biasf_h)):
        E(Y, (lambda nm=nm, srch=srch: lambda e: e.dma_start(out=B[nm][:], in_=srch))(),
          deps=[t_rowcast], inc="LDC", inc_n=16)
    t_consts = p.tok("LDC")
    for tt in range(NTT):
        E(V, (lambda tt=tt: lambda e: e.memset(
            B["vtok"][:, tt * 1032 + 1024: tt * 1032 + 1025], 1.0))())
        E(V, (lambda tt=tt: lambda e: e.memset(
            B["vtok"][:, tt * 1032 + 1025: tt * 1032 + 1032], 0.0))())
    for i in range(DT):
        E(V, (lambda i=i: lambda e: e.memset(
            B["Sb"][:, i * 1032 + 1025: i * 1032 + 1032], 0.0))())
    E(V, lambda e: e.memset(B["CC1G"][:], 0.0))

    cast_toks = []
    pieces = []
    for k in range(DT):
        for c0 in (0, 1024, 2048):
            pieces.append((wqkv_h, "wqkv_bf", F3, k, c0, 1024))
    for k in range(DT):
        pieces.append((wgate_h, "wgate_bf", D, k, 0, 1024))
    for idx, (srch, dstbuf, wdt, k, c0, wlen) in enumerate(pieces):
        wst = B[f"WST{idx % 2}"]
        deps = [cast_toks[idx - 2]] if idx >= 2 else [t_rowcast]
        tl = E(Y, (lambda wst=wst, srch=srch, k=k, c0=c0, wlen=wlen: lambda e: e.dma_start(
            out=wst[:, 0:wlen], in_=srch[k * 128:(k + 1) * 128, c0:c0 + wlen]))(),
            deps=deps, inc=f"LDW{idx % 2}", inc_n=16)
        tc = E(S, (lambda wst=wst, k=k, dstbuf=dstbuf, wdt=wdt, c0=c0, wlen=wlen:
                   lambda e: e.copy(
            out=B[dstbuf][:, k * wdt + c0: k * wdt + c0 + wlen], in_=wst[:, 0:wlen]))(),
            deps=[tl, t_rowcast], inc="SE")
        cast_toks.append(tc)
    wcast_done = cast_toks[-1]
    if phase == 1:
        finish_probe([wcast_done])
        esA.close()
        esB = None
        es.close()
        return nc

    def emit_quant():
        abs_toks = []
        for i in range(12):
            wst = B[f"WST{i % 2}"]
            deps = [wcast_done] if i < 2 else [abs_toks[i - 2]]
            tl = E(Y, (lambda i=i, wst=wst: lambda e: e.dma_start(
                out=wst[:, 0:1024], in_=wsh_h[:, i * 1024:(i + 1) * 1024]))(),
                deps=deps, inc=f"LDW{i % 2}", inc_n=16)
            ta = E(S, (lambda i=i, wst=wst: lambda e: e.activation(
                out=B["JUNK"][:], in_=wst[:, 0:1024], func=AF.Abs,
                accum_out=B["ABSA"][:, i:i + 1]))(),
                deps=[tl] + ([abs_toks[i - 1]] if i >= 1 else []), inc="SE")
            abs_toks.append(ta)
        E(V, lambda e: e.memset(B["AB3"][:, 3:4], 0.0), deps=[t_rowcast])
        tr = None
        for m in range(3):
            tr = E(V, (lambda m=m: lambda e: e.reduce_sum(
                out=B["AB3"][:, m:m + 1], in_=B["ABSA"][:, 4 * m:4 * m + 4], axis=AX))(),
                deps=[abs_toks[4 * m + 3]], inc="VE")
        tmm = E(T, lambda e: e.matmul(psf[:, 3072:3076], B["onesf"][:],
                                      B["AB3"][:, 0:4], start=True, stop=True),
                deps=[tr, t_consts, bank["f6"]], inc="TE")
        tcp = E(V, lambda e: e.tensor_copy(out=B["CC1S"][:], in_=psf[:, 3072:3076]),
                deps=[tmm], inc="VE")
        tspw = E(G, lambda e: e.dma_start(out=cc1_in, in_=B["CC1S"][0:1, 0:4]),
                 deps=[tcp], inc="SPC", inc_n=16)
        tcc1 = E(G, lambda e: e.collective_compute(
            "AllGather", ALU.bypass, replica_groups=[list(range(NCORE))],
            ins=[cc1_in], outs=[cc1_out]), deps=[tspw], inc="CC")
        A["quant_cc1"] = tcc1
        A["quant_tcp"] = tcp
        A["quant_abs"] = abs_toks
        return "SPLIT"

    def quant_fin():
        tcc1 = A["quant_cc1"]
        tcp = A["quant_tcp"]
        abs_toks = A["quant_abs"]
        tg_ = E(Y, lambda e: e.dma_start(out=B["CC1G"][0:NCORE, 0:4], in_=cc1_out),
                deps=[tcc1], inc="LDG", inc_n=16)
        tmm2 = E(T, lambda e: e.matmul(psf[:, 3072:3076], B["onesf"][:],
                                       B["CC1G"][:], start=True, stop=True),
                 deps=[tg_, tcp], inc="TE")
        tmg = E(V, lambda e: e.tensor_scalar(
            out=B["MGV"][:], in0=psf[:, 3072:3076], scalar1=1.0 / WQN,
            scalar2=1e-5, op0=ALU.mult, op1=ALU.max), deps=[tmm2], inc="VE")
        sw_ready = E(V, lambda e: e.reciprocal(out=B["SWV"][:], in_=B["MGV"][:]),
                     deps=[tmg], inc="VE")
        bank["f6"] = sw_ready

        spw_toks = []
        t1_toks = []
        q_prev = (None, None)
        for i in range(12):
            m = i // 4
            wst = B[f"WST{i % 2}"]
            wqo = B[f"WQO{i % 2}"]
            deps = [abs_toks[-1], sw_ready] + ([t1_toks[i - 2]] if i >= 2 else [])
            tl = E(Y, (lambda i=i, wst=wst: lambda e: e.dma_start(
                out=wst[:, 0:1024], in_=wsh_h[:, i * 1024:(i + 1) * 1024]))(),
                deps=deps, inc=f"LDW{i % 2}", inc_n=16)
            t1 = E(V, (lambda m=m, wst=wst: lambda e: e.tensor_scalar(
                out=B["JUNK"][:], in0=wst[:, 0:1024], scalar1=B["SWV"][:, m:m + 1],
                scalar2=MAGIC, op0=ALU.mult, op1=ALU.add))(),
                deps=[tl] + ([q_prev[0]] if i >= 1 else []), inc="VE")
            t1_toks.append(t1)
            t2 = E(V, (lambda: lambda e: e.tensor_scalar(
                out=B["JK2"][:], in0=B["JUNK"][:], scalar1=MAGIC + 1.0,
                scalar2=MAGIC - 1.0, op0=ALU.min, op1=ALU.max))(),
                deps=[t1] + ([q_prev[1]] if i >= 1 else []), inc="VE")
            t3 = E(V, (lambda wqo=wqo: lambda e: e.tensor_scalar(
                out=wqo[:], in0=B["JK2"][:], scalar1=MAGIC, scalar2=None,
                op0=ALU.subtract))(),
                deps=[t2] + ([spw_toks[i - 2]] if i >= 2 else []), inc="VE")
            q_prev = (t1, t2)
            ts_ = E(G, (lambda i=i, wqo=wqo: lambda e: e.dma_start(
                out=cc2_in[:, i * 1024:(i + 1) * 1024], in_=wqo[:]))(),
                deps=[t3], inc=f"SPW{i % 2}", inc_n=16)
            spw_toks.append(ts_)
        cc2_done = E(G, lambda e: e.collective_compute(
            "AllGather", ALU.bypass, replica_groups=[list(range(NCORE))],
            ins=[cc2_in], outs=[cc2_out]), deps=spw_toks, inc="CC")
        if phase == 2:
            finish_probe([cc2_done, sw_ready])
            return None
        return sw_ready, cc2_done


    free = {"qT": None, "numTb": None, "kT": None, "kTok": None, "vtok": None,
            "ATm": None, "xnT": None}
    xt_free = [None, None]
    xh_free = [None, None]
    den_prev = None
    A = {"vts": None}

    def emit_LN(c):
        xn_toks = []
        for tt in range(NTT):
            g = c * NTT + tt
            par = g % 2
            w = 48 * par
            XT = B[f"XT{par}"]
            XH = B[f"XH{par}"]
            tl = E(Y, (lambda c=c, tt=tt, XT=XT: lambda e: e.dma_start(
                out=XT[:], in_=x_h[c * CH + tt * 128: c * CH + (tt + 1) * 128, :]))(),
                deps=[xt_free[par]], inc=f"LDX{par}", inc_n=16)
            E(V, (lambda XT=XT, w=w: lambda e: e.bn_stats(
                B["SC"][:, w:w + 6], XT[:, 0:512]))(), deps=[tl])
            E(V, (lambda XT=XT, w=w: lambda e: e.bn_stats(
                B["SC"][:, w + 6:w + 12], XT[:, 512:1024]))())
            E(V, (lambda w=w: lambda e: e.bn_aggr(
                B["SC"][:, w + 12:w + 14], B["SC"][:, w:w + 12]))())
            E(V, (lambda w=w: lambda e: e.tensor_scalar(
                out=B["SC"][:, w + 14:w + 15], in0=B["SC"][:, w + 13:w + 14],
                scalar1=LN_EPS, scalar2=None, op0=ALU.add))())
            tv = E(V, (lambda w=w: lambda e: e.reciprocal(
                out=B["SC"][:, w + 15:w + 16], in_=B["SC"][:, w + 14:w + 15]))(), inc="VE")
            tsq = E(S, (lambda w=w: lambda e: e.activation(
                out=B["SC"][:, w + 16:w + 17], in_=B["SC"][:, w + 15:w + 16],
                func=AF.Sqrt))(), deps=[tv], inc="SE")
            tneg = E(V, (lambda w=w: lambda e: e.tensor_scalar(
                out=B["SC"][:, w + 17:w + 18], in0=B["SC"][:, w + 12:w + 13],
                scalar1=B["SC"][:, w + 16:w + 17], scalar2=-1.0,
                op0=ALU.mult, op1=ALU.mult))(), deps=[tsq], inc="VE")
            txh = E(S, (lambda XT=XT, XH=XH, w=w: lambda e: e.activation(
                out=XH[:], in_=XT[:], func=AF.Identity,
                scale=B["SC"][:, w + 16:w + 17], bias=B["SC"][:, w + 17:w + 18]))(),
                deps=[tneg, xh_free[par]], inc="SE")
            xt_free[par] = txh
            bname = "f4" if par == 0 else "f2"
            pcol = 2048 if par == 0 else 1024
            ttr = None
            for k in range(DT):
                ttr = E(T, (lambda XH=XH, k=k, pcol=pcol: lambda e: e.transpose(
                    psf[:, pcol + k * 128: pcol + (k + 1) * 128],
                    XH[:, k * 128:(k + 1) * 128], B["identf"][:]))(),
                    deps=([txh, t_consts, bank[bname],
                           bank["f5" if par == 0 else "f3"]] if k == 0 else []),
                    inc="TE" if k == DT - 1 else None)
            tev = None
            XN = B[f"xnT{c % 2}"]
            for k in range(DT):
                tev = E(V, (lambda k=k, tt=tt, pcol=pcol, XN=XN: lambda e: e.tensor_scalar(
                    out=XN[:, k * CH + tt * 128: k * CH + (tt + 1) * 128],
                    in0=psf[:, pcol + k * 128: pcol + (k + 1) * 128],
                    scalar1=B["biasf"][:, 24 + k:25 + k],
                    scalar2=B["biasf"][:, 32 + k:33 + k],
                    op0=ALU.mult, op1=ALU.add))(),
                    deps=([ttr, free["xnT"]] if k == 0 else []),
                    inc="VE" if k == DT - 1 else None)
            xh_free[par] = tev
            bank[bname] = tev
            bank["f5" if par == 0 else "f3"] = tev
            xn_toks.append(tev)
        return xn_toks[-1]

    def emit_attn(c, xn_done):
        XN = B[f"xnT{c % 2}"]
        if phase == 40 and c == 0:
            finish_probe([xn_done])
            return None

        def feat_mm(bk_id, wbuf, wbase, fcol, deps0):
            tk = None
            for k in range(DT):
                tk = E(T, (lambda k=k, bk_id=bk_id, wbuf=wbuf, wbase=wbase, fcol=fcol:
                           lambda e: e.matmul(
                               bankf(bk_id),
                               B[wbuf][:, k * wbase + fcol * 128: k * wbase + fcol * 128 + 128],
                               XN[:, k * CH:(k + 1) * CH],
                               start=(k == 0), stop=(k == DT - 1)))(),
                    deps=(deps0 if k == 0 else []), inc="TE" if k == DT - 1 else None)
            return tk

        kT_toks = []
        tG_toks = []
        tRk_toks = []
        for f in range(DT):
            bg_ = (2 * f) % 4
            bk_ = (2 * f + 1) % 4
            par = f % 2
            tgm = feat_mm(bg_, "wgate_bf", D, f,
                          [xn_done, wcast_done, bank[f"f{bg_}"]])
            tkm = feat_mm(bk_, "wqkv_bf", F3, DT + f, [bank[f"f{bk_}"]])
            tGc = E(V, (lambda bg_=bg_, par=par, f=f: lambda e: e.tensor_scalar(
                out=B[f"MB{par}"][:], in0=bankf(bg_),
                scalar1=B["biasf"][:, 16 + f:17 + f], scalar2=None, op0=ALU.add))(),
                deps=[tgm, tG_toks[f - 2] if f >= 2 else None,
                      kT_toks[f - 2] if f >= 2 else None], inc="VE")
            tG0 = E(S, (lambda par=par: lambda e: e.activation(
                out=B[f"GB{par}"][:], in_=B[f"MB{par}"][:], func=AF.Tanh,
                scale=0.5))(), deps=[tGc], inc="SE")
            tG = E(V, (lambda par=par: lambda e: e.tensor_scalar(
                out=B[f"GB{par}"][:], in0=B[f"GB{par}"][:], scalar1=0.5,
                scalar2=0.5, op0=ALU.mult, op1=ALU.add))(),
                deps=[tG0], inc="VE")
            tG_toks.append(tG0)
            bank[f"f{bg_}"] = tGc
            tK1 = E(V, (lambda bk_=bk_, par=par, f=f: lambda e: e.scalar_tensor_tensor(
                out=B[f"K1B{par}"][:], in0=bankf(bk_), scalar=B["biasf"][:, 8 + f:9 + f],
                in1=B[f"GB{par}"][:], op0=ALU.add, op1=ALU.mult))(),
                deps=[tkm, tG, tRk_toks[f - 2] if f >= 2 else None,
                      kT_toks[f - 2] if f >= 2 else None], inc="VE")
            bank[f"f{bk_}"] = tK1
            tR = E(S, (lambda par=par: lambda e: e.activation(
                out=B[f"RB{par}"][:], in_=B[f"K1B{par}"][:], func=AF.Relu))(),
                deps=[tK1, kT_toks[f - 2] if f >= 2 else None], inc="SE")
            tRk_toks.append(tR)
            tM = E(G, (lambda par=par: lambda e: e.tensor_tensor(
                out=B[f"MB{par}"][:], in0=B[f"K1B{par}"][:], in1=B[f"RB{par}"][:],
                op=ALU.subtract))(), deps=[tR, tG, A.get("g_prev")], inc="GE")
            tE_ = E(S, (lambda par=par: lambda e: e.activation(
                out=B[f"GB{par}"][:], in_=B[f"MB{par}"][:], func=AF.Exp))(),
                deps=[tM, tK1], inc="SE")
            tKT = E(G, (lambda par=par, f=f: lambda e: e.tensor_tensor(
                out=B["kT"][:, f * CH:(f + 1) * CH], in0=B[f"RB{par}"][:],
                in1=B[f"GB{par}"][:], op=ALU.add))(),
                deps=[tE_, free["kT"] if f == 0 else None], inc="GE")
            A["g_prev"] = tKT
            kT_toks.append(tKT)
        kT_done = kT_toks[-1]
        if c + 1 < NCH:
            A["xn_next"] = emit_LN(c + 1)
        if phase == 41 and c == 0:
            finish_probe([kT_done])
            return None

        qT_toks = []
        tRq_toks = []
        for f in range(DT):
            bq_ = (2 * f) % 4
            par = f % 2
            tqm = feat_mm(bq_, "wqkv_bf", F3, f, [bank[f"f{bq_}"]])
            tQc = E(V, (lambda bq_=bq_, par=par, f=f: lambda e: e.tensor_scalar(
                out=B[f"K1B{par}"][:], in0=bankf(bq_),
                scalar1=B["biasf"][:, f:f + 1], scalar2=None, op0=ALU.add))(),
                deps=[tqm, tRq_toks[f - 2] if f >= 2 else tRk_toks[DT - 2 + f],
                      qT_toks[f - 2] if f >= 2 else kT_toks[DT - 2 + f]],
                inc="VE")
            bank[f"f{bq_}"] = tQc
            tR = E(S, (lambda par=par: lambda e: e.activation(
                out=B[f"RB{par}"][:], in_=B[f"K1B{par}"][:], func=AF.Relu))(),
                deps=[tQc, qT_toks[f - 2] if f >= 2 else kT_toks[DT - 2 + f]],
                inc="SE")
            tRq_toks.append(tR)
            tM = E(G, (lambda par=par: lambda e: e.tensor_tensor(
                out=B[f"MB{par}"][:], in0=B[f"K1B{par}"][:], in1=B[f"RB{par}"][:],
                op=ALU.subtract))(), deps=[tR, A.get("g_prev")], inc="GE")
            tE_ = E(S, (lambda par=par: lambda e: e.activation(
                out=B[f"GB{par}"][:], in_=B[f"MB{par}"][:], func=AF.Exp))(),
                deps=[tM], inc="SE")
            tQT = E(G, (lambda par=par, f=f: lambda e: e.tensor_tensor(
                out=B["qT"][:, f * CH:(f + 1) * CH], in0=B[f"RB{par}"][:],
                in1=B[f"GB{par}"][:], op=ALU.add))(),
                deps=[tE_, free["qT"] if f == 0 else None], inc="GE")
            A["g_prev"] = tQT
            qT_toks.append(tQT)
        qT_done = qT_toks[-1]
        if phase == 42 and c == 0:
            finish_probe([qT_done])
            return None

        v_toks = []
        for tt in range(NTT):
            b0 = "f2" if tt % 2 == 0 else "f4"
            b1 = "f3" if tt % 2 == 0 else "f5"
            vb = 1024 if tt % 2 == 0 else 2048
            tvm = None
            for h in range(2):
                for k in range(DT):
                    tvm = E(T, (lambda tt=tt, h=h, k=k, vb=vb, XN=XN: lambda e: e.matmul(
                        psf[:, vb + h * 512: vb + (h + 1) * 512],
                        XN[:, k * CH + tt * 128: k * CH + (tt + 1) * 128],
                        B["wqkv_bf"][:, k * F3 + 2048 + h * 512: k * F3 + 2048 + (h + 1) * 512],
                        start=(k == 0), stop=(k == DT - 1)))(),
                        deps=([xn_done, bank[b0], bank[b1],
                               free["vtok"] if tt == 0 else None]
                              if (h == 0 and k == 0) else []),
                        inc="TE" if (h == 1 and k == DT - 1) else None)
            tve = E(V, (lambda tt=tt, vb=vb: lambda e: e.tensor_tensor(
                out=B["vtok"][:, tt * 1032: tt * 1032 + 1024], in0=psf[:, vb:vb + 1024],
                in1=B["BVBC"][:], op=ALU.add))(),
                deps=[tvm, t_consts], inc="VE")
            bank[b0] = tve
            bank[b1] = tve
            v_toks.append(tve)
        free["xnT"] = v_toks[-1]
        A["ln_ready"] = True

        if phase == 43 and c == 0:
            finish_probe([v_toks[-1]])
            return None

        at_toks = []
        for si in range(NTT):
            bk_id = si % 2
            n_si = CH - si * 128
            tat = None
            for k in range(DT):
                tat = E(T, (lambda si=si, k=k, bk_id=bk_id, n_si=n_si: lambda e: e.matmul(
                    bankf(bk_id, n_si),
                    B["kT"][:, k * CH + si * 128: k * CH + si * 128 + 128],
                    B["qT"][:, k * CH + si * 128:(k + 1) * CH],
                    start=(k == 0), stop=(k == DT - 1)))(),
                    deps=([kT_done, qT_done, bank[f"f{bk_id}"],
                           free["ATm"] if si == 0 else None] if k == 0 else []),
                    inc="TE" if k == DT - 1 else None)
            tdg = E(V, (lambda si=si, bk_id=bk_id: lambda e: e.tensor_tensor(
                out=B["ATm"][:, si * CH + si * 128: si * CH + si * 128 + 128],
                in0=bankf(bk_id, 128), in1=B["trilb"][:], op=ALU.mult))(),
                deps=[tat], inc="VE")
            tcp2 = tdg
            if si < NTT - 1:
                tcp2 = E(V, (lambda si=si, bk_id=bk_id, n_si=n_si: lambda e: e.tensor_copy(
                    out=B["ATm"][:, si * CH + si * 128 + 128: si * CH + CH],
                    in_=bankf(bk_id)[:, 128:n_si]))(), deps=[tat], inc="VE")
            bank[f"f{bk_id}"] = [tdg, tcp2]
            at_toks.append((tdg, tcp2))
        at_done_v = at_toks[-1][0]
        at_done_s = at_toks[-2][1]

        if phase == 44 and c == 0:
            finish_probe([at_toks[-1][0], at_toks[-2][1]])
            return None

        dgr = []
        for tt in range(NTT):
            for si in range(tt + 1):
                dgr.append((lambda si=si, tt=tt: (
                    lambda st, sp: lambda e: e.matmul(
                        psf[:, 3072 + tt: 3072 + tt + 1],
                        B["ATm"][:, si * CH + tt * 128: si * CH + (tt + 1) * 128],
                        B["onesb"][:, 0:1], start=st, stop=sp)))())
        if c > 0:
            for tt in range(NTT):
                for k in range(DT):
                    dgr.append((lambda tt=tt, k=k: (
                        lambda st, sp: lambda e: e.matmul(
                            psf[:, 3072 + tt: 3072 + tt + 1],
                            B["qT"][:, k * CH + tt * 128: k * CH + (tt + 1) * 128],
                            B["Sb"][:, k * 1032 + 1024: k * 1032 + 1025],
                            start=st, stop=sp)))())
        tden = None
        for i, mk in enumerate(dgr):
            tden = E(T, mk(i == 0, i == len(dgr) - 1),
                     deps=([at_done_v, at_done_s, qT_done, bank["f6"]] if i == 0 else []),
                     inc="TE" if i == len(dgr) - 1 else None)
        tdev = E(V, (lambda c=c: lambda e: e.tensor_copy(
            out=B["den_all"][:, c * NTT:(c + 1) * NTT], in_=psf[:, 3072:3072 + NTT]))(),
            deps=[tden], inc="VE")
        bank["f6"] = tdev

        if phase == 45 and c == 0:
            finish_probe([tdev])
            return None

        num_toks = []
        for j in range(DT):
            bk_id = j % 2
            grp = []
            if c > 0:
                for k in range(DT):
                    grp.append((lambda j=j, k=k, bk_id=bk_id: lambda e: e.matmul(
                        bankf(bk_id),
                        B["Sb"][:, k * 1032 + j * 128: k * 1032 + j * 128 + 128],
                        B["qT"][:, k * CH:(k + 1) * CH],
                        start=(k == 0), stop=False))())
            for si in range(NTT):
                st_ = (c == 0 and si == 0)
                grp.append((lambda j=j, si=si, st_=st_, bk_id=bk_id: lambda e: e.matmul(
                    psf[:, 512 * bk_id + si * 128: 512 * bk_id + 512],
                    B["vtok"][:, si * 1032 + j * 128: si * 1032 + j * 128 + 128],
                    B["ATm"][:, si * CH + si * 128: si * CH + CH],
                    start=st_, stop=(si == NTT - 1)))())
            tn = None
            for i, fn in enumerate(grp):
                tn = E(T, fn,
                       deps=([bank[f"f{bk_id}"], at_done_v, at_done_s,
                              v_toks[-1], qT_done, A["vts"] if j == 0 else None]
                             if i == 0 else []),
                       inc="TE" if i == len(grp) - 1 else None)
            tne = E(V, (lambda j=j, bk_id=bk_id: lambda e: e.tensor_copy(
                out=B["numTb"][:, j * CH:(j + 1) * CH], in_=bankf(bk_id)))(),
                deps=[tn, free["numTb"] if j == 0 else None], inc="VE")
            bank[f"f{bk_id}"] = tne
            num_toks.append(tne)

        if phase == 46 and c == 0:
            finish_probe([num_toks[-1]])
            return None

        kt_toks = []
        for tt in range(NTT):
            ttr = None
            for k in range(DT):
                ttr = E(T, (lambda tt=tt, k=k: lambda e: e.transpose(
                    psb[:, k * 128:(k + 1) * 128],
                    B["kT"][:, k * CH + tt * 128: k * CH + (tt + 1) * 128],
                    B["identb"][:]))(),
                    deps=([kT_done, bank["b7"], free["kTok"] if tt == 0 else None]
                          if k == 0 else []),
                    inc="TE" if k == DT - 1 else None)
            tev = E(V, (lambda tt=tt: lambda e: e.tensor_copy(
                out=B["kTok"][:, tt * D:(tt + 1) * D], in_=psb[:]))(),
                deps=[ttr], inc="VE")
            bank["b7"] = tev
            kt_toks.append(tev)

        if phase == 47 and c == 0:
            finish_probe([kt_toks[-1]])
            return None

        ds_toks = []
        for i in range(DT):
            b0 = "f2" if i % 2 == 0 else "f4"
            b1 = "f3" if i % 2 == 0 else "f5"
            vb = 1024 if i % 2 == 0 else 2048
            tds = None
            for h in range(2):
                for tt in range(NTT):
                    tds = E(T, (lambda i=i, h=h, tt=tt, vb=vb: lambda e: e.matmul(
                        psf[:, vb + h * 512: vb + (h + 1) * 512],
                        B["kTok"][:, tt * D + i * 128: tt * D + i * 128 + 128],
                        B["vtok"][:, tt * 1032 + h * 512: tt * 1032 + (h + 1) * 512],
                        start=(tt == 0), stop=(tt == NTT - 1)))(),
                        deps=([kt_toks[-1], v_toks[-1], bank[b0], bank[b1],
                               num_toks[-1] if i == 0 else None]
                              if (h == 0 and tt == 0) else []),
                        inc="TE" if (h == 1 and tt == NTT - 1) else None)
            tdk = None
            for tt in range(NTT):
                tdk = E(T, (lambda i=i, tt=tt: lambda e: e.matmul(
                    psf[:, 3072 + 8 + i: 3072 + 8 + i + 1],
                    B["kTok"][:, tt * D + i * 128: tt * D + i * 128 + 128],
                    B["vtok"][:, tt * 1032 + 1024: tt * 1032 + 1025],
                    start=(tt == 0), stop=(tt == NTT - 1)))(),
                    deps=([bank["f6"]] if (i == 0 and tt == 0) else []),
                    inc="TE" if tt == NTT - 1 else None)
            if c == 0:
                tsa = E(V, (lambda i=i, vb=vb: lambda e: e.tensor_copy(
                    out=B["Sb"][:, i * 1032: i * 1032 + 1024], in_=psf[:, vb:vb + 1024]))(),
                    deps=[tds], inc="VE")
                tsk = E(V, (lambda i=i: lambda e: e.tensor_copy(
                    out=B["Sb"][:, i * 1032 + 1024: i * 1032 + 1025],
                    in_=psf[:, 3072 + 8 + i: 3072 + 8 + i + 1]))(),
                    deps=[tdk], inc="VE")
            else:
                tsa = E(V, (lambda i=i, vb=vb: lambda e: e.tensor_tensor(
                    out=B["Sb"][:, i * 1032: i * 1032 + 1024], in0=psf[:, vb:vb + 1024],
                    in1=B["Sb"][:, i * 1032: i * 1032 + 1024], op=ALU.add))(),
                    deps=[tds], inc="VE")
                tsk = E(V, (lambda i=i: lambda e: e.tensor_tensor(
                    out=B["Sb"][:, i * 1032 + 1024: i * 1032 + 1025],
                    in0=psf[:, 3072 + 8 + i: 3072 + 8 + i + 1],
                    in1=B["Sb"][:, i * 1032 + 1024: i * 1032 + 1025], op=ALU.add))(),
                    deps=[tdk], inc="VE")
            bank[b0] = tsa
            bank[b1] = tsa
            ds_toks.append(tsk)
        A["vts"] = ds_toks[-1]
        A["ds_toks"] = ds_toks
        bank["f6"] = ds_toks[-1]
        tkb = ds_toks[-1]

        tq_sp = E(G, (lambda c=c: lambda e: e.dma_start(
            out=sc_qT[c * 128:(c + 1) * 128, :], in_=B["qT"][:]))(),
            deps=[qT_done, tden, num_toks[-1]], inc="SPQ", inc_n=16)
        tn_sp = E(G, (lambda c=c: lambda e: e.dma_start(
            out=sc_num[c * 128:(c + 1) * 128, :], in_=B["numTb"][:]))(),
            deps=[num_toks[-1]], inc="SPN", inc_n=16)

        free["qT"] = tq_sp
        free["numTb"] = tn_sp
        free["kT"] = kt_toks[-1]
        free["kTok"] = ds_toks[-1]
        free["vtok"] = ds_toks[-1]
        free["ATm"] = num_toks[-1]
        if phase == 30 + c:
            finish_probe([ds_toks[-1], tkb, tdev, free["qT"], free["numTb"]])
            return None
        return (ds_toks[-1], tkb, tdev)

    xn_cur = emit_LN(0)
    ret = None
    sw_ready = cc2_done = None
    for c in range(NCH):
        ret = emit_attn(c, xn_cur)
        if ret is None:
            esA.close()
            es.close()
            return nc
        if c == 0:
            q = emit_quant()
            if q is None:
                esA.close()
                es.close()
                return nc
        if c == 2:
            q = quant_fin()
            if q is None:
                esA.close()
                es.close()
                return nc
            sw_ready, cc2_done = q
        xn_cur = A.get("xn_next")
    S_done, ks_done, den_prev = ret

    if phase == 3:
        finish_probe([S_done, ks_done, den_prev, free["qT"], free["numTb"]])
        esA.close()
        es.close()
        return nc

    ds8 = A["ds_toks"]
    sps = []
    for i in range(DT):
        sps.append(E(G, (lambda i=i: lambda e: e.dma_start(
            out=cc3_in[i * 128:(i + 1) * 128, :],
            in_=B["Sb"][:, i * 1032:(i + 1) * 1032]))(),
            deps=[ds8[i]], inc="SPSA" if i < 4 else "SPS", inc_n=16))
    cc3a_done = E(G, lambda e: e.collective_compute(
        "AllGather", ALU.bypass, replica_groups=[list(range(NCORE))],
        ins=[cc3_in[0:512, :]], outs=[cc3a_out]), deps=[sps[3]], inc="CC")
    cc3b_done = E(G, lambda e: e.collective_compute(
        "AllGather", ALU.bypass, replica_groups=[list(range(NCORE))],
        ins=[cc3_in[512:1024, :]], outs=[cc3b_out]), deps=[sps[7]], inc="CC")
    cc3_done = cc3b_done

    a_end = [p.tok("TE"), p.tok("SE"), p.tok("VE"),
             p.tok("SPQ"), p.tok("SPN"), p.tok("SPS"), p.tok("SPSA")]
    if phase == 4:
        finish_probe([cc3_done] + a_end)
        esA.close()
        es.close()
        return nc
    esA.close()

    esB = ExitStack()
    for nm, sh, dt in [
        ("wqg", [128, DT * INNER], FP8), ("wqu", [128, DT * INNER], FP8),
        ("wqd", [128, 32 * D], FP8),
        ("Sgp", [128, DT * D], BF16), ("ksgp", [128, DT], BF16),
        ("wproj_bf", [128, DT * D], BF16),
    ]:
        sb(esB, nm, sh, dt)

    wq_toks = []
    for cbl in range(NCORE):
        dep = ([cc2_done] + a_end) if cbl == 0 else []
        wq_toks.append(E(S, (lambda cbl=cbl: lambda e: e.dma_start(
            out=B["wqg"][:, cbl * INNER:(cbl + 1) * INNER],
            in_=cc2_out[cbl * 128:(cbl + 1) * 128, 0:INNER]))(), deps=dep,
            inc="LDBW", inc_n=16))
        wq_toks.append(E(S, (lambda cbl=cbl: lambda e: e.dma_start(
            out=B["wqu"][:, cbl * INNER:(cbl + 1) * INNER],
            in_=cc2_out[cbl * 128:(cbl + 1) * 128, INNER:2 * INNER]))(),
            inc="LDBW", inc_n=16))
        wq_toks.append(E(S, (lambda cbl=cbl: lambda e: e.dma_start(
            out=B["wqd"][:, cbl * 4 * D:(cbl + 1) * 4 * D],
            in_=cc2_out[cbl * 128:(cbl + 1) * 128, 2 * INNER:2 * INNER + 4 * D]))(),
            inc="LDBW", inc_n=16))
    wq_done = wq_toks[-1]

    esX = ExitStack()
    sb(esX, "SG", [128, 2 * NCORE * 1032], BF16)
    sb(esX, "maskI", [128, NCORE * 128], BF16)
    t_mask = E(Y, lambda e: e.dma_start(out=B["maskI"][:], in_=maskI_h),
               deps=a_end + [sps[7]], inc="LDC", inc_n=16)
    for dmy in range(300):
        E(T, (lambda: lambda e: e.matmul(
            psf[:, 3072:3584], B["maskI"][:, 0:128], B["maskI"][:, 0:512],
            start=True, stop=True))(),
          deps=([t_mask, bank["f6"]] if dmy == 0 else []),
          inc="TE" if dmy == 299 else None)
    red_last = None
    red_evs = []
    for i in range(DT):
        half = cc3a_out if i < 4 else cc3b_out
        hbase = (i % 4) * 128
        sg0 = (i % 2) * NCORE * 1032
        dep = ([cc3a_done if i < 4 else cc3b_done]
               + (a_end if i == 0 else [])
               + ([red_evs[i - 2]] if i >= 2 else []))
        tl = None
        for cc in range(NCORE):
            tl = E(Y, (lambda cc=cc, half=half, hbase=hbase, sg0=sg0:
                       lambda e: e.dma_start(
                out=B["SG"][:, sg0 + cc * 1032: sg0 + (cc + 1) * 1032],
                in_=half[cc * 512 + hbase: cc * 512 + hbase + 128, :]))(),
                deps=(dep if cc == 0 else []),
                inc="LDS" if i % 2 == 0 else "LDS2", inc_n=16)
        evs = []
        for h in range(2):
            tm = None
            for cbl in range(NCORE):
                tm = E(T, (lambda cbl=cbl, h=h, sg0=sg0: lambda e: e.matmul(
                    bankf(h), B["maskI"][:, cbl * 128:(cbl + 1) * 128],
                    B["SG"][:, sg0 + cbl * 1032 + h * 512:
                            sg0 + cbl * 1032 + (h + 1) * 512],
                    start=(cbl == 0), stop=(cbl == NCORE - 1)))(),
                    deps=([tl, bank[f"f{h}"], t_mask] if cbl == 0 else []),
                    inc="TE" if cbl == NCORE - 1 else None)
            tev = E(V, (lambda i=i, h=h: lambda e: e.tensor_copy(
                out=B["Sgp"][:, i * D + h * 512: i * D + (h + 1) * 512],
                in_=bankf(h)))(), deps=[tm] + (a_end if i == 0 else []), inc="VE")
            bank[f"f{h}"] = tev
            evs.append(tev)
        tm = None
        for cbl in range(NCORE):
            tm = E(T, (lambda cbl=cbl, i=i, sg0=sg0: lambda e: e.matmul(
                psf[:, 1024 + i: 1024 + i + 1],
                B["maskI"][:, cbl * 128:(cbl + 1) * 128],
                B["SG"][:, sg0 + cbl * 1032 + 1024: sg0 + cbl * 1032 + 1025],
                start=(cbl == 0), stop=(cbl == NCORE - 1)))(),
                deps=([bank["f2"]] if (i == 0 and cbl == 0) else []),
                inc="TE" if cbl == NCORE - 1 else None)
        tev = E(V, (lambda i=i: lambda e: e.tensor_copy(
            out=B["ksgp"][:, i:i + 1], in_=psf[:, 1024 + i:1024 + i + 1]))(),
            deps=[tm], inc="VE")
        if i == DT - 1:
            bank["f2"] = tev
        red_last = tev
        red_evs.append(tev)
    sgp_done = red_last
    esX.close()

    for nm, sh, dt in [
        ("QTL", [128, DT * CH], BF16), ("NTL", [128, DT * CH], BF16),
        ("XT2a", [128, D], F32), ("XT2b", [128, D], F32),
        ("X2a", [128, D], F32), ("X2b", [128, D], F32),
        ("XQ", [128, D], BF16), ("XQT0", [128, D], BF16), ("XQT1", [128, D], BF16),
        ("SILU0", [128, 512], BF16), ("SILU1", [128, 512], BF16),
        ("PBUF", [128, INNER], F32),
        ("HQ", [128, INNER], BF16), ("HQT", [128, INNER], BF16),
    ]:
        sb(esB, nm, sh, dt)

    wp_toks = []
    for k in range(DT):
        stg2 = B["XT2a"] if k % 2 == 0 else B["XT2b"]
        dep = (a_end + [cc3_done, sgp_done]) if k < 2 else [wp_toks[k - 2]]
        tl = E(Y, (lambda k=k, stg2=stg2: lambda e: e.dma_start(
            out=stg2[:], in_=wproj_h[k * 128:(k + 1) * 128, :]))(),
            deps=dep, inc=f"LDW{k % 2}", inc_n=16)
        tc = E(S, (lambda k=k, stg2=stg2: lambda e: e.copy(
            out=B["wproj_bf"][:, k * D:(k + 1) * D], in_=stg2[:]))(),
            deps=[tl] + (a_end if k == 0 else []), inc="SE")
        wp_toks.append(tc)

    if phase == 5:
        finish_probe([sgp_done, wq_done, wp_toks[-1]])
        esB.close()
        es.close()
        return nc

    st = {
        "qT_load": None, "nT_load": None, "csum_free": None, "denB": None,
        "denB2": None, "denB3": None, "drow_free": None,
        "xt2_free": [wp_toks[-2], wp_toks[-1]], "hpre_free": [None, None],
        "x2_free": [None, None],
        "swin_free": [None, None],
        "xq_free": None, "xqt_free": [None, None], "hq_free": None,
        "qa_sfree": None,
        "slot_tok": [[None, None], [None, None]],
        "hqt_free": None,
        "outst_free": None, "psb_free": bank["b7"],
        "silu_free": [None, None],
    }
    st["qT_load"] = E(Y, lambda e: e.dma_start(out=B["QTL"][:], in_=sc_qT[0:128, :]),
                      deps=[a_end[3], cc3_done], inc="LDB0", inc_n=16)
    st["nT_load"] = E(Y, lambda e: e.dma_start(out=B["NTL"][:], in_=sc_num[0:128, :]),
                      deps=[a_end[4]], inc="LDB1", inc_n=16)

    for c in range(NCH):
        comb_toks = []
        inter_last = None
        for j in range(DT):
            bk_id = j % 2
            tm = None
            for k in range(DT):
                tm = E(T, (lambda j=j, k=k, bk_id=bk_id: lambda e: e.matmul(
                    bankf(bk_id), B["Sgp"][:, k * D + j * 128: k * D + j * 128 + 128],
                    B["QTL"][:, k * CH:(k + 1) * CH],
                    start=(k == 0), stop=(k == DT - 1)))(),
                    deps=([sgp_done, st["qT_load"], bank[f"f{bk_id}"]] if k == 0 else []),
                    inc="TE" if k == DT - 1 else None)
            inter_last = tm
            tcb = E(V, (lambda j=j, bk_id=bk_id: lambda e: e.tensor_tensor(
                out=B["NTL"][:, j * CH:(j + 1) * CH], in0=bankf(bk_id),
                in1=B["NTL"][:, j * CH:(j + 1) * CH], op=ALU.add))(),
                deps=[tm, st["nT_load"], st["csum_free"] if j == 0 else None],
                inc="VE")
            bank[f"f{bk_id}"] = tcb
            comb_toks.append(tcb)
        csum_done = comb_toks[-1]

        tdn = None
        pairs = [(tt, k) for tt in range(NTT) for k in range(DT)]
        for idx, (tt, k) in enumerate(pairs):
            tdn = E(T, (lambda tt=tt, k=k, idx=idx: lambda e: e.matmul(
                psf[:, 3072 + tt:3072 + tt + 1],
                B["QTL"][:, k * CH + tt * 128: k * CH + (tt + 1) * 128],
                B["ksgp"][:, k:k + 1], start=(idx == 0), stop=(idx == len(pairs) - 1)))(),
                deps=([sgp_done, bank["f6"], st["qT_load"]] if idx == 0 else []),
                inc="TE" if idx == len(pairs) - 1 else None)
        tdt = E(V, (lambda c=c: lambda e: e.scalar_tensor_tensor(
            out=B["DTOT"][:], in0=psf[:, 3072:3072 + NTT], scalar=DEN_EPS,
            in1=B["den_all"][:, c * NTT:(c + 1) * NTT], op0=ALU.add, op1=ALU.add))(),
            deps=[tdn, st["denB3"]], inc="VE")
        trd = E(V, lambda e: e.reciprocal(out=B["RDEN"][:], in_=B["DTOT"][:]),
                deps=[tdt], inc="VE")
        st["denB3"] = trd
        drow_done = trd
        bank["f6"] = tdt

        xls = {}
        for tt in range(2):
            par = tt % 2
            XT2 = B["XT2a"] if par == 0 else B["XT2b"]
            xls[tt] = E(Y, (lambda c=c, tt=tt, XT2=XT2: lambda e: e.dma_start(
                out=XT2[:], in_=x_h[c * CH + tt * 128: c * CH + (tt + 1) * 128, :]))(),
                deps=[st["xt2_free"][par]], inc=f"LDX{par}", inc_n=16)
        if c + 1 < NCH:
            st["qT_load"] = E(Y, (lambda c=c: lambda e: e.dma_start(
                out=B["QTL"][:], in_=sc_qT[(c + 1) * 128:(c + 2) * 128, :]))(),
                deps=[tdn, csum_done], inc="LDB0", inc_n=16)
            st["nT_load"] = None

        proj_last = {}
        tile = {}

        def projxq(tt):
            par = tt % 2
            w = 48 * par
            XT2 = B["XT2a"] if par == 0 else B["XT2b"]
            X2p = B["X2a"] if par == 0 else B["X2b"]
            XQT = B["XQT0"] if par == 0 else B["XQT1"]
            pb = ("f2", "f3") if tt % 2 == 0 else ("f4", "f5")
            vb = 1024 if tt % 2 == 0 else 2048
            tpm = None
            for h in range(2):
                for k in range(DT):
                    tpm = E(T, (lambda tt=tt, h=h, k=k, vb=vb: lambda e: e.matmul(
                        psf[:, vb + h * 512: vb + (h + 1) * 512],
                        B["NTL"][:, k * CH + tt * 128: k * CH + (tt + 1) * 128],
                        B["wproj_bf"][:, k * D + h * 512: k * D + (h + 1) * 512],
                        start=(k == 0), stop=(k == DT - 1)))(),
                        deps=([csum_done, drow_done, wp_toks[-1],
                               bank[pb[0]], bank[pb[1]]] if (h == 0 and k == 0) else []),
                        inc="TE" if (h == 1 and k == DT - 1) else None)
            proj_last[tt] = tpm
            thp = E(V, (lambda tt=tt, vb=vb, XT2=XT2: lambda e:
                        e.scalar_tensor_tensor(
                            out=XT2[:], in0=psf[:, vb:vb + 1024],
                            scalar=B["RDEN"][:, tt:tt + 1], in1=XT2[:],
                            op0=ALU.mult, op1=ALU.add))(),
                deps=[tpm, xls[tt], trd], inc="VE")
            thp = E(V, (lambda XT2=XT2: lambda e: e.tensor_tensor(
                out=XT2[:], in0=XT2[:], in1=B["BPBC"][:], op=ALU.add))(), inc="VE")
            tss = E(S, (lambda XT2=XT2, X2p=X2p, w=w: lambda e: e.activation(
                out=X2p[:], in_=XT2[:], func=AF.Identity,
                accum_out=B["SC"][:, w:w + 1]))(),
                deps=[thp, st["swin_free"][par], st["x2_free"][par]], inc="SE")
            tsq2 = E(S, (lambda XT2=XT2, X2p=X2p, w=w: lambda e: e.activation(
                out=X2p[:], in_=XT2[:], func=AF.Square,
                accum_out=B["SC"][:, w + 1:w + 2]))(), inc="SE")
            E(V, (lambda w=w: lambda e: e.tensor_scalar(
                out=B["SC"][:, w + 12:w + 13], in0=B["SC"][:, w:w + 1],
                scalar1=1.0 / 1024.0, scalar2=None, op0=ALU.mult))(),
              deps=[tss, tsq2])
            E(V, (lambda w=w: lambda e: e.tensor_scalar(
                out=B["SC"][:, w + 13:w + 14], in0=B["SC"][:, w + 1:w + 2],
                scalar1=1.0 / 1024.0, scalar2=LN_EPS, op0=ALU.mult, op1=ALU.add))())
            E(V, (lambda w=w: lambda e: e.tensor_scalar(
                out=B["SC"][:, w + 14:w + 15], in0=B["SC"][:, w + 12:w + 13],
                scalar1=B["SC"][:, w + 12:w + 13], scalar2=-1.0,
                op0=ALU.mult, op1=ALU.mult))())
            E(V, (lambda w=w: lambda e: e.tensor_tensor(
                out=B["SC"][:, w + 14:w + 15], in0=B["SC"][:, w + 13:w + 14],
                in1=B["SC"][:, w + 14:w + 15], op=ALU.add))())
            tv = E(V, (lambda w=w: lambda e: e.reciprocal(
                out=B["SC"][:, w + 15:w + 16], in_=B["SC"][:, w + 14:w + 15]))(), inc="VE")
            tsq = E(S, (lambda w=w: lambda e: e.activation(
                out=B["SC"][:, w + 16:w + 17], in_=B["SC"][:, w + 15:w + 16],
                func=AF.Sqrt))(), deps=[tv], inc="SE")
            tneg = E(V, (lambda w=w: lambda e: e.tensor_scalar(
                out=B["SC"][:, w + 17:w + 18], in0=B["SC"][:, w + 12:w + 13],
                scalar1=B["SC"][:, w + 16:w + 17], scalar2=-1.0,
                op0=ALU.mult, op1=ALU.mult))(), deps=[tsq], inc="VE")
            tx2 = E(S, (lambda XT2=XT2, X2p=X2p, w=w: lambda e: e.activation(
                out=X2p[:], in_=XT2[:], func=AF.Identity,
                scale=B["SC"][:, w + 16:w + 17], bias=B["SC"][:, w + 17:w + 18]))(),
                deps=[tneg, st["x2_free"][par]], inc="SE")
            E(V, (lambda X2p=X2p, w=w: lambda e: e.tensor_reduce(
                out=B["SC"][:, w + 18:w + 19], in_=X2p[:],
                axis=AX, op=ALU.max, apply_absolute_value=True))(), deps=[tx2])
            E(V, (lambda w=w: lambda e: e.tensor_scalar(
                out=B["SC"][:, w + 19:w + 20], in0=B["SC"][:, w + 18:w + 19],
                scalar1=1e-5, scalar2=None, op0=ALU.max))())
            E(V, (lambda w=w: lambda e: e.reciprocal(
                out=B["SC"][:, w + 20:w + 21], in_=B["SC"][:, w + 19:w + 20]))())
            ts1 = E(V, (lambda w=w: lambda e: e.tensor_scalar(
                out=B["SC"][:, w + 21:w + 22], in0=B["SC"][:, w + 20:w + 21],
                scalar1=127.0, scalar2=None, op0=ALU.mult))(), inc="VE")
            tc1g = E(V, (lambda w=w: lambda e: e.tensor_scalar(
                out=B["SC"][:, w + 22:w + 23], in0=B["SC"][:, w + 19:w + 20],
                scalar1=B["MGV"][:, 0:1], scalar2=1.0 / 127.0,
                op0=ALU.mult, op1=ALU.mult))(), deps=[sw_ready], inc="VE")
            tc1u = E(V, (lambda w=w: lambda e: e.tensor_scalar(
                out=B["SC"][:, w + 23:w + 24], in0=B["SC"][:, w + 19:w + 20],
                scalar1=B["MGV"][:, 1:2], scalar2=1.0 / 127.0,
                op0=ALU.mult, op1=ALU.mult))(), inc="VE")
            txq = None
            for hh in range(2):
                E(V, (lambda X2p=X2p, XT2=XT2, hh=hh, w=w: lambda e: e.tensor_scalar(
                    out=XT2[:, 0:512], in0=X2p[:, hh * 512:(hh + 1) * 512],
                    scalar1=B["SC"][:, w + 21:w + 22], scalar2=MAGIC,
                    op0=ALU.mult, op1=ALU.add))(),
                  deps=([ts1, st["qa_sfree"]] if hh == 0 else []))
                txq = E(V, (lambda XT2=XT2, hh=hh: lambda e: e.tensor_scalar(
                    out=B["XQ"][:, hh * 512:(hh + 1) * 512], in0=XT2[:, 0:512],
                    scalar1=MAGIC, scalar2=None, op0=ALU.subtract))(),
                    deps=[st["xq_free"] if hh == 0 else None], inc="VE")
            ttr = None
            for k in range(DT):
                ttr = E(T, (lambda k=k: lambda e: e.transpose(
                    psb[:, k * 128:(k + 1) * 128], B["XQ"][:, k * 128:(k + 1) * 128],
                    B["identb"][:]))(),
                    deps=([txq, st["psb_free"]] if k == 0 else []),
                    inc="TE" if k == DT - 1 else None)
            st["xq_free"] = ttr
            txqt = E(V, (lambda XQT=XQT: lambda e: e.tensor_copy(
                out=XQT[:], in_=psb[:]))(),
                deps=[ttr, st["xqt_free"][par]], inc="VE")
            st["psb_free"] = txqt
            tile[tt] = {"tx2": tx2, "txqt": txqt, "tc1g": tc1g, "tc1u": tc1u,
                        "txq": txq}

        def gup(tt):
            par = tt % 2
            w = 48 * par
            XQT = B["XQT0"] if par == 0 else B["XQT1"]
            XT2 = B["XT2a"] if par == 0 else B["XT2b"]
            txqt = tile[tt]["txqt"]
            tc1g = tile[tt]["tc1g"]
            tc1u = tile[tt]["tc1u"]
            gu_last = None
            pb_toks = []
            for ic in range(NIC):
                gb = f"f{ic % 2}"
                ub = f"f{2 + ic % 2}"
                tgm = None
                for k in range(DT):
                    tgm = E(T, (lambda ic=ic, k=k, gb=gb, XQT=XQT: lambda e: e.matmul(
                        bankf(int(gb[1])), XQT[:, k * 128:(k + 1) * 128],
                        B["wqg"][:, k * INNER + ic * 512: k * INNER + (ic + 1) * 512],
                        start=(k == 0), stop=(k == DT - 1)))(),
                        deps=([txqt, wq_done, bank[gb]] if k == 0 else []),
                        inc="TE" if k == DT - 1 else None)
                tum = None
                for k in range(DT):
                    tum = E(T, (lambda ic=ic, k=k, ub=ub, XQT=XQT: lambda e: e.matmul(
                        bankf(int(ub[1])), XQT[:, k * 128:(k + 1) * 128],
                        B["wqu"][:, k * INNER + ic * 512: k * INNER + (ic + 1) * 512],
                        start=(k == 0), stop=(k == DT - 1)))(),
                        deps=([bank[ub]] if k == 0 else []),
                        inc="TE" if k == DT - 1 else None)
                tslc = E(V, (lambda ic=ic, gb=gb, w=w, XT2=XT2: lambda e: e.tensor_scalar(
                    out=XT2[:, (ic % 2) * 512:(ic % 2) * 512 + 512],
                    in0=bankf(int(gb[1])),
                    scalar1=B["SC"][:, w + 22:w + 23], scalar2=None, op0=ALU.mult))(),
                    deps=[tgm, tc1g, st["slot_tok"][par][ic % 2]], inc="VE")
                tsl = E(S, (lambda ic=ic, XT2=XT2: lambda e: e.activation(
                    out=B[f"SILU{ic % 2}"][:],
                    in_=XT2[:, (ic % 2) * 512:(ic % 2) * 512 + 512],
                    func=AF.Silu))(),
                    deps=[tslc, st["silu_free"][ic % 2]], inc="SE")
                st["slot_tok"][par][ic % 2] = tsl
                bank[gb] = tslc
                tup = E(V, (lambda ic=ic, ub=ub, w=w: lambda e: e.tensor_scalar(
                    out=B["PBUF"][:, ic * 512:(ic + 1) * 512], in0=bankf(int(ub[1])),
                    scalar1=B["SC"][:, w + 23:w + 24], scalar2=None, op0=ALU.mult))(),
                    deps=[tum, tc1u], inc="VE")
                bank[ub] = tup
                tpp = E(V, (lambda ic=ic: lambda e: e.tensor_tensor(
                    out=B["PBUF"][:, ic * 512:(ic + 1) * 512],
                    in0=B["PBUF"][:, ic * 512:(ic + 1) * 512],
                    in1=B[f"SILU{ic % 2}"][:], op=ALU.mult))(),
                    deps=[tsl], inc="VE")
                st["silu_free"][ic % 2] = tpp
                tpm8 = E(V, (lambda ic=ic: lambda e: e.tensor_reduce(
                    out=B["PM8"][:, ic:ic + 1], in_=B["PBUF"][:, ic * 512:(ic + 1) * 512],
                    axis=AX, op=ALU.max, apply_absolute_value=True))(), inc="VE")
                pb_toks.append(tpm8)
                gu_last = tum
            tile[tt]["gu_last"] = gu_last
            tile[tt]["pb_done"] = pb_toks[-1]

        def tail_rounds(tt):
            par = tt % 2
            w = 48 * par
            E(V, (lambda w=w: lambda e: e.tensor_reduce(
                out=B["SC"][:, w + 24:w + 25], in_=B["PM8"][:], axis=AX, op=ALU.max))(),
              deps=[tile[tt]["pb_done"]])
            E(V, (lambda w=w: lambda e: e.tensor_scalar(
                out=B["SC"][:, w + 25:w + 26], in0=B["SC"][:, w + 24:w + 25],
                scalar1=1e-5, scalar2=None, op0=ALU.max))())
            E(V, (lambda w=w: lambda e: e.reciprocal(
                out=B["SC"][:, w + 26:w + 27], in_=B["SC"][:, w + 25:w + 26]))())
            ts2 = E(V, (lambda w=w: lambda e: e.tensor_scalar(
                out=B["SC"][:, w + 27:w + 28], in0=B["SC"][:, w + 26:w + 27],
                scalar1=127.0, scalar2=None, op0=ALU.mult))(), inc="VE")
            tc2 = E(V, (lambda w=w: lambda e: e.tensor_scalar(
                out=B["SC"][:, w + 28:w + 29], in0=B["SC"][:, w + 25:w + 26],
                scalar1=B["MGV"][:, 2:3], scalar2=1.0 / 127.0,
                op0=ALU.mult, op1=ALU.mult))(), inc="VE")
            XT2 = B["XT2a"] if par == 0 else B["XT2b"]
            rnd_last = None
            for ic in range(NIC):
                E(V, (lambda ic=ic, w=w, XT2=XT2: lambda e: e.tensor_scalar(
                    out=XT2[:, (ic % 2) * 512:(ic % 2) * 512 + 512],
                    in0=B["PBUF"][:, ic * 512:(ic + 1) * 512],
                    scalar1=B["SC"][:, w + 27:w + 28], scalar2=MAGIC,
                    op0=ALU.mult, op1=ALU.add))(),
                  deps=([ts2] if ic == 0 else []))
                rnd_last = E(V, (lambda ic=ic, XT2=XT2: lambda e: e.tensor_scalar(
                    out=B["HQ"][:, ic * 512:(ic + 1) * 512],
                    in0=XT2[:, (ic % 2) * 512:(ic % 2) * 512 + 512],
                    scalar1=MAGIC, scalar2=None, op0=ALU.subtract))(),
                    deps=[st["hq_free"] if ic == 0 else None], inc="VE")
            tile[tt]["rnd_last"] = rnd_last
            tile[tt]["tc2"] = tc2

        def tail_down(tt):
            par = tt % 2
            w = 48 * par
            XT2 = B["XT2a"] if par == 0 else B["XT2b"]
            X2p = B["X2a"] if par == 0 else B["X2b"]
            rnd_last = tile[tt]["rnd_last"]
            tc2 = tile[tt]["tc2"]
            tx2 = tile[tt]["tx2"]
            hqt_evs = []
            ttr2l = None
            for r in range(4):
                ttr2 = None
                for jj in range(8):
                    jblk = r * 8 + jj
                    ttr2 = E(T, (lambda jblk=jblk, jj=jj: lambda e: e.transpose(
                        psb[:, jj * 128:(jj + 1) * 128],
                        B["HQ"][:, jblk * 128:(jblk + 1) * 128],
                        B["identb"][:]))(),
                        deps=([rnd_last, st["psb_free"],
                               st["hqt_free"] if r == 0 else None] if jj == 0 else []),
                        inc="TE" if jj == 7 else None)
                ttr2l = ttr2
                tev = E(V, (lambda r=r: lambda e: e.tensor_copy(
                    out=B["HQT"][:, r * 1024:(r + 1) * 1024], in_=psb[:]))(),
                    deps=[ttr2], inc="VE")
                st["psb_free"] = tev
                hqt_evs.append(tev)
            st["hq_free"] = ttr2l
            db = ("f4", "f5") if tt % 2 == 0 else ("f2", "f3")
            dbc = 2048 if tt % 2 == 0 else 1024
            tdm = None
            for h in range(2):
                for j in range(32):
                    tdm = E(T, (lambda h=h, j=j, dbc=dbc: lambda e: e.matmul(
                        psf[:, dbc + h * 512: dbc + (h + 1) * 512],
                        B["HQT"][:, j * 128:(j + 1) * 128],
                        B["wqd"][:, j * D + h * 512: j * D + (h + 1) * 512],
                        start=(j == 0), stop=(j == 31)))(),
                        deps=([hqt_evs[-1], bank[db[0]], bank[db[1]]]
                              if (h == 0 and j == 0) else []),
                        inc="TE" if (h == 1 and j == 31) else None)
            st["hqt_free"] = tdm
            st["xqt_free"][par] = tile[tt]["gu_last"]
            th2 = E(V, (lambda X2p=X2p, XT2=XT2, dbc=dbc, w=w: lambda e:
                        e.scalar_tensor_tensor(
                out=XT2[:], in0=psf[:, dbc:dbc + 1024], scalar=B["SC"][:, w + 28:w + 29],
                in1=X2p[:], op0=ALU.mult, op1=ALU.add))(),
                deps=[tdm, tc2, tx2], inc="VE")
            bank[db[0]] = th2
            bank[db[1]] = th2
            tss3 = E(S, (lambda XT2=XT2, X2p=X2p, w=w: lambda e: e.activation(
                out=X2p[:], in_=XT2[:], func=AF.Identity,
                accum_out=B["SC"][:, w + 29:w + 30]))(), deps=[th2], inc="SE")
            tsq3p = E(S, (lambda XT2=XT2, X2p=X2p, w=w: lambda e: e.activation(
                out=X2p[:], in_=XT2[:], func=AF.Square,
                accum_out=B["SC"][:, w + 30:w + 31]))(), inc="SE")
            E(V, (lambda w=w: lambda e: e.tensor_scalar(
                out=B["SC"][:, w + 41:w + 42], in0=B["SC"][:, w + 29:w + 30],
                scalar1=1.0 / 1024.0, scalar2=None, op0=ALU.mult))(),
              deps=[tss3, tsq3p])
            E(V, (lambda w=w: lambda e: e.tensor_scalar(
                out=B["SC"][:, w + 42:w + 43], in0=B["SC"][:, w + 30:w + 31],
                scalar1=1.0 / 1024.0, scalar2=LN_EPS, op0=ALU.mult, op1=ALU.add))())
            E(V, (lambda w=w: lambda e: e.tensor_scalar(
                out=B["SC"][:, w + 43:w + 44], in0=B["SC"][:, w + 41:w + 42],
                scalar1=B["SC"][:, w + 41:w + 42], scalar2=-1.0,
                op0=ALU.mult, op1=ALU.mult))())
            E(V, (lambda w=w: lambda e: e.tensor_tensor(
                out=B["SC"][:, w + 43:w + 44], in0=B["SC"][:, w + 42:w + 43],
                in1=B["SC"][:, w + 43:w + 44], op=ALU.add))())
            tv3 = E(V, (lambda w=w: lambda e: e.reciprocal(
                out=B["SC"][:, w + 44:w + 45], in_=B["SC"][:, w + 43:w + 44]))(), inc="VE")
            tsq3 = E(S, (lambda w=w: lambda e: e.activation(
                out=B["SC"][:, w + 45:w + 46], in_=B["SC"][:, w + 44:w + 45],
                func=AF.Sqrt))(), deps=[tv3], inc="SE")
            tneg3 = E(V, (lambda w=w: lambda e: e.tensor_scalar(
                out=B["SC"][:, w + 46:w + 47], in0=B["SC"][:, w + 41:w + 42],
                scalar1=B["SC"][:, w + 45:w + 46], scalar2=-1.0,
                op0=ALU.mult, op1=ALU.mult))(), deps=[tsq3], inc="VE")
            tout = E(S, (lambda XT2=XT2, X2p=X2p, w=w: lambda e: e.activation(
                out=X2p[:], in_=XT2[:], func=AF.Identity,
                scale=B["SC"][:, w + 45:w + 46], bias=B["SC"][:, w + 46:w + 47]))(),
                deps=[tneg3, th2, st["outst_free"]], inc="SE")
            st["swin_free"][par] = tout
            st["xt2_free"][par] = tout
            tst = E(G, (lambda c=c, tt=tt, X2p=X2p: lambda e: e.dma_start(
                out=out_h[c * CH + tt * 128: c * CH + (tt + 1) * 128, :],
                in_=X2p[:]))(), deps=[tout], inc="SPO", inc_n=16)
            st["outst_free"] = tst
            st["x2_free"][par] = tst
            if tt + 2 < NTT:
                XT2n = B["XT2a"] if par == 0 else B["XT2b"]
                xls[tt + 2] = E(Y, (lambda c=c, tt=tt, XT2n=XT2n: lambda e: e.dma_start(
                    out=XT2n[:],
                    in_=x_h[c * CH + (tt + 2) * 128: c * CH + (tt + 3) * 128, :]))(),
                    deps=[tout], inc=f"LDX{par}", inc_n=16)

        projxq(0)
        projxq(1)
        gup(0)
        tail_rounds(0)
        tail_down(0)
        gup(1)
        tail_rounds(1)
        projxq(2)
        tail_down(1)
        gup(2)
        tail_rounds(2)
        projxq(3)
        tail_down(2)
        gup(3)
        tail_rounds(3)
        tail_down(3)
        st["csum_free"] = proj_last[NTT - 1]
        st["drow_free"] = proj_last[NTT - 1]
        if phase == 6 and c == 0:
            break
        if c + 1 < NCH:
            st["nT_load"] = E(Y, (lambda c=c: lambda e: e.dma_start(
                out=B["NTL"][:], in_=sc_num[(c + 1) * 128:(c + 2) * 128, :]))(),
                deps=[proj_last[NTT - 1]], inc="LDB1", inc_n=16)

    E(G, lambda e: e.nop(), deps=[p.tok("SPO")])
    replay_all()
    esB.close()
    es.close()
    return nc


_NC_CACHE = None


def kernel(L0_attn_x, L0_attn_w_qkv, L0_attn_b_qkv, L0_attn_w_gate, L0_attn_b_gate,
           L0_attn_w_proj, L0_attn_b_proj, L0_attn_ln_g, L0_attn_ln_b,
           L0_ffn_w_gate, L0_ffn_w_up, L0_ffn_w_down):
    global _NC_CACHE
    x = np.ascontiguousarray(np.asarray(L0_attn_x, dtype=np.float32).reshape(16384, 1024))
    wqkv = np.ascontiguousarray(np.asarray(L0_attn_w_qkv, dtype=np.float32))
    bqkv = np.asarray(L0_attn_b_qkv, dtype=np.float32)
    wgate = np.ascontiguousarray(np.asarray(L0_attn_w_gate, dtype=np.float32))
    bgate = np.asarray(L0_attn_b_gate, dtype=np.float32)
    wproj = np.ascontiguousarray(np.asarray(L0_attn_w_proj, dtype=np.float32))
    bproj = np.asarray(L0_attn_b_proj, dtype=np.float32)
    lng = np.asarray(L0_attn_ln_g, dtype=np.float32)
    lnb = np.asarray(L0_attn_ln_b, dtype=np.float32)
    wg = np.asarray(L0_ffn_w_gate, dtype=np.float32)
    wu = np.asarray(L0_ffn_w_up, dtype=np.float32)
    wd = np.asarray(L0_ffn_w_down, dtype=np.float32)

    biasf = np.zeros((128, 40), np.float32)
    biasf[:, 0:8] = bqkv[0:1024].reshape(8, 128).T
    biasf[:, 8:16] = bqkv[1024:2048].reshape(8, 128).T
    biasf[:, 16:24] = bgate.reshape(8, 128).T
    biasf[:, 24:32] = lng.reshape(8, 128).T
    biasf[:, 32:40] = lnb.reshape(8, 128).T
    bvbc = np.ascontiguousarray(np.tile(bqkv[2048:3072].reshape(1, 1024), (128, 1)))
    bpbc = np.ascontiguousarray(np.tile(bproj.reshape(1, 1024), (128, 1)))

    identf = np.eye(128, dtype=np.float32)
    identb = np.eye(128, dtype=ml_dtypes.bfloat16)
    sidx = np.arange(128)
    trilb = (sidx[:, None] <= sidx[None, :]).astype(ml_dtypes.bfloat16)
    onesb = np.ones((128, 128), dtype=ml_dtypes.bfloat16)
    onesf = np.ones((128, 128), dtype=np.float32)

    wgT = np.ascontiguousarray(wg.T)
    wuT = np.ascontiguousarray(wu.T)
    wdT = np.ascontiguousarray(wd.T)

    if _NC_CACHE is None:
        _NC_CACHE = build_nc()
    nc = _NC_CACHE
    in_maps = []
    for c in range(NCORE):
        maskI = np.concatenate(
            [(identb if cc < c else np.zeros_like(identb)) for cc in range(NCORE)],
            axis=1)
        wsh = np.concatenate([
            wgT[c * 128:(c + 1) * 128, :],
            wuT[c * 128:(c + 1) * 128, :],
            wdT[c * 512:(c + 1) * 512, :].reshape(4, 128, 1024)
               .transpose(1, 0, 2).reshape(128, 4096),
        ], axis=1)
        in_maps.append({
            "x": np.ascontiguousarray(x[c * TC:(c + 1) * TC]),
            "wqkv": wqkv, "wgate": wgate, "wproj": wproj,
            "biasf": biasf, "bvbc": bvbc, "bpbc": bpbc,
            "identf": identf, "identb": identb, "trilb": trilb,
            "onesb": onesb, "onesf": onesf,
            "maskI": np.ascontiguousarray(maskI),
            "wsh": np.ascontiguousarray(wsh),
        })
    res = run_bass_kernel_spmd(nc, in_maps, core_ids=list(range(NCORE)))
    out = np.concatenate([res.results[c]["out"] for c in range(NCORE)], axis=0)
    return out.reshape(1, 16384, 1024)


# version 20
# speedup vs baseline: 1.0328x; 1.0161x over previous
import numpy as np
import ml_dtypes
from contextlib import ExitStack

import concourse.bass as bass
import concourse.mybir as mybir
from concourse.bass_utils import run_bass_kernel_spmd

F32 = mybir.dt.float32
BF16 = mybir.dt.bfloat16
FP8 = mybir.dt.float8e4
ALU = mybir.AluOpType
AF = mybir.ActivationFunctionType
AX = mybir.AxisListType.X

MAGIC = float(1.5 * 2 ** 23)
NCORE = 8
TC = 2048
CH = 512
NCH = TC // CH
NTT = CH // 128
D = 1024
DT = D // 128
F3 = 3072
INNER = 4096
NIC = INNER // 512
LN_EPS = 1e-5
DEN_EPS = 1e-6
WQN = INNER * D


class Prog:
    ENGINES = ("t", "s", "v", "g", "y")

    def __init__(self):
        self.ops = {e: [] for e in self.ENGINES}
        self.cnt = {}
        self.waited = {e: {} for e in self.ENGINES}

    def tok(self, sem):
        return (sem, self.cnt.get(sem, 0))

    AUTO = {"s": "SE", "v": "VE", "g": "GE"}
    AUTO_CHAIN = True

    def emit(self, eng, fn, deps=(), inc=None, inc_n=1):
        deps = list(deps)
        auto = self.AUTO.get(eng) if self.AUTO_CHAIN else None
        if auto is not None:
            deps.append((auto, self.cnt.get(auto, 0)))
            if inc is None:
                inc, inc_n = auto, 1
        w = self.waited[eng]
        flat = []
        for d in deps:
            if d is None:
                continue
            if isinstance(d, list):
                flat.extend(x for x in d if x is not None)
            else:
                flat.append(d)
        best = {}
        for sem, val in flat:
            if val > best.get(sem, 0):
                best[sem] = val
        dd = []
        for sem, val in best.items():
            if val > 0 and w.get(sem, 0) < val:
                w[sem] = val
                dd.append((sem, val))
        if inc is not None:
            self.cnt[inc] = self.cnt.get(inc, 0) + inc_n
        self.ops[eng].append((dd, fn, (inc, inc_n) if inc is not None else None))
        return self.tok(inc) if inc is not None else None


PHASE_FULL = 99


def build_nc(phase=PHASE_FULL):
    nc = bass.Bass("TRN2", target_bir_lowering=False)

    x_h = nc.dram_tensor("x", [TC, D], F32, kind="ExternalInput").ap()
    wqkv_h = nc.dram_tensor("wqkv", [D, F3], F32, kind="ExternalInput").ap()
    wgate_h = nc.dram_tensor("wgate", [D, D], F32, kind="ExternalInput").ap()
    wproj_h = nc.dram_tensor("wproj", [D, D], F32, kind="ExternalInput").ap()
    biasf_h = nc.dram_tensor("biasf", [128, 40], F32, kind="ExternalInput").ap()
    bvbc_h = nc.dram_tensor("bvbc", [128, D], F32, kind="ExternalInput").ap()
    bpbc_h = nc.dram_tensor("bpbc", [128, D], F32, kind="ExternalInput").ap()
    identf_h = nc.dram_tensor("identf", [128, 128], F32, kind="ExternalInput").ap()
    identb_h = nc.dram_tensor("identb", [128, 128], BF16, kind="ExternalInput").ap()
    trilb_h = nc.dram_tensor("trilb", [128, 128], BF16, kind="ExternalInput").ap()
    onesb_h = nc.dram_tensor("onesb", [128, 128], BF16, kind="ExternalInput").ap()
    onesf_h = nc.dram_tensor("onesf", [128, 128], F32, kind="ExternalInput").ap()
    maskI_h = nc.dram_tensor("maskI", [128, NCORE * 128], BF16, kind="ExternalInput").ap()
    wsh_h = nc.dram_tensor("wsh", [128, 12288], F32, kind="ExternalInput").ap()
    out_h = nc.dram_tensor("out", [TC, D], F32, kind="ExternalOutput").ap()

    sc_qT = nc.dram_tensor("sc_qT", [NCH * 128, DT * CH], BF16).ap()
    sc_num = nc.dram_tensor("sc_num", [NCH * 128, DT * CH], BF16).ap()
    cc1_in = nc.dram_tensor("cc1_in", [1, 4], F32).ap()
    cc1_out = nc.dram_tensor("cc1_out", [NCORE, 4], F32).ap()
    cc2_in = nc.dram_tensor("cc2_in", [128, 12288], FP8).ap()
    cc2_out = nc.dram_tensor("cc2_out", [NCORE * 128, 12288], FP8).ap()
    cc3_in = nc.dram_tensor("cc3_in", [D, 1032], BF16).ap()
    cc3a_out = nc.dram_tensor("cc3a_out", [NCORE * 512, 1032], BF16).ap()
    cc3b_out = nc.dram_tensor("cc3b_out", [NCORE * 512, 1032], BF16).ap()

    p = Prog()
    es = ExitStack()
    sems = {}
    for sname in ("LDC", "LDG", "LDW0", "LDW1", "LDX0", "LDX1", "LDB0", "LDB1",
                  "LDBW", "LDS", "LDS2", "SPQ", "SPN", "SPW0", "SPW1", "SPC", "SPO",
                  "SPS", "SPSA", "TE", "SE", "VE", "GE", "CC"):
        sems[sname] = es.enter_context(nc.semaphore(sname))

    B = {}

    def sb(stack, name, shape, dt):
        B[name] = stack.enter_context(nc.sbuf_tensor("sb_" + name, shape, dt))
        return B[name]

    for nm, sh, dt in [
        ("identb", [128, 128], BF16), ("BPBC", [128, D], F32),
        ("den_all", [128, NCH * NTT], F32), ("SC", [128, 96], F32),
        ("MGV", [128, 4], F32), ("SWV", [128, 4], F32),
        ("DTOT", [128, NTT], F32), ("RDEN", [128, NTT], F32), ("PM8", [128, 8], F32),
    ]:
        sb(es, nm, sh, dt)

    psf = es.enter_context(nc.psum_tensor("psf", [128, 3584], F32))
    psb = es.enter_context(nc.psum_tensor("psb", [128, 1024], BF16))

    def bankf(i, w=512):
        return psf[:, 512 * i: 512 * i + w]

    T, S, V, G, Y = "t", "s", "v", "g", "y"

    def E(eng, fn, deps=(), inc=None, inc_n=1):
        return p.emit(eng, fn, deps, inc, inc_n)

    bank = {f"f{i}": None for i in range(7)}
    bank["b7"] = None

    def finish_probe(deps):
        tpr = E(G, lambda e: e.dma_start(out=out_h[0:128, 0:96], in_=B["SC"][:]),
                deps=deps, inc="SPO", inc_n=16)
        E(G, lambda e: e.nop(), deps=[p.tok("SPO")])
        replay_all()
        return True

    replay_done = []

    def replay_all():
        with nc.Block() as block:
            def mk_replay(eng_key):
                def _(eng):
                    for deps, fn, inc in p.ops[eng_key]:
                        for sem, val in deps:
                            eng.wait_ge(sems[sem], val)
                        instr = fn(eng)
                        if inc is not None:
                            instr.then_inc(sems[inc[0]], inc[1])
                return _
            block.tensor(mk_replay("t"))
            block.scalar(mk_replay("s"))
            block.vector(mk_replay("v"))
            block.gpsimd(mk_replay("g"))
            block.sync(mk_replay("y"))
        replay_done.append(True)

    E(Y, lambda e: e.dma_start(out=B["identb"][:], in_=identb_h), inc="LDC", inc_n=16)

    E(V, lambda e: e.memset(B["SC"][:], 0.0))
    E(Y, lambda e: e.dma_start(out=B["BPBC"][:], in_=bpbc_h), inc="LDC", inc_n=16)
    t_rowcast = E(S, lambda e: e.nop(), inc="SE")

    esA = ExitStack()
    for nm, sh, dt in [
        ("trilb", [128, 128], BF16), ("onesb", [128, 128], BF16),
        ("onesf", [128, 128], F32), ("BVBC", [128, D], F32),
        ("identf", [128, 128], F32), ("biasf", [128, 40], F32),
        ("CC1S", [128, 4], F32), ("CC1G", [128, 4], F32),
        ("ABSA", [128, 12], F32), ("AB3", [128, 4], F32),
        ("wqkv_bf", [128, DT * F3], BF16), ("wgate_bf", [128, DT * D], BF16),
        ("Sb", [128, DT * 1032], BF16),
        ("WST0", [128, 1024], F32), ("WST1", [128, 1024], F32),
        ("XT0", [128, D], F32), ("XT1", [128, D], F32),
        ("XH0", [128, D], F32), ("XH1", [128, D], F32),
        ("xnT0", [128, DT * CH], BF16), ("xnT1", [128, DT * CH], BF16),
        ("qT", [128, DT * CH], BF16),
        ("kT", [128, DT * CH], BF16), ("kTok", [128, NTT * D], BF16),
        ("vtok", [128, NTT * 1032], BF16), ("ATm", [128, NTT * CH], BF16),
        ("numTb", [128, DT * CH], BF16),
        ("GB0", [128, 512], F32), ("GB1", [128, 512], F32),
        ("K1B0", [128, 512], F32), ("K1B1", [128, 512], F32),
        ("RB0", [128, 512], F32), ("RB1", [128, 512], F32),
        ("MB0", [128, 512], F32), ("MB1", [128, 512], F32),
        ("MB20", [128, 512], F32), ("MB21", [128, 512], F32),
        ("JUNK", [128, 1024], F32), ("JK2", [128, 1024], F32),
        ("WQO0", [128, 1024], FP8), ("WQO1", [128, 1024], FP8),
    ]:
        sb(esA, nm, sh, dt)

    for nm, srch in (("trilb", trilb_h), ("onesb", onesb_h), ("onesf", onesf_h),
                     ("BVBC", bvbc_h), ("identf", identf_h), ("biasf", biasf_h)):
        E(Y, (lambda nm=nm, srch=srch: lambda e: e.dma_start(out=B[nm][:], in_=srch))(),
          deps=[t_rowcast], inc="LDC", inc_n=16)
    t_consts = p.tok("LDC")
    for tt in range(NTT):
        E(V, (lambda tt=tt: lambda e: e.memset(
            B["vtok"][:, tt * 1032 + 1024: tt * 1032 + 1025], 1.0))())
        E(V, (lambda tt=tt: lambda e: e.memset(
            B["vtok"][:, tt * 1032 + 1025: tt * 1032 + 1032], 0.0))())
    for i in range(DT):
        E(V, (lambda i=i: lambda e: e.memset(
            B["Sb"][:, i * 1032 + 1025: i * 1032 + 1032], 0.0))())
    E(V, lambda e: e.memset(B["CC1G"][:], 0.0))

    cast_toks = []
    pieces = []
    for k in range(DT):
        for c0 in (0, 1024, 2048):
            pieces.append((wqkv_h, "wqkv_bf", F3, k, c0, 1024))
    for k in range(DT):
        pieces.append((wgate_h, "wgate_bf", D, k, 0, 1024))
    for idx, (srch, dstbuf, wdt, k, c0, wlen) in enumerate(pieces):
        wst = B[f"WST{idx % 2}"]
        deps = [cast_toks[idx - 2]] if idx >= 2 else [t_rowcast]
        tl = E(Y, (lambda wst=wst, srch=srch, k=k, c0=c0, wlen=wlen: lambda e: e.dma_start(
            out=wst[:, 0:wlen], in_=srch[k * 128:(k + 1) * 128, c0:c0 + wlen]))(),
            deps=deps, inc=f"LDW{idx % 2}", inc_n=16)
        tc = E(S, (lambda wst=wst, k=k, dstbuf=dstbuf, wdt=wdt, c0=c0, wlen=wlen:
                   lambda e: e.copy(
            out=B[dstbuf][:, k * wdt + c0: k * wdt + c0 + wlen], in_=wst[:, 0:wlen]))(),
            deps=[tl, t_rowcast], inc="SE")
        cast_toks.append(tc)
    wcast_done = cast_toks[-1]
    if phase == 1:
        finish_probe([wcast_done])
        esA.close()
        esB = None
        es.close()
        return nc

    def emit_quant():
        abs_toks = []
        for i in range(12):
            wst = B[f"WST{i % 2}"]
            deps = [wcast_done] if i < 2 else [abs_toks[i - 2]]
            tl = E(Y, (lambda i=i, wst=wst: lambda e: e.dma_start(
                out=wst[:, 0:1024], in_=wsh_h[:, i * 1024:(i + 1) * 1024]))(),
                deps=deps, inc=f"LDW{i % 2}", inc_n=16)
            ta = E(S, (lambda i=i, wst=wst: lambda e: e.activation(
                out=B["JUNK"][:], in_=wst[:, 0:1024], func=AF.Abs,
                accum_out=B["ABSA"][:, i:i + 1]))(),
                deps=[tl] + ([abs_toks[i - 1]] if i >= 1 else []), inc="SE")
            abs_toks.append(ta)
        E(V, lambda e: e.memset(B["AB3"][:, 3:4], 0.0), deps=[t_rowcast])
        tr = None
        for m in range(3):
            tr = E(V, (lambda m=m: lambda e: e.reduce_sum(
                out=B["AB3"][:, m:m + 1], in_=B["ABSA"][:, 4 * m:4 * m + 4], axis=AX))(),
                deps=[abs_toks[4 * m + 3]], inc="VE")
        tmm = E(T, lambda e: e.matmul(psf[:, 3072:3076], B["onesf"][:],
                                      B["AB3"][:, 0:4], start=True, stop=True),
                deps=[tr, t_consts, bank["f6"]], inc="TE")
        tcp = E(V, lambda e: e.tensor_copy(out=B["CC1S"][:], in_=psf[:, 3072:3076]),
                deps=[tmm], inc="VE")
        tspw = E(G, lambda e: e.dma_start(out=cc1_in, in_=B["CC1S"][0:1, 0:4]),
                 deps=[tcp], inc="SPC", inc_n=16)
        tcc1 = E(G, lambda e: e.collective_compute(
            "AllGather", ALU.bypass, replica_groups=[list(range(NCORE))],
            ins=[cc1_in], outs=[cc1_out]), deps=[tspw], inc="CC")
        A["quant_cc1"] = tcc1
        A["quant_tcp"] = tcp
        A["quant_abs"] = abs_toks
        return "SPLIT"

    def quant_fin():
        tcc1 = A["quant_cc1"]
        tcp = A["quant_tcp"]
        abs_toks = A["quant_abs"]
        tg_ = E(Y, lambda e: e.dma_start(out=B["CC1G"][0:NCORE, 0:4], in_=cc1_out),
                deps=[tcc1], inc="LDG", inc_n=16)
        tmm2 = E(T, lambda e: e.matmul(psf[:, 3072:3076], B["onesf"][:],
                                       B["CC1G"][:], start=True, stop=True),
                 deps=[tg_, tcp], inc="TE")
        tmg = E(V, lambda e: e.tensor_scalar(
            out=B["MGV"][:], in0=psf[:, 3072:3076], scalar1=1.0 / WQN,
            scalar2=1e-5, op0=ALU.mult, op1=ALU.max), deps=[tmm2], inc="VE")
        sw_ready = E(V, lambda e: e.reciprocal(out=B["SWV"][:], in_=B["MGV"][:]),
                     deps=[tmg], inc="VE")
        bank["f6"] = sw_ready

        spw_toks = []
        t1_toks = []
        q_prev = (None, None)
        for i in range(12):
            m = i // 4
            wst = B[f"WST{i % 2}"]
            wqo = B[f"WQO{i % 2}"]
            deps = [abs_toks[-1], sw_ready] + ([t1_toks[i - 2]] if i >= 2 else [])
            tl = E(Y, (lambda i=i, wst=wst: lambda e: e.dma_start(
                out=wst[:, 0:1024], in_=wsh_h[:, i * 1024:(i + 1) * 1024]))(),
                deps=deps, inc=f"LDW{i % 2}", inc_n=16)
            t1 = E(V, (lambda m=m, wst=wst: lambda e: e.tensor_scalar(
                out=B["JUNK"][:], in0=wst[:, 0:1024], scalar1=B["SWV"][:, m:m + 1],
                scalar2=MAGIC, op0=ALU.mult, op1=ALU.add))(),
                deps=[tl] + ([q_prev[0]] if i >= 1 else []), inc="VE")
            t1_toks.append(t1)
            t2 = E(V, (lambda: lambda e: e.tensor_scalar(
                out=B["JK2"][:], in0=B["JUNK"][:], scalar1=MAGIC + 1.0,
                scalar2=MAGIC - 1.0, op0=ALU.min, op1=ALU.max))(),
                deps=[t1] + ([q_prev[1]] if i >= 1 else []), inc="VE")
            t3 = E(V, (lambda wqo=wqo: lambda e: e.tensor_scalar(
                out=wqo[:], in0=B["JK2"][:], scalar1=MAGIC, scalar2=None,
                op0=ALU.subtract))(),
                deps=[t2] + ([spw_toks[i - 2]] if i >= 2 else []), inc="VE")
            q_prev = (t1, t2)
            ts_ = E(G, (lambda i=i, wqo=wqo: lambda e: e.dma_start(
                out=cc2_in[:, i * 1024:(i + 1) * 1024], in_=wqo[:]))(),
                deps=[t3], inc=f"SPW{i % 2}", inc_n=16)
            spw_toks.append(ts_)
        cc2_done = E(G, lambda e: e.collective_compute(
            "AllGather", ALU.bypass, replica_groups=[list(range(NCORE))],
            ins=[cc2_in], outs=[cc2_out]), deps=spw_toks, inc="CC")
        if phase == 2:
            finish_probe([cc2_done, sw_ready])
            return None
        return sw_ready, cc2_done


    free = {"qT": None, "numTb": None, "kT": None, "kTok": None, "vtok": None,
            "ATm": None, "xnT": None}
    xt_free = [None, None]
    xh_free = [None, None]
    den_prev = None
    A = {"vts": None}

    def emit_LN(c):
        xn_toks = []
        for tt in range(NTT):
            g = c * NTT + tt
            par = g % 2
            w = 48 * par
            XT = B[f"XT{par}"]
            XH = B[f"XH{par}"]
            tl = E(Y, (lambda c=c, tt=tt, XT=XT: lambda e: e.dma_start(
                out=XT[:], in_=x_h[c * CH + tt * 128: c * CH + (tt + 1) * 128, :]))(),
                deps=[xt_free[par]], inc=f"LDX{par}", inc_n=16)
            E(V, (lambda XT=XT, w=w: lambda e: e.bn_stats(
                B["SC"][:, w:w + 6], XT[:, 0:512]))(), deps=[tl])
            E(V, (lambda XT=XT, w=w: lambda e: e.bn_stats(
                B["SC"][:, w + 6:w + 12], XT[:, 512:1024]))())
            E(V, (lambda w=w: lambda e: e.bn_aggr(
                B["SC"][:, w + 12:w + 14], B["SC"][:, w:w + 12]))())
            E(V, (lambda w=w: lambda e: e.tensor_scalar(
                out=B["SC"][:, w + 14:w + 15], in0=B["SC"][:, w + 13:w + 14],
                scalar1=LN_EPS, scalar2=None, op0=ALU.add))())
            tv = E(V, (lambda w=w: lambda e: e.reciprocal(
                out=B["SC"][:, w + 15:w + 16], in_=B["SC"][:, w + 14:w + 15]))(), inc="VE")
            tsq = E(S, (lambda w=w: lambda e: e.activation(
                out=B["SC"][:, w + 16:w + 17], in_=B["SC"][:, w + 15:w + 16],
                func=AF.Sqrt))(), deps=[tv], inc="SE")
            tneg = E(V, (lambda w=w: lambda e: e.tensor_scalar(
                out=B["SC"][:, w + 17:w + 18], in0=B["SC"][:, w + 12:w + 13],
                scalar1=B["SC"][:, w + 16:w + 17], scalar2=-1.0,
                op0=ALU.mult, op1=ALU.mult))(), deps=[tsq], inc="VE")
            txh = E(S, (lambda XT=XT, XH=XH, w=w: lambda e: e.activation(
                out=XH[:], in_=XT[:], func=AF.Identity,
                scale=B["SC"][:, w + 16:w + 17], bias=B["SC"][:, w + 17:w + 18]))(),
                deps=[tneg, xh_free[par]], inc="SE")
            xt_free[par] = txh
            bname = "f4" if par == 0 else "f2"
            pcol = 2048 if par == 0 else 1024
            ttr = None
            for k in range(DT):
                ttr = E(T, (lambda XH=XH, k=k, pcol=pcol: lambda e: e.transpose(
                    psf[:, pcol + k * 128: pcol + (k + 1) * 128],
                    XH[:, k * 128:(k + 1) * 128], B["identf"][:]))(),
                    deps=([txh, t_consts, bank[bname],
                           bank["f5" if par == 0 else "f3"]] if k == 0 else []),
                    inc="TE" if k == DT - 1 else None)
            tev = None
            XN = B[f"xnT{c % 2}"]
            for k in range(DT):
                tev = E(V, (lambda k=k, tt=tt, pcol=pcol, XN=XN: lambda e: e.tensor_scalar(
                    out=XN[:, k * CH + tt * 128: k * CH + (tt + 1) * 128],
                    in0=psf[:, pcol + k * 128: pcol + (k + 1) * 128],
                    scalar1=B["biasf"][:, 24 + k:25 + k],
                    scalar2=B["biasf"][:, 32 + k:33 + k],
                    op0=ALU.mult, op1=ALU.add))(),
                    deps=([ttr, free["xnT"]] if k == 0 else []),
                    inc="VE" if k == DT - 1 else None)
            xh_free[par] = tev
            bank[bname] = tev
            bank["f5" if par == 0 else "f3"] = tev
            xn_toks.append(tev)
        return xn_toks[-1]

    def emit_attn(c, xn_done):
        XN = B[f"xnT{c % 2}"]
        if phase == 40 and c == 0:
            finish_probe([xn_done])
            return None

        def feat_mm(bk_id, wbuf, wbase, fcol, deps0):
            tk = None
            for k in range(DT):
                tk = E(T, (lambda k=k, bk_id=bk_id, wbuf=wbuf, wbase=wbase, fcol=fcol:
                           lambda e: e.matmul(
                               bankf(bk_id),
                               B[wbuf][:, k * wbase + fcol * 128: k * wbase + fcol * 128 + 128],
                               XN[:, k * CH:(k + 1) * CH],
                               start=(k == 0), stop=(k == DT - 1)))(),
                    deps=(deps0 if k == 0 else []), inc="TE" if k == DT - 1 else None)
            return tk

        kT_toks = []
        tG_toks = []
        tRk_toks = []
        for f in range(DT):
            bg_ = (2 * f) % 4
            bk_ = (2 * f + 1) % 4
            par = f % 2
            tgm = feat_mm(bg_, "wgate_bf", D, f,
                          [xn_done, wcast_done, bank[f"f{bg_}"]])
            tkm = feat_mm(bk_, "wqkv_bf", F3, DT + f, [bank[f"f{bk_}"]])
            tGc = E(V, (lambda bg_=bg_, par=par, f=f: lambda e: e.tensor_scalar(
                out=B[f"MB{par}"][:], in0=bankf(bg_),
                scalar1=B["biasf"][:, 16 + f:17 + f], scalar2=None, op0=ALU.add))(),
                deps=[tgm, tG_toks[f - 2] if f >= 2 else None], inc="VE")
            tG0 = E(S, (lambda par=par: lambda e: e.activation(
                out=B[f"GB{par}"][:], in_=B[f"MB{par}"][:], func=AF.Tanh,
                scale=0.5))(), deps=[tGc, kT_toks[f - 2] if f >= 2 else None],
                inc="SE")
            tG = E(V, (lambda par=par: lambda e: e.tensor_scalar(
                out=B[f"GB{par}"][:], in0=B[f"GB{par}"][:], scalar1=0.5,
                scalar2=0.5, op0=ALU.mult, op1=ALU.add))(),
                deps=[tG0], inc="VE")
            tG_toks.append(tG0)
            bank[f"f{bg_}"] = tGc
            tK1 = E(V, (lambda bk_=bk_, par=par, f=f: lambda e: e.scalar_tensor_tensor(
                out=B[f"K1B{par}"][:], in0=bankf(bk_), scalar=B["biasf"][:, 8 + f:9 + f],
                in1=B[f"GB{par}"][:], op0=ALU.add, op1=ALU.mult))(),
                deps=[tkm, tG, tRk_toks[f - 2] if f >= 2 else None,
                      kT_toks[f - 2] if f >= 2 else None], inc="VE")
            bank[f"f{bk_}"] = tK1
            tR = E(S, (lambda par=par: lambda e: e.activation(
                out=B[f"RB{par}"][:], in_=B[f"K1B{par}"][:], func=AF.Relu))(),
                deps=[tK1, kT_toks[f - 2] if f >= 2 else None], inc="SE")
            tRk_toks.append(tR)
            tM = E(G, (lambda par=par: lambda e: e.tensor_tensor(
                out=B[f"MB2{par}"][:], in0=B[f"K1B{par}"][:], in1=B[f"RB{par}"][:],
                op=ALU.subtract))(), deps=[tR, tG, A.get("g_prev")], inc="GE")
            tE_ = E(S, (lambda par=par: lambda e: e.activation(
                out=B[f"GB{par}"][:], in_=B[f"MB2{par}"][:], func=AF.Exp))(),
                deps=[tM, tK1], inc="SE")
            tKT = E(G, (lambda par=par, f=f: lambda e: e.tensor_tensor(
                out=B["kT"][:, f * CH:(f + 1) * CH], in0=B[f"RB{par}"][:],
                in1=B[f"GB{par}"][:], op=ALU.add))(),
                deps=[tE_, free["kT"] if f == 0 else None], inc="GE")
            A["g_prev"] = tKT
            kT_toks.append(tKT)
        kT_done = kT_toks[-1]
        if c + 1 < NCH:
            A["xn_next"] = emit_LN(c + 1)
        if phase == 41 and c == 0:
            finish_probe([kT_done])
            return None

        qT_toks = []
        tRq_toks = []
        for f in range(DT):
            bq_ = (2 * f) % 4
            par = f % 2
            tqm = feat_mm(bq_, "wqkv_bf", F3, f, [bank[f"f{bq_}"]])
            tQc = E(V, (lambda bq_=bq_, par=par, f=f: lambda e: e.tensor_scalar(
                out=B[f"K1B{par}"][:], in0=bankf(bq_),
                scalar1=B["biasf"][:, f:f + 1], scalar2=None, op0=ALU.add))(),
                deps=[tqm, tRq_toks[f - 2] if f >= 2 else tRk_toks[DT - 2 + f],
                      qT_toks[f - 2] if f >= 2 else kT_toks[DT - 2 + f]],
                inc="VE")
            bank[f"f{bq_}"] = tQc
            tR = E(S, (lambda par=par: lambda e: e.activation(
                out=B[f"RB{par}"][:], in_=B[f"K1B{par}"][:], func=AF.Relu))(),
                deps=[tQc, qT_toks[f - 2] if f >= 2 else kT_toks[DT - 2 + f]],
                inc="SE")
            tRq_toks.append(tR)
            tM = E(G, (lambda par=par: lambda e: e.tensor_tensor(
                out=B[f"MB2{par}"][:], in0=B[f"K1B{par}"][:], in1=B[f"RB{par}"][:],
                op=ALU.subtract))(), deps=[tR, A.get("g_prev")], inc="GE")
            tE_ = E(S, (lambda par=par: lambda e: e.activation(
                out=B[f"GB{par}"][:], in_=B[f"MB2{par}"][:], func=AF.Exp))(),
                deps=[tM], inc="SE")
            tQT = E(G, (lambda par=par, f=f: lambda e: e.tensor_tensor(
                out=B["qT"][:, f * CH:(f + 1) * CH], in0=B[f"RB{par}"][:],
                in1=B[f"GB{par}"][:], op=ALU.add))(),
                deps=[tE_, free["qT"] if f == 0 else None], inc="GE")
            A["g_prev"] = tQT
            qT_toks.append(tQT)
        qT_done = qT_toks[-1]
        if phase == 42 and c == 0:
            finish_probe([qT_done])
            return None

        v_toks = []
        for tt in range(NTT):
            b0 = "f2" if tt % 2 == 0 else "f4"
            b1 = "f3" if tt % 2 == 0 else "f5"
            vb = 1024 if tt % 2 == 0 else 2048
            tvm = None
            for h in range(2):
                for k in range(DT):
                    tvm = E(T, (lambda tt=tt, h=h, k=k, vb=vb, XN=XN: lambda e: e.matmul(
                        psf[:, vb + h * 512: vb + (h + 1) * 512],
                        XN[:, k * CH + tt * 128: k * CH + (tt + 1) * 128],
                        B["wqkv_bf"][:, k * F3 + 2048 + h * 512: k * F3 + 2048 + (h + 1) * 512],
                        start=(k == 0), stop=(k == DT - 1)))(),
                        deps=([xn_done, bank[b0], bank[b1],
                               free["vtok"] if tt == 0 else None]
                              if (h == 0 and k == 0) else []),
                        inc="TE" if (h == 1 and k == DT - 1) else None)
            tve = E(V, (lambda tt=tt, vb=vb: lambda e: e.tensor_tensor(
                out=B["vtok"][:, tt * 1032: tt * 1032 + 1024], in0=psf[:, vb:vb + 1024],
                in1=B["BVBC"][:], op=ALU.add))(),
                deps=[tvm, t_consts], inc="VE")
            bank[b0] = tve
            bank[b1] = tve
            v_toks.append(tve)
        free["xnT"] = v_toks[-1]
        A["ln_ready"] = True

        if phase == 43 and c == 0:
            finish_probe([v_toks[-1]])
            return None

        at_toks = []
        for si in range(NTT):
            bk_id = si % 2
            n_si = CH - si * 128
            tat = None
            for k in range(DT):
                tat = E(T, (lambda si=si, k=k, bk_id=bk_id, n_si=n_si: lambda e: e.matmul(
                    bankf(bk_id, n_si),
                    B["kT"][:, k * CH + si * 128: k * CH + si * 128 + 128],
                    B["qT"][:, k * CH + si * 128:(k + 1) * CH],
                    start=(k == 0), stop=(k == DT - 1)))(),
                    deps=([kT_done, qT_done, bank[f"f{bk_id}"],
                           free["ATm"] if si == 0 else None] if k == 0 else []),
                    inc="TE" if k == DT - 1 else None)
            tdg = E(V, (lambda si=si, bk_id=bk_id: lambda e: e.tensor_tensor(
                out=B["ATm"][:, si * CH + si * 128: si * CH + si * 128 + 128],
                in0=bankf(bk_id, 128), in1=B["trilb"][:], op=ALU.mult))(),
                deps=[tat], inc="VE")
            tcp2 = tdg
            if si < NTT - 1:
                tcp2 = E(V, (lambda si=si, bk_id=bk_id, n_si=n_si: lambda e: e.tensor_copy(
                    out=B["ATm"][:, si * CH + si * 128 + 128: si * CH + CH],
                    in_=bankf(bk_id)[:, 128:n_si]))(), deps=[tat], inc="VE")
            bank[f"f{bk_id}"] = [tdg, tcp2]
            at_toks.append((tdg, tcp2))
        at_done_v = at_toks[-1][0]
        at_done_s = at_toks[-2][1]

        if phase == 44 and c == 0:
            finish_probe([at_toks[-1][0], at_toks[-2][1]])
            return None

        dgr = []
        for tt in range(NTT):
            for si in range(tt + 1):
                dgr.append((lambda si=si, tt=tt: (
                    lambda st, sp: lambda e: e.matmul(
                        psf[:, 3072 + tt: 3072 + tt + 1],
                        B["ATm"][:, si * CH + tt * 128: si * CH + (tt + 1) * 128],
                        B["onesb"][:, 0:1], start=st, stop=sp)))())
        if c > 0:
            for tt in range(NTT):
                for k in range(DT):
                    dgr.append((lambda tt=tt, k=k: (
                        lambda st, sp: lambda e: e.matmul(
                            psf[:, 3072 + tt: 3072 + tt + 1],
                            B["qT"][:, k * CH + tt * 128: k * CH + (tt + 1) * 128],
                            B["Sb"][:, k * 1032 + 1024: k * 1032 + 1025],
                            start=st, stop=sp)))())
        tden = None
        for i, mk in enumerate(dgr):
            tden = E(T, mk(i == 0, i == len(dgr) - 1),
                     deps=([at_done_v, at_done_s, qT_done, bank["f6"]] if i == 0 else []),
                     inc="TE" if i == len(dgr) - 1 else None)
        tdev = E(V, (lambda c=c: lambda e: e.tensor_copy(
            out=B["den_all"][:, c * NTT:(c + 1) * NTT], in_=psf[:, 3072:3072 + NTT]))(),
            deps=[tden], inc="VE")
        bank["f6"] = tdev

        if phase == 45 and c == 0:
            finish_probe([tdev])
            return None

        num_toks = []
        for j in range(DT):
            bk_id = j % 2
            grp = []
            if c > 0:
                for k in range(DT):
                    grp.append((lambda j=j, k=k, bk_id=bk_id: lambda e: e.matmul(
                        bankf(bk_id),
                        B["Sb"][:, k * 1032 + j * 128: k * 1032 + j * 128 + 128],
                        B["qT"][:, k * CH:(k + 1) * CH],
                        start=(k == 0), stop=False))())
            for si in range(NTT):
                st_ = (c == 0 and si == 0)
                grp.append((lambda j=j, si=si, st_=st_, bk_id=bk_id: lambda e: e.matmul(
                    psf[:, 512 * bk_id + si * 128: 512 * bk_id + 512],
                    B["vtok"][:, si * 1032 + j * 128: si * 1032 + j * 128 + 128],
                    B["ATm"][:, si * CH + si * 128: si * CH + CH],
                    start=st_, stop=(si == NTT - 1)))())
            tn = None
            for i, fn in enumerate(grp):
                tn = E(T, fn,
                       deps=([bank[f"f{bk_id}"], at_done_v, at_done_s,
                              v_toks[-1], qT_done, A["vts"] if j == 0 else None]
                             if i == 0 else []),
                       inc="TE" if i == len(grp) - 1 else None)
            tne = E(V, (lambda j=j, bk_id=bk_id: lambda e: e.tensor_copy(
                out=B["numTb"][:, j * CH:(j + 1) * CH], in_=bankf(bk_id)))(),
                deps=[tn, free["numTb"] if j == 0 else None], inc="VE")
            bank[f"f{bk_id}"] = tne
            num_toks.append(tne)

        if phase == 46 and c == 0:
            finish_probe([num_toks[-1]])
            return None

        kt_toks = []
        for tt in range(NTT):
            ttr = None
            for k in range(DT):
                ttr = E(T, (lambda tt=tt, k=k: lambda e: e.transpose(
                    psb[:, k * 128:(k + 1) * 128],
                    B["kT"][:, k * CH + tt * 128: k * CH + (tt + 1) * 128],
                    B["identb"][:]))(),
                    deps=([kT_done, bank["b7"], free["kTok"] if tt == 0 else None]
                          if k == 0 else []),
                    inc="TE" if k == DT - 1 else None)
            tev = E(V, (lambda tt=tt: lambda e: e.tensor_copy(
                out=B["kTok"][:, tt * D:(tt + 1) * D], in_=psb[:]))(),
                deps=[ttr], inc="VE")
            bank["b7"] = tev
            kt_toks.append(tev)

        if phase == 47 and c == 0:
            finish_probe([kt_toks[-1]])
            return None

        ds_toks = []
        for i in range(DT):
            b0 = "f2" if i % 2 == 0 else "f4"
            b1 = "f3" if i % 2 == 0 else "f5"
            vb = 1024 if i % 2 == 0 else 2048
            tds = None
            for h in range(2):
                for tt in range(NTT):
                    tds = E(T, (lambda i=i, h=h, tt=tt, vb=vb: lambda e: e.matmul(
                        psf[:, vb + h * 512: vb + (h + 1) * 512],
                        B["kTok"][:, tt * D + i * 128: tt * D + i * 128 + 128],
                        B["vtok"][:, tt * 1032 + h * 512: tt * 1032 + (h + 1) * 512],
                        start=(tt == 0), stop=(tt == NTT - 1)))(),
                        deps=([kt_toks[-1], v_toks[-1], bank[b0], bank[b1],
                               num_toks[-1] if i == 0 else None]
                              if (h == 0 and tt == 0) else []),
                        inc="TE" if (h == 1 and tt == NTT - 1) else None)
            tdk = None
            for tt in range(NTT):
                tdk = E(T, (lambda i=i, tt=tt: lambda e: e.matmul(
                    psf[:, 3072 + 8 + i: 3072 + 8 + i + 1],
                    B["kTok"][:, tt * D + i * 128: tt * D + i * 128 + 128],
                    B["vtok"][:, tt * 1032 + 1024: tt * 1032 + 1025],
                    start=(tt == 0), stop=(tt == NTT - 1)))(),
                    deps=([bank["f6"]] if (i == 0 and tt == 0) else []),
                    inc="TE" if tt == NTT - 1 else None)
            if c == 0:
                tsa = E(V, (lambda i=i, vb=vb: lambda e: e.tensor_copy(
                    out=B["Sb"][:, i * 1032: i * 1032 + 1024], in_=psf[:, vb:vb + 1024]))(),
                    deps=[tds], inc="VE")
                tsk = E(V, (lambda i=i: lambda e: e.tensor_copy(
                    out=B["Sb"][:, i * 1032 + 1024: i * 1032 + 1025],
                    in_=psf[:, 3072 + 8 + i: 3072 + 8 + i + 1]))(),
                    deps=[tdk], inc="VE")
            else:
                tsa = E(V, (lambda i=i, vb=vb: lambda e: e.tensor_tensor(
                    out=B["Sb"][:, i * 1032: i * 1032 + 1024], in0=psf[:, vb:vb + 1024],
                    in1=B["Sb"][:, i * 1032: i * 1032 + 1024], op=ALU.add))(),
                    deps=[tds], inc="VE")
                tsk = E(V, (lambda i=i: lambda e: e.tensor_tensor(
                    out=B["Sb"][:, i * 1032 + 1024: i * 1032 + 1025],
                    in0=psf[:, 3072 + 8 + i: 3072 + 8 + i + 1],
                    in1=B["Sb"][:, i * 1032 + 1024: i * 1032 + 1025], op=ALU.add))(),
                    deps=[tdk], inc="VE")
            bank[b0] = tsa
            bank[b1] = tsa
            ds_toks.append(tsk)
        A["vts"] = ds_toks[-1]
        A["ds_toks"] = ds_toks
        bank["f6"] = ds_toks[-1]
        tkb = ds_toks[-1]

        tq_sp = E(G, (lambda c=c: lambda e: e.dma_start(
            out=sc_qT[c * 128:(c + 1) * 128, :], in_=B["qT"][:]))(),
            deps=[qT_done, tden, num_toks[-1]], inc="SPQ", inc_n=16)
        tn_sp = E(G, (lambda c=c: lambda e: e.dma_start(
            out=sc_num[c * 128:(c + 1) * 128, :], in_=B["numTb"][:]))(),
            deps=[num_toks[-1]], inc="SPN", inc_n=16)

        free["qT"] = tq_sp
        free["numTb"] = tn_sp
        free["kT"] = kt_toks[-1]
        free["kTok"] = ds_toks[-1]
        free["vtok"] = ds_toks[-1]
        free["ATm"] = num_toks[-1]
        if phase == 30 + c:
            finish_probe([ds_toks[-1], tkb, tdev, free["qT"], free["numTb"]])
            return None
        return (ds_toks[-1], tkb, tdev)

    xn_cur = emit_LN(0)
    ret = None
    sw_ready = cc2_done = None
    for c in range(NCH):
        ret = emit_attn(c, xn_cur)
        if ret is None:
            esA.close()
            es.close()
            return nc
        if c == 0:
            q = emit_quant()
            if q is None:
                esA.close()
                es.close()
                return nc
        if c == 2:
            q = quant_fin()
            if q is None:
                esA.close()
                es.close()
                return nc
            sw_ready, cc2_done = q
        xn_cur = A.get("xn_next")
    S_done, ks_done, den_prev = ret

    if phase == 3:
        finish_probe([S_done, ks_done, den_prev, free["qT"], free["numTb"]])
        esA.close()
        es.close()
        return nc

    ds8 = A["ds_toks"]
    sps = []
    for i in range(DT):
        sps.append(E(G, (lambda i=i: lambda e: e.dma_start(
            out=cc3_in[i * 128:(i + 1) * 128, :],
            in_=B["Sb"][:, i * 1032:(i + 1) * 1032]))(),
            deps=[ds8[i]], inc="SPSA" if i < 4 else "SPS", inc_n=16))
    cc3a_done = E(G, lambda e: e.collective_compute(
        "AllGather", ALU.bypass, replica_groups=[list(range(NCORE))],
        ins=[cc3_in[0:512, :]], outs=[cc3a_out]), deps=[sps[3]], inc="CC")
    cc3b_done = E(G, lambda e: e.collective_compute(
        "AllGather", ALU.bypass, replica_groups=[list(range(NCORE))],
        ins=[cc3_in[512:1024, :]], outs=[cc3b_out]), deps=[sps[7]], inc="CC")
    cc3_done = cc3b_done

    a_end = [p.tok("TE"), p.tok("SE"), p.tok("VE"),
             p.tok("SPQ"), p.tok("SPN"), p.tok("SPS"), p.tok("SPSA")]
    if phase == 4:
        finish_probe([cc3_done] + a_end)
        esA.close()
        es.close()
        return nc
    esA.close()

    esB = ExitStack()
    for nm, sh, dt in [
        ("wqg", [128, DT * INNER], FP8), ("wqu", [128, DT * INNER], FP8),
        ("wqd", [128, 32 * D], FP8),
        ("Sgp", [128, DT * D], BF16), ("ksgp", [128, DT], BF16),
        ("wproj_bf", [128, DT * D], BF16),
    ]:
        sb(esB, nm, sh, dt)

    wq_toks = []
    for cbl in range(NCORE):
        dep = ([cc2_done] + a_end) if cbl == 0 else []
        wq_toks.append(E(S, (lambda cbl=cbl: lambda e: e.dma_start(
            out=B["wqg"][:, cbl * INNER:(cbl + 1) * INNER],
            in_=cc2_out[cbl * 128:(cbl + 1) * 128, 0:INNER]))(), deps=dep,
            inc="LDBW", inc_n=16))
        wq_toks.append(E(S, (lambda cbl=cbl: lambda e: e.dma_start(
            out=B["wqu"][:, cbl * INNER:(cbl + 1) * INNER],
            in_=cc2_out[cbl * 128:(cbl + 1) * 128, INNER:2 * INNER]))(),
            inc="LDBW", inc_n=16))
        wq_toks.append(E(S, (lambda cbl=cbl: lambda e: e.dma_start(
            out=B["wqd"][:, cbl * 4 * D:(cbl + 1) * 4 * D],
            in_=cc2_out[cbl * 128:(cbl + 1) * 128, 2 * INNER:2 * INNER + 4 * D]))(),
            inc="LDBW", inc_n=16))
    wq_done = wq_toks[-1]

    esX = ExitStack()
    sb(esX, "SG", [128, 2 * NCORE * 1032], BF16)
    sb(esX, "maskI", [128, NCORE * 128], BF16)
    t_mask = E(Y, lambda e: e.dma_start(out=B["maskI"][:], in_=maskI_h),
               deps=a_end + [sps[7]], inc="LDS", inc_n=16)
    red_last = None
    red_evs = []
    for i in range(DT):
        half = cc3a_out if i < 4 else cc3b_out
        hbase = (i % 4) * 128
        sg0 = (i % 2) * NCORE * 1032
        dep = ([cc3a_done if i < 4 else cc3b_done]
               + (a_end if i == 0 else [])
               + ([red_evs[i - 2]] if i >= 2 else []))
        tl = None
        for cc in range(NCORE):
            tl = E(Y, (lambda cc=cc, half=half, hbase=hbase, sg0=sg0:
                       lambda e: e.dma_start(
                out=B["SG"][:, sg0 + cc * 1032: sg0 + (cc + 1) * 1032],
                in_=half[cc * 512 + hbase: cc * 512 + hbase + 128, :]))(),
                deps=(dep if cc == 0 else []),
                inc="LDS" if i % 2 == 0 else "LDS2", inc_n=16)
        evs = []
        for h in range(2):
            tm = None
            for cbl in range(NCORE):
                tm = E(T, (lambda cbl=cbl, h=h, sg0=sg0: lambda e: e.matmul(
                    bankf(h), B["maskI"][:, cbl * 128:(cbl + 1) * 128],
                    B["SG"][:, sg0 + cbl * 1032 + h * 512:
                            sg0 + cbl * 1032 + (h + 1) * 512],
                    start=(cbl == 0), stop=(cbl == NCORE - 1)))(),
                    deps=([tl, bank[f"f{h}"], t_mask] if cbl == 0 else []),
                    inc="TE" if cbl == NCORE - 1 else None)
            tev = E(V, (lambda i=i, h=h: lambda e: e.tensor_copy(
                out=B["Sgp"][:, i * D + h * 512: i * D + (h + 1) * 512],
                in_=bankf(h)))(), deps=[tm] + (a_end if i == 0 else []), inc="VE")
            bank[f"f{h}"] = tev
            evs.append(tev)
        tm = None
        for cbl in range(NCORE):
            tm = E(T, (lambda cbl=cbl, i=i, sg0=sg0: lambda e: e.matmul(
                psf[:, 1024 + i: 1024 + i + 1],
                B["maskI"][:, cbl * 128:(cbl + 1) * 128],
                B["SG"][:, sg0 + cbl * 1032 + 1024: sg0 + cbl * 1032 + 1025],
                start=(cbl == 0), stop=(cbl == NCORE - 1)))(),
                deps=([bank["f2"]] if (i == 0 and cbl == 0) else []),
                inc="TE" if cbl == NCORE - 1 else None)
        tev = E(V, (lambda i=i: lambda e: e.tensor_copy(
            out=B["ksgp"][:, i:i + 1], in_=psf[:, 1024 + i:1024 + i + 1]))(),
            deps=[tm], inc="VE")
        if i == DT - 1:
            bank["f2"] = tev
        red_last = tev
        red_evs.append(tev)
    sgp_done = red_last
    esX.close()

    for nm, sh, dt in [
        ("QTL", [128, DT * CH], BF16), ("NTL", [128, DT * CH], BF16),
        ("XT2a", [128, D], F32), ("XT2b", [128, D], F32),
        ("X2a", [128, D], F32), ("X2b", [128, D], F32),
        ("XQ", [128, D], BF16), ("XQT0", [128, D], BF16), ("XQT1", [128, D], BF16),
        ("SILU0", [128, 512], BF16), ("SILU1", [128, 512], BF16),
        ("PBUF", [128, INNER], F32),
        ("HQ", [128, INNER], BF16), ("HQT", [128, INNER], BF16),
    ]:
        sb(esB, nm, sh, dt)

    wp_toks = []
    for k in range(DT):
        stg2 = B["XT2a"] if k % 2 == 0 else B["XT2b"]
        dep = (a_end + [cc3_done, sgp_done]) if k < 2 else [wp_toks[k - 2]]
        tl = E(Y, (lambda k=k, stg2=stg2: lambda e: e.dma_start(
            out=stg2[:], in_=wproj_h[k * 128:(k + 1) * 128, :]))(),
            deps=dep, inc=f"LDW{k % 2}", inc_n=16)
        tc = E(S, (lambda k=k, stg2=stg2: lambda e: e.copy(
            out=B["wproj_bf"][:, k * D:(k + 1) * D], in_=stg2[:]))(),
            deps=[tl] + (a_end if k == 0 else []), inc="SE")
        wp_toks.append(tc)

    if phase == 5:
        finish_probe([sgp_done, wq_done, wp_toks[-1]])
        esB.close()
        es.close()
        return nc

    st = {
        "qT_load": None, "nT_load": None, "csum_free": None, "denB": None,
        "denB2": None, "denB3": None, "drow_free": None,
        "xt2_free": [wp_toks[-2], wp_toks[-1]], "hpre_free": [None, None],
        "x2_free": [None, None],
        "swin_free": [None, None],
        "xq_free": None, "xqt_free": [None, None], "hq_free": None,
        "qa_sfree": None,
        "slot_tok": [[None, None], [None, None]],
        "hqt_free": None,
        "outst_free": None, "psb_free": bank["b7"],
        "silu_free": [None, None],
    }
    st["qT_load"] = E(Y, lambda e: e.dma_start(out=B["QTL"][:], in_=sc_qT[0:128, :]),
                      deps=[a_end[3], cc3_done], inc="LDB0", inc_n=16)
    st["nT_load"] = E(Y, lambda e: e.dma_start(out=B["NTL"][:], in_=sc_num[0:128, :]),
                      deps=[a_end[4]], inc="LDB1", inc_n=16)

    for c in range(NCH):
        comb_toks = []
        inter_last = None
        for j in range(DT):
            bk_id = j % 2
            tm = None
            for k in range(DT):
                tm = E(T, (lambda j=j, k=k, bk_id=bk_id: lambda e: e.matmul(
                    bankf(bk_id), B["Sgp"][:, k * D + j * 128: k * D + j * 128 + 128],
                    B["QTL"][:, k * CH:(k + 1) * CH],
                    start=(k == 0), stop=(k == DT - 1)))(),
                    deps=([sgp_done, st["qT_load"], bank[f"f{bk_id}"]] if k == 0 else []),
                    inc="TE" if k == DT - 1 else None)
            inter_last = tm
            tcb = E(V, (lambda j=j, bk_id=bk_id: lambda e: e.tensor_tensor(
                out=B["NTL"][:, j * CH:(j + 1) * CH], in0=bankf(bk_id),
                in1=B["NTL"][:, j * CH:(j + 1) * CH], op=ALU.add))(),
                deps=[tm, st["nT_load"], st["csum_free"] if j == 0 else None],
                inc="VE")
            bank[f"f{bk_id}"] = tcb
            comb_toks.append(tcb)
        csum_done = comb_toks[-1]

        tdn = None
        pairs = [(tt, k) for tt in range(NTT) for k in range(DT)]
        for idx, (tt, k) in enumerate(pairs):
            tdn = E(T, (lambda tt=tt, k=k, idx=idx: lambda e: e.matmul(
                psf[:, 3072 + tt:3072 + tt + 1],
                B["QTL"][:, k * CH + tt * 128: k * CH + (tt + 1) * 128],
                B["ksgp"][:, k:k + 1], start=(idx == 0), stop=(idx == len(pairs) - 1)))(),
                deps=([sgp_done, bank["f6"], st["qT_load"]] if idx == 0 else []),
                inc="TE" if idx == len(pairs) - 1 else None)
        tdt = E(V, (lambda c=c: lambda e: e.scalar_tensor_tensor(
            out=B["DTOT"][:], in0=psf[:, 3072:3072 + NTT], scalar=DEN_EPS,
            in1=B["den_all"][:, c * NTT:(c + 1) * NTT], op0=ALU.add, op1=ALU.add))(),
            deps=[tdn, st["denB3"]], inc="VE")
        trd = E(V, lambda e: e.reciprocal(out=B["RDEN"][:], in_=B["DTOT"][:]),
                deps=[tdt], inc="VE")
        st["denB3"] = trd
        drow_done = trd
        bank["f6"] = tdt

        xls = {}
        for tt in range(2):
            par = tt % 2
            XT2 = B["XT2a"] if par == 0 else B["XT2b"]
            xls[tt] = E(Y, (lambda c=c, tt=tt, XT2=XT2: lambda e: e.dma_start(
                out=XT2[:], in_=x_h[c * CH + tt * 128: c * CH + (tt + 1) * 128, :]))(),
                deps=[st["xt2_free"][par]], inc=f"LDX{par}", inc_n=16)
        if c + 1 < NCH:
            st["qT_load"] = E(Y, (lambda c=c: lambda e: e.dma_start(
                out=B["QTL"][:], in_=sc_qT[(c + 1) * 128:(c + 2) * 128, :]))(),
                deps=[tdn, csum_done], inc="LDB0", inc_n=16)
            st["nT_load"] = None

        proj_last = {}
        tile = {}

        def projxq(tt):
            par = tt % 2
            w = 48 * par
            XT2 = B["XT2a"] if par == 0 else B["XT2b"]
            X2p = B["X2a"] if par == 0 else B["X2b"]
            XQT = B["XQT0"] if par == 0 else B["XQT1"]
            pb = ("f2", "f3") if tt % 2 == 0 else ("f4", "f5")
            vb = 1024 if tt % 2 == 0 else 2048
            tpm = None
            for h in range(2):
                for k in range(DT):
                    tpm = E(T, (lambda tt=tt, h=h, k=k, vb=vb: lambda e: e.matmul(
                        psf[:, vb + h * 512: vb + (h + 1) * 512],
                        B["NTL"][:, k * CH + tt * 128: k * CH + (tt + 1) * 128],
                        B["wproj_bf"][:, k * D + h * 512: k * D + (h + 1) * 512],
                        start=(k == 0), stop=(k == DT - 1)))(),
                        deps=([csum_done, drow_done, wp_toks[-1],
                               bank[pb[0]], bank[pb[1]]] if (h == 0 and k == 0) else []),
                        inc="TE" if (h == 1 and k == DT - 1) else None)
            proj_last[tt] = tpm
            thp = E(V, (lambda tt=tt, vb=vb, XT2=XT2: lambda e:
                        e.scalar_tensor_tensor(
                            out=XT2[:], in0=psf[:, vb:vb + 1024],
                            scalar=B["RDEN"][:, tt:tt + 1], in1=XT2[:],
                            op0=ALU.mult, op1=ALU.add))(),
                deps=[tpm, xls[tt], trd], inc="VE")
            thp = E(V, (lambda XT2=XT2: lambda e: e.tensor_tensor(
                out=XT2[:], in0=XT2[:], in1=B["BPBC"][:], op=ALU.add))(), inc="VE")
            tss = E(S, (lambda XT2=XT2, X2p=X2p, w=w: lambda e: e.activation(
                out=X2p[:], in_=XT2[:], func=AF.Identity,
                accum_out=B["SC"][:, w:w + 1]))(),
                deps=[thp, st["swin_free"][par], st["x2_free"][par]], inc="SE")
            tsq2 = E(S, (lambda XT2=XT2, X2p=X2p, w=w: lambda e: e.activation(
                out=X2p[:], in_=XT2[:], func=AF.Square,
                accum_out=B["SC"][:, w + 1:w + 2]))(), inc="SE")
            E(V, (lambda w=w: lambda e: e.tensor_scalar(
                out=B["SC"][:, w + 12:w + 13], in0=B["SC"][:, w:w + 1],
                scalar1=1.0 / 1024.0, scalar2=None, op0=ALU.mult))(),
              deps=[tss, tsq2])
            E(V, (lambda w=w: lambda e: e.tensor_scalar(
                out=B["SC"][:, w + 13:w + 14], in0=B["SC"][:, w + 1:w + 2],
                scalar1=1.0 / 1024.0, scalar2=LN_EPS, op0=ALU.mult, op1=ALU.add))())
            E(V, (lambda w=w: lambda e: e.tensor_scalar(
                out=B["SC"][:, w + 14:w + 15], in0=B["SC"][:, w + 12:w + 13],
                scalar1=B["SC"][:, w + 12:w + 13], scalar2=-1.0,
                op0=ALU.mult, op1=ALU.mult))())
            E(V, (lambda w=w: lambda e: e.tensor_tensor(
                out=B["SC"][:, w + 14:w + 15], in0=B["SC"][:, w + 13:w + 14],
                in1=B["SC"][:, w + 14:w + 15], op=ALU.add))())
            tv = E(V, (lambda w=w: lambda e: e.reciprocal(
                out=B["SC"][:, w + 15:w + 16], in_=B["SC"][:, w + 14:w + 15]))(), inc="VE")
            tsq = E(S, (lambda w=w: lambda e: e.activation(
                out=B["SC"][:, w + 16:w + 17], in_=B["SC"][:, w + 15:w + 16],
                func=AF.Sqrt))(), deps=[tv], inc="SE")
            tneg = E(V, (lambda w=w: lambda e: e.tensor_scalar(
                out=B["SC"][:, w + 17:w + 18], in0=B["SC"][:, w + 12:w + 13],
                scalar1=B["SC"][:, w + 16:w + 17], scalar2=-1.0,
                op0=ALU.mult, op1=ALU.mult))(), deps=[tsq], inc="VE")
            tx2 = E(S, (lambda XT2=XT2, X2p=X2p, w=w: lambda e: e.activation(
                out=X2p[:], in_=XT2[:], func=AF.Identity,
                scale=B["SC"][:, w + 16:w + 17], bias=B["SC"][:, w + 17:w + 18]))(),
                deps=[tneg, st["x2_free"][par]], inc="SE")
            E(V, (lambda X2p=X2p, w=w: lambda e: e.tensor_reduce(
                out=B["SC"][:, w + 18:w + 19], in_=X2p[:],
                axis=AX, op=ALU.max, apply_absolute_value=True))(), deps=[tx2])
            E(V, (lambda w=w: lambda e: e.tensor_scalar(
                out=B["SC"][:, w + 19:w + 20], in0=B["SC"][:, w + 18:w + 19],
                scalar1=1e-5, scalar2=None, op0=ALU.max))())
            E(V, (lambda w=w: lambda e: e.reciprocal(
                out=B["SC"][:, w + 20:w + 21], in_=B["SC"][:, w + 19:w + 20]))())
            ts1 = E(V, (lambda w=w: lambda e: e.tensor_scalar(
                out=B["SC"][:, w + 21:w + 22], in0=B["SC"][:, w + 20:w + 21],
                scalar1=127.0, scalar2=None, op0=ALU.mult))(), inc="VE")
            tc1g = E(V, (lambda w=w: lambda e: e.tensor_scalar(
                out=B["SC"][:, w + 22:w + 23], in0=B["SC"][:, w + 19:w + 20],
                scalar1=B["MGV"][:, 0:1], scalar2=1.0 / 127.0,
                op0=ALU.mult, op1=ALU.mult))(), deps=[sw_ready], inc="VE")
            tc1u = E(V, (lambda w=w: lambda e: e.tensor_scalar(
                out=B["SC"][:, w + 23:w + 24], in0=B["SC"][:, w + 19:w + 20],
                scalar1=B["MGV"][:, 1:2], scalar2=1.0 / 127.0,
                op0=ALU.mult, op1=ALU.mult))(), inc="VE")
            txq = None
            for hh in range(2):
                E(V, (lambda X2p=X2p, XT2=XT2, hh=hh, w=w: lambda e: e.tensor_scalar(
                    out=XT2[:, 0:512], in0=X2p[:, hh * 512:(hh + 1) * 512],
                    scalar1=B["SC"][:, w + 21:w + 22], scalar2=MAGIC,
                    op0=ALU.mult, op1=ALU.add))(),
                  deps=([ts1, st["qa_sfree"]] if hh == 0 else []))
                txq = E(V, (lambda XT2=XT2, hh=hh: lambda e: e.tensor_scalar(
                    out=B["XQ"][:, hh * 512:(hh + 1) * 512], in0=XT2[:, 0:512],
                    scalar1=MAGIC, scalar2=None, op0=ALU.subtract))(),
                    deps=[st["xq_free"] if hh == 0 else None], inc="VE")
            ttr = None
            for k in range(DT):
                ttr = E(T, (lambda k=k: lambda e: e.transpose(
                    psb[:, k * 128:(k + 1) * 128], B["XQ"][:, k * 128:(k + 1) * 128],
                    B["identb"][:]))(),
                    deps=([txq, st["psb_free"]] if k == 0 else []),
                    inc="TE" if k == DT - 1 else None)
            st["xq_free"] = ttr
            txqt = E(V, (lambda XQT=XQT: lambda e: e.tensor_copy(
                out=XQT[:], in_=psb[:]))(),
                deps=[ttr, st["xqt_free"][par]], inc="VE")
            st["psb_free"] = txqt
            tile[tt] = {"tx2": tx2, "txqt": txqt, "tc1g": tc1g, "tc1u": tc1u,
                        "txq": txq}

        def gup(tt):
            par = tt % 2
            w = 48 * par
            XQT = B["XQT0"] if par == 0 else B["XQT1"]
            XT2 = B["XT2a"] if par == 0 else B["XT2b"]
            txqt = tile[tt]["txqt"]
            tc1g = tile[tt]["tc1g"]
            tc1u = tile[tt]["tc1u"]
            gu_last = None
            pb_toks = []
            for ic in range(NIC):
                gb = f"f{ic % 2}"
                ub = f"f{2 + ic % 2}"
                tgm = None
                for k in range(DT):
                    tgm = E(T, (lambda ic=ic, k=k, gb=gb, XQT=XQT: lambda e: e.matmul(
                        bankf(int(gb[1])), XQT[:, k * 128:(k + 1) * 128],
                        B["wqg"][:, k * INNER + ic * 512: k * INNER + (ic + 1) * 512],
                        start=(k == 0), stop=(k == DT - 1)))(),
                        deps=([txqt, wq_done, bank[gb]] if k == 0 else []),
                        inc="TE" if k == DT - 1 else None)
                tum = None
                for k in range(DT):
                    tum = E(T, (lambda ic=ic, k=k, ub=ub, XQT=XQT: lambda e: e.matmul(
                        bankf(int(ub[1])), XQT[:, k * 128:(k + 1) * 128],
                        B["wqu"][:, k * INNER + ic * 512: k * INNER + (ic + 1) * 512],
                        start=(k == 0), stop=(k == DT - 1)))(),
                        deps=([bank[ub]] if k == 0 else []),
                        inc="TE" if k == DT - 1 else None)
                tslc = E(V, (lambda ic=ic, gb=gb, w=w, XT2=XT2: lambda e: e.tensor_scalar(
                    out=XT2[:, (ic % 2) * 512:(ic % 2) * 512 + 512],
                    in0=bankf(int(gb[1])),
                    scalar1=B["SC"][:, w + 22:w + 23], scalar2=None, op0=ALU.mult))(),
                    deps=[tgm, tc1g, st["slot_tok"][par][ic % 2]], inc="VE")
                tsl = E(S, (lambda ic=ic, XT2=XT2: lambda e: e.activation(
                    out=B[f"SILU{ic % 2}"][:],
                    in_=XT2[:, (ic % 2) * 512:(ic % 2) * 512 + 512],
                    func=AF.Silu))(),
                    deps=[tslc, st["silu_free"][ic % 2]], inc="SE")
                st["slot_tok"][par][ic % 2] = tsl
                bank[gb] = tslc
                tup = E(V, (lambda ic=ic, ub=ub, w=w: lambda e: e.tensor_scalar(
                    out=B["PBUF"][:, ic * 512:(ic + 1) * 512], in0=bankf(int(ub[1])),
                    scalar1=B["SC"][:, w + 23:w + 24], scalar2=None, op0=ALU.mult))(),
                    deps=[tum, tc1u], inc="VE")
                bank[ub] = tup
                tpp = E(V, (lambda ic=ic: lambda e: e.tensor_tensor(
                    out=B["PBUF"][:, ic * 512:(ic + 1) * 512],
                    in0=B["PBUF"][:, ic * 512:(ic + 1) * 512],
                    in1=B[f"SILU{ic % 2}"][:], op=ALU.mult))(),
                    deps=[tsl], inc="VE")
                st["silu_free"][ic % 2] = tpp
                tpm8 = E(V, (lambda ic=ic: lambda e: e.tensor_reduce(
                    out=B["PM8"][:, ic:ic + 1], in_=B["PBUF"][:, ic * 512:(ic + 1) * 512],
                    axis=AX, op=ALU.max, apply_absolute_value=True))(), inc="VE")
                pb_toks.append(tpm8)
                gu_last = tum
            tile[tt]["gu_last"] = gu_last
            tile[tt]["pb_done"] = pb_toks[-1]

        def tail_rounds(tt):
            par = tt % 2
            w = 48 * par
            E(V, (lambda w=w: lambda e: e.tensor_reduce(
                out=B["SC"][:, w + 24:w + 25], in_=B["PM8"][:], axis=AX, op=ALU.max))(),
              deps=[tile[tt]["pb_done"]])
            E(V, (lambda w=w: lambda e: e.tensor_scalar(
                out=B["SC"][:, w + 25:w + 26], in0=B["SC"][:, w + 24:w + 25],
                scalar1=1e-5, scalar2=None, op0=ALU.max))())
            E(V, (lambda w=w: lambda e: e.reciprocal(
                out=B["SC"][:, w + 26:w + 27], in_=B["SC"][:, w + 25:w + 26]))())
            ts2 = E(V, (lambda w=w: lambda e: e.tensor_scalar(
                out=B["SC"][:, w + 27:w + 28], in0=B["SC"][:, w + 26:w + 27],
                scalar1=127.0, scalar2=None, op0=ALU.mult))(), inc="VE")
            tc2 = E(V, (lambda w=w: lambda e: e.tensor_scalar(
                out=B["SC"][:, w + 28:w + 29], in0=B["SC"][:, w + 25:w + 26],
                scalar1=B["MGV"][:, 2:3], scalar2=1.0 / 127.0,
                op0=ALU.mult, op1=ALU.mult))(), inc="VE")
            XT2 = B["XT2a"] if par == 0 else B["XT2b"]
            rnd_last = None
            for ic in range(NIC):
                E(V, (lambda ic=ic, w=w, XT2=XT2: lambda e: e.tensor_scalar(
                    out=XT2[:, (ic % 2) * 512:(ic % 2) * 512 + 512],
                    in0=B["PBUF"][:, ic * 512:(ic + 1) * 512],
                    scalar1=B["SC"][:, w + 27:w + 28], scalar2=MAGIC,
                    op0=ALU.mult, op1=ALU.add))(),
                  deps=([ts2] if ic == 0 else []))
                rnd_last = E(V, (lambda ic=ic, XT2=XT2: lambda e: e.tensor_scalar(
                    out=B["HQ"][:, ic * 512:(ic + 1) * 512],
                    in0=XT2[:, (ic % 2) * 512:(ic % 2) * 512 + 512],
                    scalar1=MAGIC, scalar2=None, op0=ALU.subtract))(),
                    deps=[st["hq_free"] if ic == 0 else None], inc="VE")
            tile[tt]["rnd_last"] = rnd_last
            tile[tt]["tc2"] = tc2

        def tail_down(tt):
            par = tt % 2
            w = 48 * par
            XT2 = B["XT2a"] if par == 0 else B["XT2b"]
            X2p = B["X2a"] if par == 0 else B["X2b"]
            rnd_last = tile[tt]["rnd_last"]
            tc2 = tile[tt]["tc2"]
            tx2 = tile[tt]["tx2"]
            hqt_evs = []
            ttr2l = None
            for r in range(4):
                ttr2 = None
                for jj in range(8):
                    jblk = r * 8 + jj
                    ttr2 = E(T, (lambda jblk=jblk, jj=jj: lambda e: e.transpose(
                        psb[:, jj * 128:(jj + 1) * 128],
                        B["HQ"][:, jblk * 128:(jblk + 1) * 128],
                        B["identb"][:]))(),
                        deps=([rnd_last, st["psb_free"],
                               st["hqt_free"] if r == 0 else None] if jj == 0 else []),
                        inc="TE" if jj == 7 else None)
                ttr2l = ttr2
                tev = E(V, (lambda r=r: lambda e: e.tensor_copy(
                    out=B["HQT"][:, r * 1024:(r + 1) * 1024], in_=psb[:]))(),
                    deps=[ttr2], inc="VE")
                st["psb_free"] = tev
                hqt_evs.append(tev)
            st["hq_free"] = ttr2l
            db = ("f4", "f5") if tt % 2 == 0 else ("f2", "f3")
            dbc = 2048 if tt % 2 == 0 else 1024
            tdm = None
            for h in range(2):
                for j in range(32):
                    tdm = E(T, (lambda h=h, j=j, dbc=dbc: lambda e: e.matmul(
                        psf[:, dbc + h * 512: dbc + (h + 1) * 512],
                        B["HQT"][:, j * 128:(j + 1) * 128],
                        B["wqd"][:, j * D + h * 512: j * D + (h + 1) * 512],
                        start=(j == 0), stop=(j == 31)))(),
                        deps=([hqt_evs[-1], bank[db[0]], bank[db[1]]]
                              if (h == 0 and j == 0) else []),
                        inc="TE" if (h == 1 and j == 31) else None)
            st["hqt_free"] = tdm
            st["xqt_free"][par] = tile[tt]["gu_last"]
            th2 = E(V, (lambda X2p=X2p, XT2=XT2, dbc=dbc, w=w: lambda e:
                        e.scalar_tensor_tensor(
                out=XT2[:], in0=psf[:, dbc:dbc + 1024], scalar=B["SC"][:, w + 28:w + 29],
                in1=X2p[:], op0=ALU.mult, op1=ALU.add))(),
                deps=[tdm, tc2, tx2], inc="VE")
            bank[db[0]] = th2
            bank[db[1]] = th2
            tss3 = E(S, (lambda XT2=XT2, X2p=X2p, w=w: lambda e: e.activation(
                out=X2p[:], in_=XT2[:], func=AF.Identity,
                accum_out=B["SC"][:, w + 29:w + 30]))(), deps=[th2], inc="SE")
            tsq3p = E(S, (lambda XT2=XT2, X2p=X2p, w=w: lambda e: e.activation(
                out=X2p[:], in_=XT2[:], func=AF.Square,
                accum_out=B["SC"][:, w + 30:w + 31]))(), inc="SE")
            E(V, (lambda w=w: lambda e: e.tensor_scalar(
                out=B["SC"][:, w + 41:w + 42], in0=B["SC"][:, w + 29:w + 30],
                scalar1=1.0 / 1024.0, scalar2=None, op0=ALU.mult))(),
              deps=[tss3, tsq3p])
            E(V, (lambda w=w: lambda e: e.tensor_scalar(
                out=B["SC"][:, w + 42:w + 43], in0=B["SC"][:, w + 30:w + 31],
                scalar1=1.0 / 1024.0, scalar2=LN_EPS, op0=ALU.mult, op1=ALU.add))())
            E(V, (lambda w=w: lambda e: e.tensor_scalar(
                out=B["SC"][:, w + 43:w + 44], in0=B["SC"][:, w + 41:w + 42],
                scalar1=B["SC"][:, w + 41:w + 42], scalar2=-1.0,
                op0=ALU.mult, op1=ALU.mult))())
            E(V, (lambda w=w: lambda e: e.tensor_tensor(
                out=B["SC"][:, w + 43:w + 44], in0=B["SC"][:, w + 42:w + 43],
                in1=B["SC"][:, w + 43:w + 44], op=ALU.add))())
            tv3 = E(V, (lambda w=w: lambda e: e.reciprocal(
                out=B["SC"][:, w + 44:w + 45], in_=B["SC"][:, w + 43:w + 44]))(), inc="VE")
            tsq3 = E(S, (lambda w=w: lambda e: e.activation(
                out=B["SC"][:, w + 45:w + 46], in_=B["SC"][:, w + 44:w + 45],
                func=AF.Sqrt))(), deps=[tv3], inc="SE")
            tneg3 = E(V, (lambda w=w: lambda e: e.tensor_scalar(
                out=B["SC"][:, w + 46:w + 47], in0=B["SC"][:, w + 41:w + 42],
                scalar1=B["SC"][:, w + 45:w + 46], scalar2=-1.0,
                op0=ALU.mult, op1=ALU.mult))(), deps=[tsq3], inc="VE")
            tout = E(S, (lambda XT2=XT2, X2p=X2p, w=w: lambda e: e.activation(
                out=X2p[:], in_=XT2[:], func=AF.Identity,
                scale=B["SC"][:, w + 45:w + 46], bias=B["SC"][:, w + 46:w + 47]))(),
                deps=[tneg3, th2, st["outst_free"]], inc="SE")
            st["swin_free"][par] = tout
            st["xt2_free"][par] = tout
            tst = E(G, (lambda c=c, tt=tt, X2p=X2p: lambda e: e.dma_start(
                out=out_h[c * CH + tt * 128: c * CH + (tt + 1) * 128, :],
                in_=X2p[:]))(), deps=[tout], inc="SPO", inc_n=16)
            st["outst_free"] = tst
            st["x2_free"][par] = tst
            if tt + 2 < NTT:
                XT2n = B["XT2a"] if par == 0 else B["XT2b"]
                xls[tt + 2] = E(Y, (lambda c=c, tt=tt, XT2n=XT2n: lambda e: e.dma_start(
                    out=XT2n[:],
                    in_=x_h[c * CH + (tt + 2) * 128: c * CH + (tt + 3) * 128, :]))(),
                    deps=[tout], inc=f"LDX{par}", inc_n=16)

        projxq(0)
        projxq(1)
        gup(0)
        tail_rounds(0)
        tail_down(0)
        gup(1)
        tail_rounds(1)
        projxq(2)
        tail_down(1)
        gup(2)
        tail_rounds(2)
        projxq(3)
        tail_down(2)
        gup(3)
        tail_rounds(3)
        tail_down(3)
        st["csum_free"] = proj_last[NTT - 1]
        st["drow_free"] = proj_last[NTT - 1]
        if phase == 6 and c == 0:
            break
        if c + 1 < NCH:
            st["nT_load"] = E(Y, (lambda c=c: lambda e: e.dma_start(
                out=B["NTL"][:], in_=sc_num[(c + 1) * 128:(c + 2) * 128, :]))(),
                deps=[proj_last[NTT - 1]], inc="LDB1", inc_n=16)

    E(G, lambda e: e.nop(), deps=[p.tok("SPO")])
    replay_all()
    esB.close()
    es.close()
    return nc


_NC_CACHE = None


def kernel(L0_attn_x, L0_attn_w_qkv, L0_attn_b_qkv, L0_attn_w_gate, L0_attn_b_gate,
           L0_attn_w_proj, L0_attn_b_proj, L0_attn_ln_g, L0_attn_ln_b,
           L0_ffn_w_gate, L0_ffn_w_up, L0_ffn_w_down):
    global _NC_CACHE
    x = np.ascontiguousarray(np.asarray(L0_attn_x, dtype=np.float32).reshape(16384, 1024))
    wqkv = np.ascontiguousarray(np.asarray(L0_attn_w_qkv, dtype=np.float32))
    bqkv = np.asarray(L0_attn_b_qkv, dtype=np.float32)
    wgate = np.ascontiguousarray(np.asarray(L0_attn_w_gate, dtype=np.float32))
    bgate = np.asarray(L0_attn_b_gate, dtype=np.float32)
    wproj = np.ascontiguousarray(np.asarray(L0_attn_w_proj, dtype=np.float32))
    bproj = np.asarray(L0_attn_b_proj, dtype=np.float32)
    lng = np.asarray(L0_attn_ln_g, dtype=np.float32)
    lnb = np.asarray(L0_attn_ln_b, dtype=np.float32)
    wg = np.asarray(L0_ffn_w_gate, dtype=np.float32)
    wu = np.asarray(L0_ffn_w_up, dtype=np.float32)
    wd = np.asarray(L0_ffn_w_down, dtype=np.float32)

    biasf = np.zeros((128, 40), np.float32)
    biasf[:, 0:8] = bqkv[0:1024].reshape(8, 128).T
    biasf[:, 8:16] = bqkv[1024:2048].reshape(8, 128).T
    biasf[:, 16:24] = bgate.reshape(8, 128).T
    biasf[:, 24:32] = lng.reshape(8, 128).T
    biasf[:, 32:40] = lnb.reshape(8, 128).T
    bvbc = np.ascontiguousarray(np.tile(bqkv[2048:3072].reshape(1, 1024), (128, 1)))
    bpbc = np.ascontiguousarray(np.tile(bproj.reshape(1, 1024), (128, 1)))

    identf = np.eye(128, dtype=np.float32)
    identb = np.eye(128, dtype=ml_dtypes.bfloat16)
    sidx = np.arange(128)
    trilb = (sidx[:, None] <= sidx[None, :]).astype(ml_dtypes.bfloat16)
    onesb = np.ones((128, 128), dtype=ml_dtypes.bfloat16)
    onesf = np.ones((128, 128), dtype=np.float32)

    wgT = np.ascontiguousarray(wg.T)
    wuT = np.ascontiguousarray(wu.T)
    wdT = np.ascontiguousarray(wd.T)

    if _NC_CACHE is None:
        _NC_CACHE = build_nc()
    nc = _NC_CACHE
    in_maps = []
    for c in range(NCORE):
        maskI = np.concatenate(
            [(identb if cc < c else np.zeros_like(identb)) for cc in range(NCORE)],
            axis=1)
        wsh = np.concatenate([
            wgT[c * 128:(c + 1) * 128, :],
            wuT[c * 128:(c + 1) * 128, :],
            wdT[c * 512:(c + 1) * 512, :].reshape(4, 128, 1024)
               .transpose(1, 0, 2).reshape(128, 4096),
        ], axis=1)
        in_maps.append({
            "x": np.ascontiguousarray(x[c * TC:(c + 1) * TC]),
            "wqkv": wqkv, "wgate": wgate, "wproj": wproj,
            "biasf": biasf, "bvbc": bvbc, "bpbc": bpbc,
            "identf": identf, "identb": identb, "trilb": trilb,
            "onesb": onesb, "onesf": onesf,
            "maskI": np.ascontiguousarray(maskI),
            "wsh": np.ascontiguousarray(wsh),
        })
    res = run_bass_kernel_spmd(nc, in_maps, core_ids=list(range(NCORE)))
    out = np.concatenate([res.results[c]["out"] for c in range(NCORE)], axis=0)
    return out.reshape(1, 16384, 1024)
